# version 28
# baseline (speedup 1.0000x reference)
import numpy as np
import ml_dtypes
_BF = ml_dtypes.bfloat16
from contextlib import ExitStack
import concourse.bass as bass
import concourse.mybir as mybir

F32 = mybir.dt.float32
BF16 = mybir.dt.bfloat16
I32 = mybir.dt.int32
AF = mybir.ActivationFunctionType
ALU = mybir.AluOpType

ENGINES = ("sync", "scalar", "vector", "tensor", "gpsimd")
CSEM = {"scalar": "cS", "vector": "cV", "tensor": "cT", "gpsimd": "cG"}
DQ = {"sync": tuple(f"dS{i}" for i in range(8)),
      "scalar": tuple(f"dA{i}" for i in range(8)),
      "gpsimd": tuple(f"dG{i}" for i in range(4))}
ALL_SEMS = ("cS", "cV", "cT", "cG", "cC") + DQ["sync"] + DQ["scalar"] + DQ["gpsimd"]
BATCH_MAX = 2


class Ev:
    __slots__ = ("sem", "val")
    def __init__(self, sem, val):
        self.sem = sem
        self.val = val


class DmaEv:
    __slots__ = ("q", "batch")
    def __init__(self, q, batch):
        self.q = q
        self.batch = batch


class PendEv:
    __slots__ = ("eng", "ev")
    def __init__(self, eng):
        self.eng = eng
        self.ev = None


class Buf:
    __slots__ = ("w", "reads")
    def __init__(self):
        self.w = None
        self.reads = []


class Prog:
    def __init__(self):
        self.ops = {e: [] for e in ENGINES}
        self.ctr = {s: 0 for s in ALL_SEMS}
        self.waited = {e: {s: 0 for s in ALL_SEMS} for e in ENGINES}
        self.ninstr = 0
        self.q = {e: dict(nopen=0, nbatch=0, open=None) for e in DQ}
        self.pending = {e: [] for e in ENGINES}

    def _close_batch(self, qeng, which=None):
        st = self.q[qeng]
        if st["open"] is not None:
            batch = st["open"]
            batch[0] = True
            batch[2] = self.ctr[batch[1]]
            st["open"] = None
            st["nopen"] = 0

    def _resolve_pending(self, eng2):
        sem = CSEM[eng2]
        self.ctr[sem] += 1
        val = self.ctr[sem]
        self.ops[eng2].append(lambda e, S, _s=sem: e.nop().then_inc(S[_s], 1))
        self.ninstr += 1
        ev = Ev(sem, val)
        for p in self.pending[eng2]:
            p.ev = ev
        self.pending[eng2] = []
        return ev

    def _wait(self, eng, ev, raw=True):
        if ev is None:
            return
        if isinstance(ev, PendEv):
            if ev.ev is None:
                self._resolve_pending(ev.eng)
            ev = ev.ev
        if isinstance(ev, DmaEv):
            if not ev.batch[0]:
                self._close_batch(ev.q)
            ev = Ev(ev.batch[1], ev.batch[2])
        if self.waited[eng].get(ev.sem, 0) >= ev.val:
            return
        self.waited[eng][ev.sem] = ev.val
        sem_name, val = ev.sem, ev.val
        self.ops[eng].append(lambda e, S, _s=sem_name, _v=val: e.wait_ge(S[_s], _v))
        self.ninstr += 1

    def op(self, eng, fn, reads=(), writes=(), war=(), dma=False, track=True,
           sem_override=None):
        for b in reads:
            self._wait(eng, b.w, raw=True)
        for b in war:
            self._wait(eng, b.w, raw=False)
            for r in b.reads:
                self._wait(eng, r, raw=False)
        for b in writes:
            self._wait(eng, b.w, raw=False)
            for r in b.reads:
                self._wait(eng, r, raw=False)
        if dma:
            st = self.q[eng]
            if st["open"] is None or st["nopen"] >= BATCH_MAX:
                self._close_batch(eng)
                pool = DQ[eng]
                psem = pool[st["nbatch"] % len(pool)]
                st["nbatch"] += 1
                self._wait(eng, Ev(psem, self.ctr[psem]))
                st["open"] = [False, psem, 0]
            batch = st["open"]
            psem = batch[1]
            self.ctr[psem] += 16
            st["nopen"] += 1
            def run(e, S, fn=fn, psem=psem):
                fn(e, S).then_inc(S[psem], 16)
            ev = DmaEv(eng, batch)
            self.ops[eng].append(run)
            self.ninstr += 1
            for b in writes:
                b.w = ev
                b.reads = []
            for b in reads:
                b.reads.append(ev)
            return ev
        if track:
            sem = sem_override or CSEM[eng]
            self.ctr[sem] += 1
            val = self.ctr[sem]
            def run(e, S, fn=fn, sem=sem):
                fn(e, S).then_inc(S[sem], 1)
            ev = Ev(sem, val)
            for p in self.pending[eng]:
                p.ev = ev
            self.pending[eng] = []
        else:
            def run(e, S, fn=fn):
                fn(e, S)
            ev = None
        self.ops[eng].append(run)
        self.ninstr += 1
        if track:
            for b in writes:
                b.w = ev
                b.reads = []
            for b in reads:
                b.reads.append(ev)
        else:
            pe = PendEv(eng)
            self.pending[eng].append(pe)
            for b in reads:
                b.reads.append(pe)
        return ev

    def mm(self, eng, fn, reads=(), psum=None, start=False, stop=False):
        if start and stop:
            return self.op(eng, fn, reads=reads, writes=[psum])
        if start:
            return self.op(eng, fn, reads=reads, war=[psum])
        if stop:
            return self.op(eng, fn, reads=reads, writes=[psum])
        return self.op(eng, fn, reads=reads)

    def raw_wait(self, eng, ev):
        self._wait(eng, ev)

    def flush_queues(self, eng):
        for qeng in DQ:
            self._close_batch(qeng)
            for psem in DQ[qeng]:
                self._wait(eng, Ev(psem, self.ctr[psem]))

    def finalize_into(self, nc, sems):
        with nc.Block() as block:
            for eng in ENGINES:
                ops = self.ops[eng]
                if not ops:
                    continue
                getattr(block, eng)(_mkbody(ops, sems))


def _mkbody(ops, sems):
    def body(e):
        for fn in ops:
            fn(e, sems)
    return body


DEBUG_SCR = False

def build_program(L=4096, T=512, NCH_PRE=None):
    DM, DI, N, RANK, HID = 1024, 2048, 16, 128, 3072
    NCH = L // T
    NCH_PRE = 0
    L_OWN = L
    DIH = DI // 2
    CBS = DIH // 128
    LH = L // 2
    RG = [[0, 1], [2, 3], [4, 5], [6, 7]]
    KB = DM // 128
    CB = DI // 128
    HB = HID // 128
    TT = T // 128
    NG = 1
    NGRP = N // NG
    TF = T // 2 if T >= 256 else T
    EPS = 1e-5
    QINV = float(np.float32(1.0 / 127.0))
    NCO = (DI + DIH) // 128

    nc = bass.Bass("TRN2", target_bir_lowering=False, debug=False)

    def din(name, shape, dt):
        return nc.dram_tensor(name, list(shape), dt, kind="ExternalInput").ap()

    x_d = din("x", (L, DM), F32)
    mask_d = din("mask", (128, NCH), F32)
    w_in_d = din("w_in", (DM, DI + DIH), F32)
    norm_w_d = din("norm_w", (128, KB), F32)
    conv_w_d = din("conv_w_r", (128, CB * 4), F32)
    conv_b_d = din("conv_b_r", (128, CB), F32)
    xproj_d = din("xproj_T", (DI, 160), F32)
    dtw_d = din("dtw_T", (RANK, DIH), F32)
    dtb_d = din("dtb_r", (128, CBS), F32)
    alog_d = din("alog_r", (128, CBS * N), F32)
    dvec_d = din("d_r", (128, CBS), F32)
    wout_d = din("wout_T", (DI, DM), F32)
    normf_d = din("norm_f", (1, DM), F32)
    wg_d = din("wg_T", (HB, 128, KB * 128), BF16)
    wu_d = din("wu_T", (HB, 128, KB * 128), BF16)
    wd_d = din("wd_T", (HID, DM), BF16)
    eye_d = din("eye", (128, 128), F32)
    xf_d = din("xf", (LH, DM), F32)

    out_d = nc.dram_tensor("out", [L_OWN, DM], F32, kind="ExternalOutput").ap()

    def scr(name, shape, dt):
        return nc.dram_tensor(name, list(shape), dt, kind="Internal").ap()

    scr_win = scr("scr_win", (NCO, 128, KB * 128), BF16)
    scr_wout = scr("scr_wout", (DI, DM), BF16)
    scr_bc = scr("scr_bc", (2, 32, T), BF16)
    scr_xn = scr("scr_xn", (L, DM), F32)
    scr_nf = scr("scr_nf", (1, DM), F32)
    cc_yin = scr("cc_yin", (NCH, 128, CBS * T), BF16)
    cc_yout = scr("cc_yout", (NCH, 256, CBS * T), BF16)
    cc_gin = scr("cc_gin", (NCH // 2, T, DM), BF16)
    cc_gout = scr("cc_gout", (NCH // 2, 2 * T, DM), BF16)
    scr_du = scr("scr_du", (NCH, 128, CB * T), BF16) if DEBUG_SCR else None
    scr_dyg = scr("scr_dyg", (NCH, 128, CB * T), BF16) if DEBUG_SCR else None

    ctx = ExitStack()
    sb = lambda n, s, d: ctx.enter_context(nc.sbuf_tensor(n, list(s), d))
    psx = lambda n, s, d: ctx.enter_context(nc.psum_tensor(n, list(s), d))

    eye_bf = sb("eye_bf", (128, 128), BF16)
    norm_w_t = sb("norm_w_t", (128, KB), F32)
    conv_w_t = sb("conv_w_t", (128, CB * 4), F32)
    conv_b_t = sb("conv_b_t", (128, CB), F32)
    dtb_t = sb("dtb_t", (128, CBS), F32)
    acol_t = sb("acol_t", (128, CBS * N), F32)
    dvec_t = sb("dvec_t", (128, CBS), F32)
    mask_t = sb("mask_t", (128, NCH), F32)
    normf_rep = sb("normf_rep", (128, DM), BF16)
    carry = sb("carry", (128, CBS * N), F32)
    xpj = sb("xpj", (128, CB * 160), BF16)
    dtw_bf = sb("dtw_bf", (128, DIH), BF16)
    stat6 = sb("stat6", (128, 12), F32)
    stat2 = sb("stat2", (128, 2), F32)
    sc1 = sb("sc1", (128, 8), F32)
    cst = sb("cst", (128, 2), F32)
    xt = [sb(f"xt{i}", (128, DM), F32) for i in range(2)]
    xnt = [sb(f"xnt{i}", (128, DM), F32) for i in range(2)]
    rms_bf = [sb(f"rmsbf{i}", (128, DM), BF16) for i in range(1)]
    rms_T = [sb(f"rms_T{i}", (128, KB * T), BF16) for i in range(2)]
    xn_T = sb("xn_T", (128, KB * T), BF16)
    xbr = [sb(f"xbr{i}", (128, T + 4), BF16) for i in range(2)]
    halo_t = sb("halo_t", (128, CB * 4), BF16)
    u_bf = [sb(f"u_bf{i}", (128, CBS * T), BF16) for i in range(2)]
    u_oth = sb("u_oth", (128, (CB - CBS) * T), BF16)
    z_bf = [sb(f"zbf{i}", (128, T), BF16) for i in range(2)]
    dtl_bf = [sb(f"dtl{i}", (128, T), BF16) for i in range(2)]
    sp_t = [sb(f"sp{i}", (128, T), F32) for i in range(2)]
    dtu_t = [sb(f"dtu{i}", (128, T), BF16) for i in range(2)]
    a_t = [sb(f"a{i}", (128, NG * T), F32) for i in range(3)]
    b_t = [sb(f"b{i}", (128, NG * T), F32) for i in range(2)]
    yh_t = [sb(f"yh{i}", (128, NG * T), BF16) for i in range(3)]
    brep = sb("brep", (128, N * T), BF16)
    crep = sb("crep", (128, N * T), BF16)
    bc_sb = sb("bc_sb", (32, T), BF16)
    yg_bf = sb("yg_bf", (128, CBS * T), BF16)
    yg_lo = sb("yg_lo", (128, CBS * T), BF16)
    yg_hi = sb("yg_hi", (128, CBS * T), BF16)
    diag_t = [sb("diag0", (128, 4 * 128), BF16)]
    wblk = [sb(f"wblk{i}", (128, KB * 128), BF16) for i in range(2)]
    wffn = [sb(f"wffn{i}", (128, KB * 128), BF16) for i in range(3)]
    wffn2 = [sb(f"wffn2{i}", (128, KB * 128), BF16) for i in range(3)]
    wdn = [sb(f"wdn{i}", (128, 2 * DM), BF16) for i in range(2)]
    woutb = [sb(f"woutb{i}", (128, 2 * DM), BF16) for i in range(2)]
    gu_s = [sb(f"gu_s{i}", (128, TF), BF16) for i in range(2)]
    hffn = sb("hffn", (128, HB * TF), BF16)
    g_t = sb("g_t", (128, DM), BF16)
    castf_t = sp_t[0] if T >= 256 else sb("cast_fs", (128, 256), F32)
    castf_key = "sp0" if T >= 256 else "cast_fs"

    p_mm = psx("p_mm", (128, 512), F32)
    p_mm2 = psx("p_mm2", (128, 512), F32)
    p_conv = psx("p_conv", (128, 512), F32)
    p_ys = [psx(f"p_ys{i}", (128, 512), F32) for i in range(2)]
    p_tr = psx("p_tr", (128, 1024), BF16)
    p_o = psx("p_o", (128, 1024), F32)

    bufs = {}
    def gb(key):
        if key not in bufs:
            bufs[key] = Buf()
        return bufs[key]

    P = Prog()

    def load(dst_ap, src_ap, key=None, eng="sync", reads=(), writes=(), war=()):
        w = list(writes)
        if key is not None:
            w.append(gb(key))
        return P.op(eng, lambda e, S, d=dst_ap, s=src_ap: e.dma_start(out=d, in_=s),
                    reads=list(reads), writes=w, war=list(war), dma=True)

    load(xt[0][:, 0:128], eye_d[:], "xt0")
    P.op("vector", lambda e, S: e.tensor_copy(eye_bf[:], xt[0][:, 0:128]),
         reads=[gb("xt0")], writes=[gb("eye_bf")])
    load(norm_w_t[:], norm_w_d[:], "norm_w")
    load(conv_w_t[:], conv_w_d[:], "conv_w")
    load(conv_b_t[:], conv_b_d[:], "conv_b")
    load(dtb_t[:], dtb_d[:], "dtb")
    load(acol_t[:], alog_d[:], "acol")
    load(dvec_t[:], dvec_d[:], "dvec")
    load(mask_t[:], mask_d[:], "mask")
    load(scr_nf[:], normf_d[:], "scr_nf", eng="scalar")
    load(xnt[0][:], scr_nf[:].broadcast_to((128, DM)), "xnt0", eng="scalar",
         reads=[gb("scr_nf")])
    P.op("vector", lambda e, S: e.tensor_copy(normf_rep[:], xnt[0][:]),
         reads=[gb("xnt0")], writes=[gb("normf_rep")])
    P.op("scalar", lambda e, S: e.activation(acol_t[:], acol_t[:], AF.Exp),
         reads=[gb("acol")], writes=[gb("acol")])
    P.op("vector", lambda e, S: e.tensor_scalar_mul(acol_t[:], acol_t[:], -1.0),
         reads=[gb("acol")], writes=[gb("acol")])
    for cb in range(CB):
        load(castf_t[:, 0:160], xproj_d[cb * 128:(cb + 1) * 128, :], castf_key)
        P.op("vector", lambda e, S, cb=cb: e.tensor_copy(xpj[:, cb * 160:(cb + 1) * 160],
                                                         castf_t[:, 0:160]),
             reads=[gb(castf_key)], writes=[gb("xpj")])
    for j in range(DIH // 256):
        load(castf_t[:, 0:256], dtw_d[:, j * 256:(j + 1) * 256], castf_key)
        P.op("vector", lambda e, S, j=j: e.tensor_copy(dtw_bf[:, j * 256:(j + 1) * 256], castf_t[:, 0:256]),
             reads=[gb(castf_key)], writes=[gb("dtw_bf")])

    def precast_wide(srcd, dst, rows, cols, skey, tiled, fold=False):
        ring = [(xt[0], "xt0"), (xt[1], "xt1"), (xnt[1], "xnt1")]
        CWW = min(DM, cols)
        it = 0
        for rb in range(rows // 128):
            for cc in range(cols // CWW):
                c0 = cc * CWW
                tile, tkey = ring[it % 3]
                it += 1
                load(tile[:, 0:CWW], srcd[rb * 128:(rb + 1) * 128, c0:c0 + CWW], tkey)
                btile, bkey = ((rms_bf[0], "rmsbf0") if it % 2 == 0 else (g_t, "g_raw"))
                if fold:
                    P.op("vector", lambda e, S, rb=rb, tile=tile, btile=btile, CWW=CWW: e.tensor_scalar(
                            btile[:, 0:CWW], tile[:, 0:CWW], norm_w_t[:, rb:rb + 1], None, ALU.mult),
                         reads=[gb(tkey), gb("norm_w")], writes=[gb(bkey)])
                else:
                    P.op("vector", lambda e, S, tile=tile, btile=btile, CWW=CWW: e.tensor_copy(
                            btile[:, 0:CWW], tile[:, 0:CWW]),
                         reads=[gb(tkey)], writes=[gb(bkey)])
                if tiled:
                    for q in range(CWW // 128):
                        co = (c0 + q * 128) // 128
                        P.op("scalar", lambda e, S, co=co, rb=rb, q=q, btile=btile, dst=dst: e.dma_start(
                                out=dst[co, :, rb * 128:(rb + 1) * 128],
                                in_=btile[:, q * 128:(q + 1) * 128]),
                             reads=[gb(bkey)], writes=[gb(skey + "scr")], dma=True)
                else:
                    P.op("scalar", lambda e, S, rb=rb, c0=c0, btile=btile, dst=dst, CWW=CWW: e.dma_start(
                            out=dst[rb * 128:(rb + 1) * 128, c0:c0 + CWW],
                            in_=btile[:, 0:CWW]),
                         reads=[gb(bkey)], writes=[gb(skey + "scr")], dma=True)

    precast_wide(w_in_d, scr_win, DM, DI + DIH, "win", tiled=True, fold=True)


    P.op("vector", lambda e, S: e.memset(cst[:, 0:1], 1e-5), writes=[gb("cst")])
    P.op("vector", lambda e, S: e.memset(cst[:, 1:2], 1.0), writes=[gb("cst")])
    P.op("vector", lambda e, S: e.memset(carry[:], 0.0), writes=[gb("carry")])
    _evx = P.op("vector", lambda e, S: e.memset(halo_t[:], 0.0), writes=[gb("halo_all")])
    for _cb in range(CB):
        gb(f"halo{_cb}").w = _evx

    def ln_rms_stats(src_ap, src_buf, out_col, is_rms):
        P.op("vector", lambda e, S: e.bn_stats(stat6[:, 0:6], src_ap[:, 0:512]),
             reads=[src_buf], writes=[gb("st6")])
        P.op("vector", lambda e, S: e.bn_stats(stat6[:, 6:12], src_ap[:, 512:1024]),
             reads=[src_buf], writes=[gb("st6")])
        P.op("vector", lambda e, S: e.bn_aggr(stat2[:], stat6[:]),
             reads=[gb("st6")], writes=[gb("st2")])
        if is_rms:
            P.op("vector", lambda e, S: e.tensor_tensor(sc1[:, 3:4], stat2[:, 0:1],
                                                        stat2[:, 0:1], ALU.mult),
                 reads=[gb("st2")], writes=[gb("m2")])
            P.op("vector", lambda e, S: e.tensor_tensor(sc1[:, 4:5], sc1[:, 3:4],
                                                        stat2[:, 1:2], ALU.add),
                 reads=[gb("m2"), gb("st2")], writes=[gb("ms")])
            P.op("scalar", lambda e, S: e.activation(sc1[:, 5:6], sc1[:, 4:5], AF.Ln, bias=cst[:, 0:1]),
                 reads=[gb("ms"), gb("cst")], writes=[gb("sd")])
        else:
            P.op("vector", lambda e, S: e.tensor_scalar_mul(
                    sc1[:, 1:2], stat2[:, 0:1], -1.0),
                 reads=[gb("st2")], writes=[gb("negm")])
            P.op("scalar", lambda e, S: e.activation(sc1[:, 5:6], stat2[:, 1:2], AF.Ln, bias=cst[:, 0:1]),
                 reads=[gb("st2"), gb("cst")], writes=[gb("sd")])
        P.op("scalar", lambda e, S, c=out_col: e.activation(sc1[:, c:c + 1], sc1[:, 5:6],
                                                            AF.Exp, scale=-0.5),
             reads=[gb("sd")], writes=[gb(f"rstd{out_col}")])


    def ln_closures(k):
        t0 = k * T
        dbk = k % 2
        a = [(0, _ln_tt(k, tt, True, t0, dbk)) for tt in range(TT)]
        return a, []

    def emit_ln(k):
        a, _ = ln_closures(k)
        for _, f in a:
            f()

    def _ln_tt(k, tt, full, t0, dbk):
      def f(tt=tt):
              row = t0 + tt * 128
              pp = tt % 2
              bx = gb(f"xt{pp}")
              load(xt[pp][:], x_d[row:row + 128, :], f"xt{pp}")
              ln_rms_stats(xt[pp][:], bx, 2, is_rms=False)
              P.op("vector", lambda e, S, pp=pp: e.tensor_scalar(
                      xnt[pp][:], xt[pp][:], sc1[:, 1:2], sc1[:, 2:3], ALU.add, ALU.mult),
                   reads=[bx, gb("negm"), gb("rstd2")], writes=[gb(f"xnt{pp}")])
              if full:
                  P.op("scalar", lambda e, S, pp=pp, row=row: e.dma_start(
                          out=scr_xn[row:row + 128, :], in_=xnt[pp][:]),
                       reads=[gb(f"xnt{pp}")], writes=[gb(f"sxn{k}_{tt}")], dma=True)
              ln_rms_stats(xnt[pp][:], gb(f"xnt{pp}"), 6, is_rms=True)
              P.op("vector", lambda e, S, pp=pp: e.tensor_scalar_mul(
                      rms_bf[0][:], xnt[pp][:], sc1[:, 6:7]),
                   reads=[gb(f"xnt{pp}"), gb("rstd6")], writes=[gb("rmsbf0")])
              for kb in range(KB):
                  P.mm("tensor", lambda e, S, pp=pp, kb=kb: e.transpose(
                          p_tr[:, kb * 128:(kb + 1) * 128],
                          rms_bf[0][:, kb * 128:(kb + 1) * 128], eye_bf[:]),
                       reads=[gb("rmsbf0"), gb("eye_bf")], psum=gb("p_tr"),
                       start=(kb == 0), stop=(kb == KB - 1))
              P.op("scalar", lambda e, S, tt=tt, dbk=dbk: e.copy(
                      rms_T[dbk][:].rearrange("p (kb t) -> p kb t", kb=KB)[:, :, tt * 128:(tt + 1) * 128],
                      p_tr[:].rearrange("p (kb c) -> p kb c", kb=KB)),
                   reads=[gb("p_tr")], writes=[gb(f"rms_T{dbk}")])
      return f

    def _ln_xnT(m, tt):
      def f(tt=tt):
          row = m * T + tt * 128
          pp = tt % 2
          load(xt[pp][:], xf_d[row:row + 128, :], f"xt{pp}", eng="gpsimd")
          ln_rms_stats(xt[pp][:], gb(f"xt{pp}"), 2, is_rms=False)
          P.op("vector", lambda e, S, pp=pp: e.tensor_scalar(
                  xnt[pp][:], xt[pp][:], sc1[:, 1:2], sc1[:, 2:3], ALU.add, ALU.mult),
               reads=[gb(f"xt{pp}"), gb("negm"), gb("rstd2")], writes=[gb(f"xnt{pp}")])
          P.op("vector", lambda e, S, pp=pp: e.tensor_copy(rms_bf[0][:], xnt[pp][:]),
               reads=[gb(f"xnt{pp}")], writes=[gb("rmsbf0")])
          for kb in range(KB):
              P.mm("tensor", lambda e, S, pp=pp, kb=kb: e.transpose(
                      p_tr[:, kb * 128:(kb + 1) * 128],
                      rms_bf[0][:, kb * 128:(kb + 1) * 128], eye_bf[:]),
                   reads=[gb("rmsbf0"), gb("eye_bf")], psum=gb("p_tr"),
                   start=(kb == 0), stop=(kb == KB - 1))
          P.op("scalar", lambda e, S, tt=tt: e.copy(
                  xn_T[:].rearrange("p (kb t) -> p kb t", kb=KB)[:, :, tt * 128:(tt + 1) * 128],
                  p_tr[:].rearrange("p (kb c) -> p kb c", kb=KB)),
               reads=[gb("p_tr")], writes=[gb("xn_T")])
      return f
    def x_closures(k):
        full = k >= NCH_PRE
        dbk = k % 2
        out = []

        def inproj_conv(cb):
            def f(cb=cb):
                wb = cb % 2
                xb = xbr[wb]
                load(wblk[wb][:], scr_win[cb, :, :], f"wblk{wb}", reads=[gb("winscr")])
                pm, pmb = (p_mm, gb("p_mm")) if cb % 2 == 0 else (p_mm2, gb("p_mm2"))
                for kb in range(KB):
                    P.mm("tensor", lambda e, S, pm=pm, wb=wb, kb=kb, dbk=dbk: e.matmul(
                            pm[:, 0:T], wblk[wb][:, kb * 128:(kb + 1) * 128],
                            rms_T[dbk][:, kb * T:(kb + 1) * T],
                            start=(kb == 0), stop=(kb == KB - 1)),
                         reads=[gb(f"wblk{wb}"), gb(f"rms_T{dbk}")], psum=pmb,
                         start=(kb == 0), stop=(kb == KB - 1))
                P.op("scalar", lambda e, S, pm=pm, xb=xb: e.copy(
                        xb[:, 4: 4 + T], pm[:, 0:T]),
                     reads=[pmb], writes=[gb(f"xbrd{wb}")])
                P.op("vector", lambda e, S, cb=cb, xb=xb: e.tensor_copy(
                        xb[:, 1:4], halo_t[:, cb * 4 + 1: cb * 4 + 4]),
                     reads=[gb(f"halo{cb}")], writes=[gb(f"xbrh{wb}")])
                for tap in range(4):
                    P.op("vector", lambda e, S, cb=cb, tap=tap, wb=wb: e.tensor_scalar(
                            diag_t[0][:, tap * 128:(tap + 1) * 128], eye_bf[:],
                            conv_w_t[:, cb * 4 + tap:cb * 4 + tap + 1], None, ALU.mult),
                         reads=[gb("eye_bf"), gb("conv_w")], writes=[gb("diag0")])
                for tap in range(4):
                    P.mm("tensor", lambda e, S, tap=tap, xb=xb: e.matmul(
                            p_conv[:, 0:T], diag_t[0][:, tap * 128:(tap + 1) * 128],
                            xb[:, 1 + tap: 1 + tap + T],
                            start=(tap == 0), stop=(tap == 3)),
                         reads=[gb("diag0"), gb(f"xbrd{wb}"), gb(f"xbrh{wb}")],
                         psum=gb("p_conv"), start=(tap == 0), stop=(tap == 3))
                if cb < CBS:
                    P.op("scalar", lambda e, S, cb=cb, dbk=dbk: e.activation(
                            u_bf[dbk][:, cb * T:(cb + 1) * T], p_conv[:, 0:T], AF.Silu,
                            bias=conv_b_t[:, cb:cb + 1]),
                         reads=[gb("p_conv"), gb("conv_b")], writes=[gb(f"u{dbk}_{cb}")])
                else:
                    P.op("scalar", lambda e, S, cb=cb: e.activation(
                            u_oth[:, (cb - CBS) * T:(cb - CBS + 1) * T], p_conv[:, 0:T],
                            AF.Silu, bias=conv_b_t[:, cb:cb + 1]),
                         reads=[gb("p_conv"), gb("conv_b")], writes=[gb(f"uo_{cb}")])
                P.op("vector", lambda e, S, cb=cb, xb=xb: e.tensor_copy(
                        halo_t[:, cb * 4 + 1: cb * 4 + 4], xb[:, T + 1: T + 4]),
                     reads=[gb(f"xbrd{wb}")], writes=[gb(f"halo{cb}")])
            return f

        def xproj():
            def f():
                def usrc(cb):
                    if cb < CBS:
                        return u_bf[dbk][:, cb * T:(cb + 1) * T], gb(f"u{dbk}_{cb}")
                    return u_oth[:, (cb - CBS) * T:(cb - CBS + 1) * T], gb(f"uo_{cb}")
                for cb in range(CB):
                    ua, ub = usrc(cb)
                    P.mm("tensor", lambda e, S, cb=cb, ua=ua: e.matmul(
                            p_mm[:, 0:T], xpj[:, cb * 160: cb * 160 + 128], ua,
                            start=(cb == 0), stop=(cb == CB - 1)),
                         reads=[gb("xpj"), ub], psum=gb("p_mm"),
                         start=(cb == 0), stop=(cb == CB - 1))
                for cb in range(CB):
                    ua, ub = usrc(cb)
                    P.mm("tensor", lambda e, S, cb=cb, ua=ua: e.matmul(
                            p_mm2[0:32, 0:T], xpj[:, cb * 160 + 128: (cb + 1) * 160], ua,
                            start=(cb == 0), stop=(cb == CB - 1)),
                         reads=[gb("xpj"), ub], psum=gb("p_mm2"),
                         start=(cb == 0), stop=(cb == CB - 1))
                P.op("scalar", lambda e, S, dbk=dbk: e.copy(dtl_bf[dbk][:], p_mm[:, 0:T]),
                     reads=[gb("p_mm")], writes=[gb(f"dtl{dbk}")])
                P.op("scalar", lambda e, S, k=k: e.activation(bc_sb[:], p_mm2[0:32, 0:T],
                                                              AF.Copy, scale=mask_t[0:32, k:k + 1]),
                     reads=[gb("p_mm2"), gb("mask")], writes=[gb("bc_sb")])
                P.op("scalar", lambda e, S, dbk=dbk: e.dma_start(out=scr_bc[dbk], in_=bc_sb[:]),
                     reads=[gb("bc_sb")], writes=[gb(f"sbc{dbk}")], dma=True)
            return f

        def bcasts():
            def f():
                bsrc = scr_bc[dbk:dbk + 1].rearrange("c a b -> c (a b)")
                P.op("sync", lambda e, S, bsrc=bsrc: e.dma_start(
                        out=brep[:],
                        in_=bsrc[:, 0:N * T].broadcast_to((128, N * T))),
                     reads=[gb(f"sbc{dbk}")],
                     writes=[gb(f"brep{nn}") for nn in range(N)], dma=True)
                P.op("sync", lambda e, S, bsrc=bsrc: e.dma_start(
                        out=crep[:],
                        in_=bsrc[:, N * T:2 * N * T].broadcast_to((128, N * T))),
                     reads=[gb(f"sbc{dbk}")],
                     writes=[gb(f"crep{nn}") for nn in range(N)], dma=True)
            return f

        for cb in range(CB):
            out.append((0, inproj_conv(cb)))
        out.append((0, xproj()))
        out.append((CBS, bcasts()))
        return out

    def ffn_closures(m):
        if m < 0 or m >= LH // T:
            return []
        items = []
        for tt in range(TT):
            items.append(_ln_xnT(m, tt))
        HF = 2 if TT >= 2 else 1
        fg, fd = _ffn_makers(m)
        for hf in range(HF):
            for hj in range(HB):
                items.append(fg(hf, hj))
            for tl in range(TT // HF):
                items.append(fd(hf, tl))
        nit = len(items)
        return [((i * (CBS - 2)) // nit, f) for i, f in enumerate(items)]

    def _ffn_makers(k):
        dbk = k % 2
        NGU = (2 if TT >= 2 else 1) * HB

        def gu_load(idx):
            hj = idx % HB
            wb = idx % 3
            qe = "sync" if idx % 2 == 0 else "scalar"
            load(wffn[wb][:], wg_d[hj, :, :], f"wffn{wb}", eng=qe)
            load(wffn2[wb][:], wu_d[hj, :, :], f"wffn2{wb}", eng=qe)

        def ffn_gu(hf, hj):
            idx = hf * HB + hj
            def f(hf=hf, hj=hj, idx=idx):
                if idx == 0:
                    gu_load(0)
                    gu_load(1)
                if idx + 2 < NGU:
                    gu_load(idx + 2)
                wb = idx % 3
                for kb in range(KB):
                    P.mm("tensor", lambda e, S, wb=wb, kb=kb, hf=hf: e.matmul(
                            p_mm[:, 0:TF], wffn[wb][:, kb * 128:(kb + 1) * 128],
                            xn_T[:, kb * T + hf * TF: kb * T + (hf + 1) * TF],
                            start=(kb == 0), stop=(kb == KB - 1)),
                         reads=[gb(f"wffn{wb}"), gb("xn_T")], psum=gb("p_mm"),
                         start=(kb == 0), stop=(kb == KB - 1))
                for kb in range(KB):
                    P.mm("tensor", lambda e, S, wb=wb, kb=kb, hf=hf: e.matmul(
                            p_mm2[:, 0:TF], wffn2[wb][:, kb * 128:(kb + 1) * 128],
                            xn_T[:, kb * T + hf * TF: kb * T + (hf + 1) * TF],
                            start=(kb == 0), stop=(kb == KB - 1)),
                         reads=[gb(f"wffn2{wb}"), gb("xn_T")], psum=gb("p_mm2"),
                         start=(kb == 0), stop=(kb == KB - 1))
                s0, s1 = 0, 1
                P.op("scalar", lambda e, S, s0=s0: e.activation(gu_s[s0][:], p_mm[:, 0:TF],
                                                                AF.Silu, scale=QINV),
                     reads=[gb("p_mm")], writes=[gb(f"gu_s{s0}")])
                P.op("scalar", lambda e, S, s1=s1: e.activation(gu_s[s1][:], p_mm2[:, 0:TF],
                                                                AF.Copy, scale=QINV),
                     reads=[gb("p_mm2")], writes=[gb(f"gu_s{s1}")])
                P.op("gpsimd", lambda e, S, hj=hj, s0=s0, s1=s1: e.tensor_tensor(
                        hffn[:, hj * TF:(hj + 1) * TF], gu_s[s0][:], gu_s[s1][:], ALU.mult),
                     reads=[gb(f"gu_s{s0}"), gb(f"gu_s{s1}")], writes=[gb(f"h{hj}")])
            return f

        def dn_load(pu, tl):
            wb = pu % 2
            qe = "sync" if pu % 2 == 0 else "scalar"
            s = wd_d[pu * 256:(pu + 1) * 256, :].rearrange("(h p) m -> p h m", p=128)
            load(wdn[wb][:].rearrange("p (h m) -> p h m", h=2), s, f"wdn{wb}", eng=qe)

        def ffn_dn(hf, tl):
            def f(hf=hf, tl=tl):
                tt = hf * (TT // (2 if TT >= 2 else 1)) + tl
                dn_load(0, tl)
                for pu in range(HB // 2):
                    wb = pu % 2
                    if pu + 1 < HB // 2:
                        dn_load(pu + 1, tl)
                    for h in range(2):
                        hj = 2 * pu + h
                        for dh in range(2):
                            P.mm("tensor", lambda e, S, hj=hj, h=h, wb=wb, dh=dh, tl=tl: e.matmul(
                                    p_o[:, dh * 512:(dh + 1) * 512],
                                    hffn[:, hj * TF + tl * 128: hj * TF + tl * 128 + 128],
                                    wdn[wb][:, h * DM + dh * 512: h * DM + (dh + 1) * 512],
                                    start=(hj == 0), stop=(hj == HB - 1)),
                                 reads=[gb(f"h{hj}"), gb(f"wdn{wb}")], psum=gb("p_o"),
                                 start=(hj == 0 and dh == 0), stop=(hj == HB - 1 and dh == 1))
                P.op("scalar", lambda e, S, tt=tt: e.activation(
                        g_t[:], p_o[:], AF.Silu, scale=QINV),
                     reads=[gb("p_o")], writes=[gb("g_raw")])
                P.op("scalar", lambda e, S, tt=tt: e.dma_start(
                        out=cc_gin[k, tt * 128:(tt + 1) * 128, :], in_=g_t[:]),
                     reads=[gb("g_raw")], writes=[gb(f"sg{k}_{tt}")], dma=True)
            return f

        return ffn_gu, ffn_dn

    def o_closures(k):
        if k < 0:
            return []
        t0 = k * T
        out = []

        def oload():
            def f():
                pw = ([gb(f"pci{r}") for r in range(4)] + [gb(f"pco{r}") for r in range(4)]
                      if k == 0 else [])
                load(yg_lo[:], cc_yout[k, 0:128, :], "yglo", eng="sync",
                     reads=[gb(f"ccyo{k}")], war=pw)
                load(yg_hi[:], cc_yout[k, 128:256, :], "yghi", eng="sync",
                     reads=[gb(f"ccyo{k}")], war=pw)
            return f

        def wo_load(pu):
            wb = pu % 2
            qe = "sync" if pu % 2 == 0 else "scalar"
            s = scr_wout[pu * 256:(pu + 1) * 256, :].rearrange("(h p) m -> p h m", p=128)
            load(woutb[wb][:].rearrange("p (h m) -> p h m", h=2), s, f"woutb{wb}", eng=qe,
                 reads=[gb("woutscr")])

        def omm(tt):
            def f(tt=tt):
                wo_load(0)
                for pu in range(CB // 2):
                    wb = pu % 2
                    if pu + 1 < CB // 2:
                        wo_load(pu + 1)
                    for h2 in range(2):
                        cb = 2 * pu + h2
                        src_t = yg_lo if cb < CBS else yg_hi
                        sk = "yglo" if cb < CBS else "yghi"
                        cbl = cb % CBS
                        for dh in range(2):
                            P.mm("tensor", lambda e, S, cb=cb, cbl=cbl, h2=h2, wb=wb, dh=dh, tt=tt, src_t=src_t: e.matmul(
                                    p_o[:, dh * 512:(dh + 1) * 512],
                                    src_t[:, cbl * T + tt * 128: cbl * T + (tt + 1) * 128],
                                    woutb[wb][:, h2 * DM + dh * 512: h2 * DM + (dh + 1) * 512],
                                    start=(cb == 0), stop=(cb == CB - 1)),
                                 reads=[gb(sk), gb(f"woutb{wb}")], psum=gb("p_o"),
                                 start=(cb == 0 and dh == 0), stop=(cb == CB - 1 and dh == 1))
            return f

        def ofin(tt):
            def f(tt=tt):
                row = t0 + tt * 128
                pp = tt % 2
                load(xt[pp][:], scr_xn[row:row + 128, :], f"xt{pp}", eng="gpsimd",
                     reads=[gb(f"sxn{k}_{tt}")])
                P.op("vector", lambda e, S, pp=pp: e.tensor_tensor(
                        xnt[pp][:], p_o[:], xt[pp][:], ALU.add),
                     reads=[gb("p_o"), gb(f"xt{pp}")], writes=[gb(f"xnt{pp}")])
                ln_rms_stats(xnt[pp][:], gb(f"xnt{pp}"), 7, is_rms=True)
                P.op("vector", lambda e, S, pp=pp: e.tensor_scalar_mul(
                        xnt[pp][:], xnt[pp][:], sc1[:, 7:8]),
                     reads=[gb(f"xnt{pp}"), gb("rstd7")], writes=[gb(f"xnt{pp}")])
                m = k if k < NCH // 2 else k - NCH // 2
                goff = (0 if k < NCH // 2 else T) + tt * 128
                load(g_t[:], cc_gout[m, goff:goff + 128, :], "g_raw", eng="gpsimd",
                     reads=[gb(f"ccgo{m}")])
                P.op("vector", lambda e, S, pp=pp: e.tensor_tensor(
                        xnt[pp][:], xnt[pp][:], normf_rep[:], ALU.mult),
                     reads=[gb(f"xnt{pp}"), gb("normf_rep")], writes=[gb(f"xnt{pp}")])
                P.op("vector", lambda e, S, pp=pp: e.tensor_tensor(
                        xnt[pp][:], xnt[pp][:], g_t[:], ALU.mult),
                     reads=[gb(f"xnt{pp}"), gb("g_raw")], writes=[gb(f"xnt{pp}")])
                load(xt[pp][:], x_d[row:row + 128, :], f"xt{pp}", eng="gpsimd")
                P.op("vector", lambda e, S, pp=pp: e.tensor_tensor(
                        xnt[pp][:], xnt[pp][:], xt[pp][:], ALU.add),
                     reads=[gb(f"xnt{pp}"), gb(f"xt{pp}")], writes=[gb(f"xnt{pp}")])
                orow = k * T + tt * 128
                P.op("scalar", lambda e, S, pp=pp, orow=orow: e.dma_start(
                        out=out_d[orow:orow + 128, :], in_=xnt[pp][:]),
                     reads=[gb(f"xnt{pp}")], writes=[gb("outw")], dma=True)
            return f

        out.append((0, oload()))
        for tt in range(TT):
            out.append((tt * 2, omm(tt)))
            out.append((tt * 2 + 1, ofin(tt)))
        return out

    def emit_scan(k, pumpq, first_budget=0):
        full = True
        dbk = k % 2

        def jit_zdt(cb):
            pp = cb % 2
            P.mm("tensor", lambda e, S, cb=cb, dbk=dbk: e.matmul(
                    p_conv[:, 0:T], dtw_bf[:, cb * 128:(cb + 1) * 128], dtl_bf[dbk][:],
                    start=True, stop=True),
                 reads=[gb("dtw_bf"), gb(f"dtl{dbk}")], psum=gb("p_conv"),
                 start=True, stop=True)
            P.op("scalar", lambda e, S, cb=cb, pp=pp: e.activation(
                    sp_t[pp][:], p_conv[:, 0:T], AF.Exp, bias=dtb_t[:, cb:cb + 1]),
                 reads=[gb("p_conv"), gb("dtb")], writes=[gb(f"sp{pp}")])
            P.op("scalar", lambda e, S, pp=pp: e.activation(
                    sp_t[pp][:], sp_t[pp][:], AF.Ln, bias=cst[:, 1:2]),
                 reads=[gb(f"sp{pp}"), gb("cst")], writes=[gb(f"sp{pp}")])
            if full:
                pmz, pmzb = ((p_mm, gb("p_mm")) if pp == 0 else (p_mm2, gb("p_mm2")))
                wb = pp
                load(wblk[wb][:], scr_win[CB + cb, :, :], f"wblk{wb}", reads=[gb("winscr")])
                for kb in range(KB):
                    P.mm("tensor", lambda e, S, pmz=pmz, wb=wb, kb=kb, dbk=dbk: e.matmul(
                            pmz[:, 0:T], wblk[wb][:, kb * 128:(kb + 1) * 128],
                            rms_T[dbk][:, kb * T:(kb + 1) * T],
                            start=(kb == 0), stop=(kb == KB - 1)),
                         reads=[gb(f"wblk{wb}"), gb(f"rms_T{dbk}")], psum=pmzb,
                         start=(kb == 0), stop=(kb == KB - 1))
                P.op("scalar", lambda e, S, pmz=pmz, pp=pp: e.activation(
                        z_bf[pp][:], pmz[:, 0:T], AF.Silu),
                     reads=[pmzb], writes=[gb(f"z{pp}")])

        nq = len(pumpq)
        budget_per = (nq + CBS - 1) // CBS + 1

        def pump(cur_cb, budget):
            while pumpq and budget > 0 and pumpq[0][0] <= cur_cb:
                pumpq.pop(0)[1]()
                budget -= 1

        jit_zdt(0)
        for cb in range(CBS):
            pp = cb % 2
            if cb + 1 < CBS:
                jit_zdt(cb + 1)
            pump(cb, 2 + (first_budget if cb == 0 else 0))
            if DEBUG_SCR:
                P.op("scalar", lambda e, S, cb=cb, k=k, dbk=dbk: e.dma_start(
                        out=scr_du[k, :, cb * T:(cb + 1) * T],
                        in_=u_bf[dbk][:, cb * T:(cb + 1) * T]),
                     reads=[gb(f"u{dbk}_{cb}")], writes=[gb("dbgu")], dma=True)
            P.op("vector", lambda e, S, cb=cb, pp=pp, dbk=dbk: e.tensor_tensor(
                    dtu_t[pp][:], sp_t[pp][:], u_bf[dbk][:, cb * T:(cb + 1) * T], ALU.mult),
                 reads=[gb(f"sp{pp}"), gb(f"u{dbk}_{cb}")], writes=[gb(f"dtu{pp}")])
            for g in range(NGRP):
                if g:
                    pump(cb, 1)
                gp = (cb * NGRP + g) % 3
                yhi = (cb * NGRP + g) % 3
                P.op("scalar", lambda e, S, cb=cb, gp=gp, nn=g, pp=pp: e.activation(
                        a_t[gp][:], sp_t[pp][:],
                        AF.Exp, scale=acol_t[:, cb * N + nn: cb * N + nn + 1]),
                     reads=[gb(f"sp{pp}"), gb("acol")], writes=[gb(f"a{gp}")])
                bp = g % 2
                P.op("vector", lambda e, S, pp=pp, bp=bp, nn=g: e.tensor_tensor(
                        b_t[bp][:], dtu_t[pp][:],
                        brep[:, nn * T:(nn + 1) * T], ALU.mult),
                     reads=[gb(f"dtu{pp}"), gb(f"brep{g}")], writes=[gb(f"b{bp}")])
                P.op("vector", lambda e, S, cb=cb, gp=gp, bp=bp, nn=g: e.tensor_tensor_scan(
                        a_t[gp][:], a_t[gp][:], b_t[bp][:],
                        carry[:, cb * N + nn: cb * N + nn + 1], ALU.mult, ALU.add),
                     reads=[gb(f"a{gp}"), gb(f"b{bp}"), gb("carry")],
                     writes=[gb(f"a{gp}")])
                P.op("vector", lambda e, S, cb=cb, gp=gp, nn=g: e.tensor_copy(
                        carry[:, cb * N + nn: cb * N + nn + 1], a_t[gp][:, T - 1:T]),
                     reads=[gb(f"a{gp}")], writes=[gb("carry")])
                if full:
                    P.op("gpsimd", lambda e, S, gp=gp, nn=g, yhi=yhi: e.tensor_tensor(
                            yh_t[yhi][:], a_t[gp][:],
                            crep[:, nn * T:(nn + 1) * T], ALU.mult),
                         reads=[gb(f"a{gp}"), gb(f"crep{g}")], writes=[gb(f"yh{yhi}")])
                    P.mm("tensor", lambda e, S, cb=cb, yhi=yhi, g=g: e.matmul(
                            p_ys[cb % 2][:, 0:T], eye_bf[:], yh_t[yhi][:],
                            start=(g == 0), stop=(g == NGRP - 1)),
                         reads=[gb(f"yh{yhi}")], psum=gb(f"p_ys{cb % 2}"),
                         start=(g == 0), stop=(g == NGRP - 1))
            if full:
                P.op("vector", lambda e, S, cb=cb, pp=pp, dbk=dbk: e.scalar_tensor_tensor(
                        dtu_t[pp][:], u_bf[dbk][:, cb * T:(cb + 1) * T], dvec_t[:, cb:cb + 1],
                        p_ys[cb % 2][:, 0:T], ALU.mult, ALU.add),
                     reads=[gb(f"u{dbk}_{cb}"), gb("dvec"), gb(f"p_ys{cb % 2}")],
                     writes=[gb(f"dtu{pp}")])
                P.op("vector", lambda e, S, cb=cb, pp=pp: e.tensor_tensor(
                        yg_bf[:, cb * T:(cb + 1) * T], dtu_t[pp][:], z_bf[pp][:], ALU.mult),
                     reads=[gb(f"dtu{pp}"), gb(f"z{pp}")], writes=[gb(f"yg{cb}")])
                if DEBUG_SCR:
                    P.op("scalar", lambda e, S, cb=cb, k=k: e.dma_start(
                            out=scr_dyg[k, :, cb * T:(cb + 1) * T],
                            in_=yg_bf[:, cb * T:(cb + 1) * T]),
                         reads=[gb(f"yg{cb}")], writes=[gb("dbgyg")], dma=True)
        pump(CBS + CB, 10 ** 9)
        P.op("scalar", lambda e, S, k=k: e.dma_start(out=cc_yin[k], in_=yg_bf[:]),
             reads=[gb(f"yg{cb}") for cb in range(CBS)], writes=[gb(f"ccyi{k}")], dma=True)
        P.op("gpsimd", lambda e, S, k=k: e.collective_compute(
                "AllGather", ALU.bypass, ins=[cc_yin[k]], outs=[cc_yout[k]],
                replica_groups=RG),
             reads=[gb(f"ccyi{k}")], writes=[gb(f"ccyo{k}")], sem_override="cC")

    precast_tasks = []
    def precast_to_tasks(fn_emit_iters):
        for c in fn_emit_iters:
            precast_tasks.append((0, c))

    def gen_precast_big(srcd, dst, rows, cols, skey, i32):
        icols = (CBS * T) // 2
        CW = min(512, icols // 2)
        RING = icols // CW
        ivw = yg_lo[:].bitcast(I32 if i32 else F32)
        tiled3 = len(srcd.shape) == 3
        if tiled3:
            blocks = [(t, cc) for t in range(srcd.shape[0])
                      for cc in range(srcd.shape[2] // CW)]
        else:
            blocks = [(rb, cc) for rb in range(rows // 128) for cc in range(cols // CW)]
        nb = len(blocks)
        LA = min(3, RING - 1)

        def emit_load(i):
            rb, cc = blocks[i]
            r = i % RING
            qe = "sync" if i % 2 == 0 else "scalar"
            s = (srcd[rb, :, cc * CW:cc * CW + CW] if tiled3 else
                 srcd[rb * 128:(rb + 1) * 128, cc * CW:cc * CW + CW])
            load(ivw[:, r * CW:(r + 1) * CW], s, f"pci{r}", eng=qe)

        for bi in range(nb):
            def one(bi=bi):
                if bi == 0:
                    for jj in range(min(LA, nb)):
                        emit_load(jj)
                elif bi + LA - 1 < nb:
                    emit_load(bi + LA - 1)
                rb, cc = blocks[bi]
                c0 = cc * CW
                r = bi % RING
                P.op("vector", lambda e, S, r=r: e.tensor_copy(
                        yg_hi[:, r * CW:(r + 1) * CW], ivw[:, r * CW:(r + 1) * CW]),
                     reads=[gb(f"pci{r}")], writes=[gb(f"pco{r}")])
                qe = "scalar" if bi % 2 == 0 else "sync"
                dd = (dst[rb, :, c0:c0 + CW] if tiled3 else
                      dst[rb * 128:(rb + 1) * 128, c0:c0 + CW])
                P.op(qe, lambda e, S, dd=dd, r=r: e.dma_start(
                        out=dd, in_=yg_hi[:, r * CW:(r + 1) * CW]),
                     reads=[gb(f"pco{r}")], writes=[gb(skey + "scr")], dma=True)
            yield one

    precast_to_tasks(gen_precast_big(wout_d, scr_wout, DI, DM, "wout", False))

    emit_ln(0)
    fq0 = [f for _, f in ffn_closures(0)]
    n_head = TT + (2 if TT >= 2 else 1) * HB
    fq0_head, fq0_tail = fq0[:n_head], fq0[n_head:]
    for _, f in x_closures(0):
        f()
        for _ in range(3):
            if fq0_head:
                fq0_head.pop(0)()
    for f in fq0_head:
        f()
    for k in range(NCH):
        q = []
        q += o_closures(k - 2)
        if k == 1:
            nfq = len(fq0_tail)
            q += [((i * (CBS - 2)) // max(nfq, 1), f) for i, f in enumerate(fq0_tail)]
        else:
            q += ffn_closures(k - 1)
        lnA, _ = ln_closures(k + 1) if k + 1 < NCH else ([], [])
        q += lnA
        if k + 1 < NCH:
            q += x_closures(k + 1)
        if k == 0:
            q += precast_tasks
            del precast_tasks[:]
        emit_scan(k, q, first_budget=2)
        m = k - 1
        if 0 <= m < LH // T:
            P.op("gpsimd", lambda e, S, m=m: e.collective_compute(
                    "AllGather", ALU.bypass, ins=[cc_gin[m]], outs=[cc_gout[m]],
                    replica_groups=RG),
                 reads=[gb(f"sg{m}_{tt}") for tt in range(TT)], writes=[gb(f"ccgo{m}")],
                 sem_override="cC")
    for _, f in ffn_closures(NCH - 1):
        f()
    m = NCH - 1
    if m < LH // T:
        P.op("gpsimd", lambda e, S, m=m: e.collective_compute(
                "AllGather", ALU.bypass, ins=[cc_gin[m]], outs=[cc_gout[m]],
                replica_groups=RG),
             reads=[gb(f"sg{m}_{tt}") for tt in range(TT)], writes=[gb(f"ccgo{m}")],
             sem_override="cC")
    for _, f in o_closures(NCH - 2):
        f()
    for _, f in o_closures(NCH - 1):
        f()

    P.raw_wait("gpsimd", gb("outw").w)
    P.flush_queues("gpsimd")

    sems = {}
    sem_ctx = ExitStack()
    for s in ALL_SEMS:
        sems[s] = sem_ctx.enter_context(nc.semaphore(s))
    P.finalize_into(nc, sems)
    ctx.close()
    sem_ctx.close()
    return nc, dict(L=L, T=T, NCH=NCH, NCH_PRE=NCH_PRE, L_OWN=L_OWN, ninstr=P.ninstr)


def make_core_inputs(core_id, inputs, L=4096, NCH=8, NCH_PRE=0):
    b, j = core_id // 2, core_id % 2
    DM, DI, N = 1024, 2048, 16
    DIH = DI // 2
    CB, CBS = 16, 8
    T = L // NCH
    LH = L // 2
    x = np.asarray(inputs["L0_attn_x"][b], dtype=np.float32)
    own = slice(j * DIH, (j + 1) * DIH)
    oth = slice((1 - j) * DIH, (2 - j) * DIH)
    perm = np.r_[np.arange(j * DIH, (j + 1) * DIH), np.arange((1 - j) * DIH, (2 - j) * DIH)]
    ipw = np.asarray(inputs["L0_attn_in_proj_w"], np.float32)
    w_in = np.concatenate([ipw[perm, :], ipw[2048 + j * DIH: 2048 + (j + 1) * DIH, :]], axis=0).T
    cw = np.asarray(inputs["L0_attn_conv_w"], np.float32)[perm]
    cb_ = np.asarray(inputs["L0_attn_conv_b"], np.float32)[perm]
    xpw = np.asarray(inputs["L0_attn_x_proj_w"], np.float32)[:, perm]
    dtw = np.asarray(inputs["L0_attn_dt_proj_w"], np.float32)[own]
    dtb = np.asarray(inputs["L0_attn_dt_proj_b"], np.float32)[own]
    alog = np.asarray(inputs["L0_attn_A_log"], np.float32)[own]
    dvec = np.asarray(inputs["L0_attn_D"], np.float32)[own]
    d = {
        "x": np.ascontiguousarray(x),
        "xf": np.ascontiguousarray(x[j * LH:(j + 1) * LH]),
        "mask": np.ones((128, NCH), np.float32),
        "w_in": np.ascontiguousarray(w_in),
        "norm_w": np.ascontiguousarray(np.asarray(inputs["L0_attn_norm_w"], np.float32).reshape(8, 128).T),
        "conv_w_r": np.ascontiguousarray(cw.reshape(CB, 128, 4).transpose(1, 0, 2).reshape(128, CB * 4)),
        "conv_b_r": np.ascontiguousarray(cb_.reshape(CB, 128).T),
        "xproj_T": np.ascontiguousarray(xpw.T),
        "dtw_T": np.ascontiguousarray(dtw.T),
        "dtb_r": np.ascontiguousarray(dtb.reshape(CBS, 128).T),
        "alog_r": np.ascontiguousarray(alog.reshape(CBS, 128, N).transpose(1, 0, 2).reshape(128, CBS * N)),
        "d_r": np.ascontiguousarray(dvec.reshape(CBS, 128).T),
        "wout_T": np.ascontiguousarray(np.asarray(inputs["L0_attn_out_proj_w"], np.float32).T),
        "norm_f": np.ascontiguousarray(np.asarray(inputs["L0_attn_norm_f_w"], np.float32).reshape(1, DM)),
        "wg_T": _tile_ffn(np.asarray(inputs["L0_ffn_w_gate"], np.int32).T.astype(_BF)),
        "wu_T": _tile_ffn(np.asarray(inputs["L0_ffn_w_up"], np.int32).T.astype(_BF)),
        "wd_T": np.ascontiguousarray(np.asarray(inputs["L0_ffn_w_down"], np.int32).T.astype(_BF)),
        "eye": np.eye(128, dtype=np.float32),
    }
    return d


def _tile_ffn(wT):
    DM, HID = wT.shape
    return np.ascontiguousarray(
        wT.reshape(DM // 128, 128, HID // 128, 128).transpose(2, 1, 0, 3)
          .reshape(HID // 128, 128, DM))


def assemble_output(results, L=4096, NCH=8, NCH_PRE=0, B=4):
    DM = 1024
    LH = L // 2
    out = np.empty((B, L, DM), np.float32)
    for core_id in range(2 * B):
        b, j = core_id // 2, core_id % 2
        o = results[core_id]["out"]
        out[b, j * LH:(j + 1) * LH] = o[j * LH:(j + 1) * LH]
    return out


from concourse.bass_utils import run_bass_kernel_spmd

_BUILT = {}

def _get_program():
    if "nc" not in _BUILT:
        nc, meta = build_program(L=4096, T=512)
        _BUILT["nc"] = nc
        _BUILT["meta"] = meta
    return _BUILT["nc"], _BUILT["meta"]


def kernel(**inputs):
    nc, meta = _get_program()
    in_maps = [make_core_inputs(c, inputs, L=meta["L"], NCH=meta["NCH"],
                                NCH_PRE=meta["NCH_PRE"]) for c in range(8)]
    res = run_bass_kernel_spmd(nc, in_maps, core_ids=list(range(8)))
    return assemble_output(list(res.results), L=meta["L"], NCH=meta["NCH"],
                           NCH_PRE=meta["NCH_PRE"], B=4)


# version 29
# speedup vs baseline: 1.0944x; 1.0944x over previous
import numpy as np
import ml_dtypes
_BF = ml_dtypes.bfloat16
from contextlib import ExitStack
import concourse.bass as bass
import concourse.mybir as mybir

F32 = mybir.dt.float32
BF16 = mybir.dt.bfloat16
I32 = mybir.dt.int32
AF = mybir.ActivationFunctionType
ALU = mybir.AluOpType

ENGINES = ("sync", "scalar", "vector", "tensor", "gpsimd")
CSEM = {"scalar": "cS", "vector": "cV", "tensor": "cT", "gpsimd": "cG"}
DQ = {"sync": tuple(f"dS{i}" for i in range(8)),
      "scalar": tuple(f"dA{i}" for i in range(8)),
      "gpsimd": tuple(f"dG{i}" for i in range(4))}
ALL_SEMS = ("cS", "cV", "cT", "cG", "cC") + DQ["sync"] + DQ["scalar"] + DQ["gpsimd"]
BATCH_MAX = 2


class Ev:
    __slots__ = ("sem", "val")
    def __init__(self, sem, val):
        self.sem = sem
        self.val = val


class DmaEv:
    __slots__ = ("q", "batch")
    def __init__(self, q, batch):
        self.q = q
        self.batch = batch


class PendEv:
    __slots__ = ("eng", "ev")
    def __init__(self, eng):
        self.eng = eng
        self.ev = None


class Buf:
    __slots__ = ("w", "reads")
    def __init__(self):
        self.w = None
        self.reads = []


class Prog:
    def __init__(self):
        self.ops = {e: [] for e in ENGINES}
        self.ctr = {s: 0 for s in ALL_SEMS}
        self.waited = {e: {s: 0 for s in ALL_SEMS} for e in ENGINES}
        self.ninstr = 0
        self.q = {e: dict(nopen=0, nbatch=0, open=None) for e in DQ}
        self.pending = {e: [] for e in ENGINES}

    def _close_batch(self, qeng, which=None):
        st = self.q[qeng]
        if st["open"] is not None:
            batch = st["open"]
            batch[0] = True
            batch[2] = self.ctr[batch[1]]
            st["open"] = None
            st["nopen"] = 0

    def _resolve_pending(self, eng2):
        sem = CSEM[eng2]
        self.ctr[sem] += 1
        val = self.ctr[sem]
        self.ops[eng2].append(lambda e, S, _s=sem: e.nop().then_inc(S[_s], 1))
        self.ninstr += 1
        ev = Ev(sem, val)
        for p in self.pending[eng2]:
            p.ev = ev
        self.pending[eng2] = []
        return ev

    def _wait(self, eng, ev, raw=True):
        if ev is None:
            return
        if isinstance(ev, PendEv):
            if ev.ev is None:
                self._resolve_pending(ev.eng)
            ev = ev.ev
        if isinstance(ev, DmaEv):
            if not ev.batch[0]:
                self._close_batch(ev.q)
            ev = Ev(ev.batch[1], ev.batch[2])
        if self.waited[eng].get(ev.sem, 0) >= ev.val:
            return
        self.waited[eng][ev.sem] = ev.val
        sem_name, val = ev.sem, ev.val
        self.ops[eng].append(lambda e, S, _s=sem_name, _v=val: e.wait_ge(S[_s], _v))
        self.ninstr += 1

    def op(self, eng, fn, reads=(), writes=(), war=(), dma=False, track=True,
           sem_override=None):
        for b in reads:
            self._wait(eng, b.w, raw=True)
        for b in war:
            self._wait(eng, b.w, raw=False)
            for r in b.reads:
                self._wait(eng, r, raw=False)
        for b in writes:
            self._wait(eng, b.w, raw=False)
            for r in b.reads:
                self._wait(eng, r, raw=False)
        if dma:
            st = self.q[eng]
            if st["open"] is None or st["nopen"] >= BATCH_MAX:
                self._close_batch(eng)
                pool = DQ[eng]
                psem = pool[st["nbatch"] % len(pool)]
                st["nbatch"] += 1
                self._wait(eng, Ev(psem, self.ctr[psem]))
                st["open"] = [False, psem, 0]
            batch = st["open"]
            psem = batch[1]
            self.ctr[psem] += 16
            st["nopen"] += 1
            def run(e, S, fn=fn, psem=psem):
                fn(e, S).then_inc(S[psem], 16)
            ev = DmaEv(eng, batch)
            self.ops[eng].append(run)
            self.ninstr += 1
            for b in writes:
                b.w = ev
                b.reads = []
            for b in reads:
                b.reads.append(ev)
            return ev
        if track:
            sem = sem_override or CSEM[eng]
            self.ctr[sem] += 1
            val = self.ctr[sem]
            def run(e, S, fn=fn, sem=sem):
                fn(e, S).then_inc(S[sem], 1)
            ev = Ev(sem, val)
            for p in self.pending[eng]:
                p.ev = ev
            self.pending[eng] = []
        else:
            def run(e, S, fn=fn):
                fn(e, S)
            ev = None
        self.ops[eng].append(run)
        self.ninstr += 1
        if track:
            for b in writes:
                b.w = ev
                b.reads = []
            for b in reads:
                b.reads.append(ev)
        else:
            pe = PendEv(eng)
            self.pending[eng].append(pe)
            for b in reads:
                b.reads.append(pe)
        return ev

    def mm(self, eng, fn, reads=(), psum=None, start=False, stop=False):
        if start and stop:
            return self.op(eng, fn, reads=reads, writes=[psum])
        if start:
            return self.op(eng, fn, reads=reads, war=[psum])
        if stop:
            return self.op(eng, fn, reads=reads, writes=[psum])
        return self.op(eng, fn, reads=reads)

    def raw_wait(self, eng, ev):
        self._wait(eng, ev)

    def flush_queues(self, eng):
        for qeng in DQ:
            self._close_batch(qeng)
            for psem in DQ[qeng]:
                self._wait(eng, Ev(psem, self.ctr[psem]))

    def finalize_into(self, nc, sems):
        with nc.Block() as block:
            for eng in ENGINES:
                ops = self.ops[eng]
                if not ops:
                    continue
                getattr(block, eng)(_mkbody(ops, sems))


def _mkbody(ops, sems):
    def body(e):
        for fn in ops:
            fn(e, sems)
    return body


DEBUG_SCR = False

def build_program(L=4096, T=512, NCH_PRE=None):
    DM, DI, N, RANK, HID = 1024, 2048, 16, 128, 3072
    NCH = L // T
    NCH_PRE = 0
    L_OWN = L
    DIH = DI // 2
    CBS = DIH // 128
    LH = L // 2
    RG = [[0, 1], [2, 3], [4, 5], [6, 7]]
    KB = DM // 128
    CB = DI // 128
    HB = HID // 128
    TT = T // 128
    NG = 1
    NGRP = N // NG
    TF = T // 2 if T >= 256 else T
    EPS = 1e-5
    QINV = float(np.float32(1.0 / 127.0))
    NCO = (DI + DIH) // 128

    nc = bass.Bass("TRN2", target_bir_lowering=False, debug=False)

    def din(name, shape, dt):
        return nc.dram_tensor(name, list(shape), dt, kind="ExternalInput").ap()

    x_d = din("x", (L, DM), F32)
    mask_d = din("mask", (128, NCH), F32)
    w_in_d = din("w_in", (DM, DI + DIH), F32)
    norm_w_d = din("norm_w", (128, KB), F32)
    conv_w_d = din("conv_w_r", (128, CB * 4), F32)
    conv_b_d = din("conv_b_r", (128, CB), F32)
    xproj_d = din("xproj_T", (DI, 160), F32)
    dtw_d = din("dtw_T", (RANK, DIH), F32)
    dtb_d = din("dtb_r", (128, CBS), F32)
    alog_d = din("alog_r", (128, CBS * N), F32)
    dvec_d = din("d_r", (128, CBS), F32)
    wout_d = din("wout_T", (DI, DM), F32)
    normf_d = din("norm_f", (1, DM), F32)
    wg_d = din("wg_T", (HB, 128, KB * 128), BF16)
    wu_d = din("wu_T", (HB, 128, KB * 128), BF16)
    wd_d = din("wd_T", (HID, DM), BF16)
    eye_d = din("eye", (128, 128), F32)
    xf_d = din("xf", (LH, DM), F32)

    out_d = nc.dram_tensor("out", [L_OWN, DM], F32, kind="ExternalOutput").ap()

    def scr(name, shape, dt):
        return nc.dram_tensor(name, list(shape), dt, kind="Internal").ap()

    scr_win = scr("scr_win", (NCO, 128, KB * 128), BF16)
    scr_wout = scr("scr_wout", (DI, DM), BF16)
    scr_bc = scr("scr_bc", (2, 32, T), BF16)
    scr_xn = scr("scr_xn", (L, DM), F32)
    scr_nf = scr("scr_nf", (1, DM), F32)
    cc_yin = scr("cc_yin", (NCH, 128, CBS * T), BF16)
    cc_yout = scr("cc_yout", (NCH, 256, CBS * T), BF16)
    cc_gin = scr("cc_gin", (NCH // 2, T, DM), BF16)
    cc_gout = scr("cc_gout", (NCH // 2, 2 * T, DM), BF16)
    scr_du = scr("scr_du", (NCH, 128, CB * T), BF16) if DEBUG_SCR else None
    scr_dyg = scr("scr_dyg", (NCH, 128, CB * T), BF16) if DEBUG_SCR else None

    ctx = ExitStack()
    sb = lambda n, s, d: ctx.enter_context(nc.sbuf_tensor(n, list(s), d))
    psx = lambda n, s, d: ctx.enter_context(nc.psum_tensor(n, list(s), d))

    eye_bf = sb("eye_bf", (128, 128), BF16)
    norm_w_t = sb("norm_w_t", (128, KB), F32)
    conv_w_t = sb("conv_w_t", (128, CB * 4), F32)
    conv_b_t = sb("conv_b_t", (128, CB), F32)
    dtb_t = sb("dtb_t", (128, CBS), F32)
    acol_t = sb("acol_t", (128, CBS * N), F32)
    dvec_t = sb("dvec_t", (128, CBS), F32)
    mask_t = sb("mask_t", (128, NCH), F32)
    normf_rep = sb("normf_rep", (128, DM), BF16)
    carry = sb("carry", (128, CBS * N), F32)
    xpj = sb("xpj", (128, CB * 160), BF16)
    dtw_bf = sb("dtw_bf", (128, DIH), BF16)
    stat6 = sb("stat6", (128, 12), F32)
    stat2 = sb("stat2", (128, 2), F32)
    sc1 = sb("sc1", (128, 8), F32)
    cst = sb("cst", (128, 2), F32)
    xt = [sb(f"xt{i}", (128, DM), F32) for i in range(2)]
    xnt = [sb(f"xnt{i}", (128, DM), F32) for i in range(2)]
    rms_bf = [sb(f"rmsbf{i}", (128, DM), BF16) for i in range(1)]
    rms_T = [sb(f"rms_T{i}", (128, KB * T), BF16) for i in range(2)]
    xn_T = sb("xn_T", (128, KB * T), BF16)
    xbr = [sb(f"xbr{i}", (128, T + 4), BF16) for i in range(2)]
    halo_t = sb("halo_t", (128, CB * 4), BF16)
    u_bf = [sb(f"u_bf{i}", (128, CBS * T), BF16) for i in range(2)]
    u_oth = sb("u_oth", (128, (CB - CBS) * T), BF16)
    z_bf = [sb(f"zbf{i}", (128, T), BF16) for i in range(2)]
    dtl_bf = [sb(f"dtl{i}", (128, T), BF16) for i in range(2)]
    sp_t = [sb(f"sp{i}", (128, T), F32) for i in range(2)]
    dtu_t = [sb(f"dtu{i}", (128, T), BF16) for i in range(2)]
    a_t = [sb(f"a{i}", (128, NG * T), F32) for i in range(3)]
    b_t = [sb(f"b{i}", (128, NG * T), F32) for i in range(2)]
    yh_t = [sb(f"yh{i}", (128, NG * T), BF16) for i in range(3)]
    brep = sb("brep", (128, N * T), BF16)
    crep = sb("crep", (128, N * T), BF16)
    bc_sb = sb("bc_sb", (32, T), BF16)
    yg_bf = sb("yg_bf", (128, CBS * T), BF16)
    yg_lo = sb("yg_lo", (128, CBS * T), BF16)
    yg_hi = sb("yg_hi", (128, CBS * T), BF16)
    diag_t = [sb("diag0", (128, 4 * 128), BF16)]
    wblk = [sb(f"wblk{i}", (128, KB * 128), BF16) for i in range(2)]
    wffn = [sb(f"wffn{i}", (128, KB * 128), BF16) for i in range(3)]
    wffn2 = [sb(f"wffn2{i}", (128, KB * 128), BF16) for i in range(3)]
    wdn = [sb(f"wdn{i}", (128, 2 * DM), BF16) for i in range(2)]
    woutb = [sb(f"woutb{i}", (128, 2 * DM), BF16) for i in range(2)]
    gu_s = [sb(f"gu_s{i}", (128, TF), BF16) for i in range(2)]
    hffn = sb("hffn", (128, HB * TF), BF16)
    g_t = sb("g_t", (128, DM), BF16)
    castf_t = sp_t[0] if T >= 256 else sb("cast_fs", (128, 256), F32)
    castf_key = "sp0" if T >= 256 else "cast_fs"

    p_mm = psx("p_mm", (128, 512), F32)
    p_mm2 = psx("p_mm2", (128, 512), F32)
    p_conv = psx("p_conv", (128, 512), F32)
    p_ys = [psx(f"p_ys{i}", (128, 512), F32) for i in range(2)]
    p_tr = psx("p_tr", (128, 1024), BF16)
    p_o = psx("p_o", (128, 1024), F32)

    bufs = {}
    def gb(key):
        if key not in bufs:
            bufs[key] = Buf()
        return bufs[key]

    P = Prog()

    def load(dst_ap, src_ap, key=None, eng="sync", reads=(), writes=(), war=()):
        w = list(writes)
        if key is not None:
            w.append(gb(key))
        return P.op(eng, lambda e, S, d=dst_ap, s=src_ap: e.dma_start(out=d, in_=s),
                    reads=list(reads), writes=w, war=list(war), dma=True)

    load(xt[0][:, 0:128], eye_d[:], "xt0")
    P.op("vector", lambda e, S: e.tensor_copy(eye_bf[:], xt[0][:, 0:128]),
         reads=[gb("xt0")], writes=[gb("eye_bf")])
    load(norm_w_t[:], norm_w_d[:], "norm_w")
    load(conv_w_t[:], conv_w_d[:], "conv_w")
    load(conv_b_t[:], conv_b_d[:], "conv_b")
    load(dtb_t[:], dtb_d[:], "dtb")
    load(acol_t[:], alog_d[:], "acol")
    load(dvec_t[:], dvec_d[:], "dvec")
    load(mask_t[:], mask_d[:], "mask")
    load(scr_nf[:], normf_d[:], "scr_nf", eng="scalar")
    load(xnt[0][:], scr_nf[:].broadcast_to((128, DM)), "xnt0", eng="scalar",
         reads=[gb("scr_nf")])
    P.op("vector", lambda e, S: e.tensor_copy(normf_rep[:], xnt[0][:]),
         reads=[gb("xnt0")], writes=[gb("normf_rep")])
    P.op("scalar", lambda e, S: e.activation(acol_t[:], acol_t[:], AF.Exp),
         reads=[gb("acol")], writes=[gb("acol")])
    P.op("vector", lambda e, S: e.tensor_scalar_mul(acol_t[:], acol_t[:], -1.0),
         reads=[gb("acol")], writes=[gb("acol")])
    for cb in range(CB):
        load(castf_t[:, 0:160], xproj_d[cb * 128:(cb + 1) * 128, :], castf_key)
        P.op("vector", lambda e, S, cb=cb: e.tensor_copy(xpj[:, cb * 160:(cb + 1) * 160],
                                                         castf_t[:, 0:160]),
             reads=[gb(castf_key)], writes=[gb("xpj")])
    for j in range(DIH // 256):
        load(castf_t[:, 0:256], dtw_d[:, j * 256:(j + 1) * 256], castf_key)
        P.op("vector", lambda e, S, j=j: e.tensor_copy(dtw_bf[:, j * 256:(j + 1) * 256], castf_t[:, 0:256]),
             reads=[gb(castf_key)], writes=[gb("dtw_bf")])

    def precast_wide(srcd, dst, rows, cols, skey, tiled, fold=False):
        ring = [(xt[0], "xt0"), (xt[1], "xt1"), (xnt[1], "xnt1")]
        CWW = min(DM, cols)
        it = 0
        for rb in range(rows // 128):
            for cc in range(cols // CWW):
                c0 = cc * CWW
                tile, tkey = ring[it % 3]
                it += 1
                load(tile[:, 0:CWW], srcd[rb * 128:(rb + 1) * 128, c0:c0 + CWW], tkey)
                btile, bkey = ((rms_bf[0], "rmsbf0") if it % 2 == 0 else (g_t, "g_raw"))
                if fold:
                    P.op("vector", lambda e, S, rb=rb, tile=tile, btile=btile, CWW=CWW: e.tensor_scalar(
                            btile[:, 0:CWW], tile[:, 0:CWW], norm_w_t[:, rb:rb + 1], None, ALU.mult),
                         reads=[gb(tkey), gb("norm_w")], writes=[gb(bkey)])
                else:
                    P.op("vector", lambda e, S, tile=tile, btile=btile, CWW=CWW: e.tensor_copy(
                            btile[:, 0:CWW], tile[:, 0:CWW]),
                         reads=[gb(tkey)], writes=[gb(bkey)])
                if tiled:
                    for q in range(CWW // 128):
                        co = (c0 + q * 128) // 128
                        P.op("scalar", lambda e, S, co=co, rb=rb, q=q, btile=btile, dst=dst: e.dma_start(
                                out=dst[co, :, rb * 128:(rb + 1) * 128],
                                in_=btile[:, q * 128:(q + 1) * 128]),
                             reads=[gb(bkey)], writes=[gb(skey + "scr")], dma=True)
                else:
                    P.op("scalar", lambda e, S, rb=rb, c0=c0, btile=btile, dst=dst, CWW=CWW: e.dma_start(
                            out=dst[rb * 128:(rb + 1) * 128, c0:c0 + CWW],
                            in_=btile[:, 0:CWW]),
                         reads=[gb(bkey)], writes=[gb(skey + "scr")], dma=True)

    precast_wide(w_in_d, scr_win, DM, DI + DIH, "win", tiled=True, fold=True)


    P.op("vector", lambda e, S: e.memset(cst[:, 0:1], 1e-5), writes=[gb("cst")])
    P.op("vector", lambda e, S: e.memset(cst[:, 1:2], 1.0), writes=[gb("cst")])
    P.op("vector", lambda e, S: e.memset(carry[:], 0.0), writes=[gb("carry")])
    _evx = P.op("vector", lambda e, S: e.memset(halo_t[:], 0.0), writes=[gb("halo_all")])
    for _cb in range(CB):
        gb(f"halo{_cb}").w = _evx

    def ln_rms_stats(src_ap, src_buf, out_col, is_rms):
        P.op("vector", lambda e, S: e.bn_stats(stat6[:, 0:6], src_ap[:, 0:512]),
             reads=[src_buf], writes=[gb("st6")])
        P.op("vector", lambda e, S: e.bn_stats(stat6[:, 6:12], src_ap[:, 512:1024]),
             reads=[src_buf], writes=[gb("st6")])
        P.op("vector", lambda e, S: e.bn_aggr(stat2[:], stat6[:]),
             reads=[gb("st6")], writes=[gb("st2")])
        if is_rms:
            P.op("vector", lambda e, S: e.tensor_tensor(sc1[:, 3:4], stat2[:, 0:1],
                                                        stat2[:, 0:1], ALU.mult),
                 reads=[gb("st2")], writes=[gb("m2")])
            P.op("vector", lambda e, S: e.tensor_tensor(sc1[:, 4:5], sc1[:, 3:4],
                                                        stat2[:, 1:2], ALU.add),
                 reads=[gb("m2"), gb("st2")], writes=[gb("ms")])
            P.op("scalar", lambda e, S: e.activation(sc1[:, 5:6], sc1[:, 4:5], AF.Ln, bias=cst[:, 0:1]),
                 reads=[gb("ms"), gb("cst")], writes=[gb("sd")])
        else:
            P.op("vector", lambda e, S: e.tensor_scalar_mul(
                    sc1[:, 1:2], stat2[:, 0:1], -1.0),
                 reads=[gb("st2")], writes=[gb("negm")])
            P.op("scalar", lambda e, S: e.activation(sc1[:, 5:6], stat2[:, 1:2], AF.Ln, bias=cst[:, 0:1]),
                 reads=[gb("st2"), gb("cst")], writes=[gb("sd")])
        P.op("scalar", lambda e, S, c=out_col: e.activation(sc1[:, c:c + 1], sc1[:, 5:6],
                                                            AF.Exp, scale=-0.5),
             reads=[gb("sd")], writes=[gb(f"rstd{out_col}")])


    def ln_closures(k):
        t0 = k * T
        dbk = k % 2
        a = [(0, _ln_tt(k, tt, True, t0, dbk)) for tt in range(TT)]
        return a, []

    def emit_ln(k):
        a, _ = ln_closures(k)
        for _, f in a:
            f()

    def _ln_tt(k, tt, full, t0, dbk):
      def f(tt=tt):
              row = t0 + tt * 128
              pp = tt % 2
              bx = gb(f"xt{pp}")
              load(xt[pp][:], x_d[row:row + 128, :], f"xt{pp}")
              ln_rms_stats(xt[pp][:], bx, 2, is_rms=False)
              P.op("vector", lambda e, S, pp=pp: e.tensor_scalar(
                      xnt[pp][:], xt[pp][:], sc1[:, 1:2], sc1[:, 2:3], ALU.add, ALU.mult),
                   reads=[bx, gb("negm"), gb("rstd2")], writes=[gb(f"xnt{pp}")])
              if full:
                  P.op("scalar", lambda e, S, pp=pp, row=row: e.dma_start(
                          out=scr_xn[row:row + 128, :], in_=xnt[pp][:]),
                       reads=[gb(f"xnt{pp}")], writes=[gb(f"sxn{k}_{tt}")], dma=True)
              ln_rms_stats(xnt[pp][:], gb(f"xnt{pp}"), 6, is_rms=True)
              P.op("vector", lambda e, S, pp=pp: e.tensor_scalar_mul(
                      rms_bf[0][:], xnt[pp][:], sc1[:, 6:7]),
                   reads=[gb(f"xnt{pp}"), gb("rstd6")], writes=[gb("rmsbf0")])
              for kb in range(KB):
                  P.mm("tensor", lambda e, S, pp=pp, kb=kb: e.transpose(
                          p_tr[:, kb * 128:(kb + 1) * 128],
                          rms_bf[0][:, kb * 128:(kb + 1) * 128], eye_bf[:]),
                       reads=[gb("rmsbf0"), gb("eye_bf")], psum=gb("p_tr"),
                       start=(kb == 0), stop=(kb == KB - 1))
              P.op("scalar", lambda e, S, tt=tt, dbk=dbk: e.copy(
                      rms_T[dbk][:].rearrange("p (kb t) -> p kb t", kb=KB)[:, :, tt * 128:(tt + 1) * 128],
                      p_tr[:].rearrange("p (kb c) -> p kb c", kb=KB)),
                   reads=[gb("p_tr")], writes=[gb(f"rms_T{dbk}")])
      return f

    def _ln_xnT(m, tt):
      def f(tt=tt):
          row = m * T + tt * 128
          pp = tt % 2
          load(xt[pp][:], xf_d[row:row + 128, :], f"xt{pp}", eng="gpsimd")
          ln_rms_stats(xt[pp][:], gb(f"xt{pp}"), 2, is_rms=False)
          P.op("vector", lambda e, S, pp=pp: e.tensor_scalar(
                  xnt[pp][:], xt[pp][:], sc1[:, 1:2], sc1[:, 2:3], ALU.add, ALU.mult),
               reads=[gb(f"xt{pp}"), gb("negm"), gb("rstd2")], writes=[gb(f"xnt{pp}")])
          P.op("vector", lambda e, S, pp=pp: e.tensor_copy(rms_bf[0][:], xnt[pp][:]),
               reads=[gb(f"xnt{pp}")], writes=[gb("rmsbf0")])
          for kb in range(KB):
              P.mm("tensor", lambda e, S, pp=pp, kb=kb: e.transpose(
                      p_tr[:, kb * 128:(kb + 1) * 128],
                      rms_bf[0][:, kb * 128:(kb + 1) * 128], eye_bf[:]),
                   reads=[gb("rmsbf0"), gb("eye_bf")], psum=gb("p_tr"),
                   start=(kb == 0), stop=(kb == KB - 1))
          P.op("scalar", lambda e, S, tt=tt: e.copy(
                  xn_T[:].rearrange("p (kb t) -> p kb t", kb=KB)[:, :, tt * 128:(tt + 1) * 128],
                  p_tr[:].rearrange("p (kb c) -> p kb c", kb=KB)),
               reads=[gb("p_tr")], writes=[gb("xn_T")])
      return f
    def x_closures(k):
        full = k >= NCH_PRE
        dbk = k % 2
        out = []

        def inproj_conv(cb):
            def f(cb=cb):
                wb = cb % 2
                xb = xbr[wb]
                load(wblk[wb][:], scr_win[cb, :, :], f"wblk{wb}", reads=[gb("winscr")])
                pm, pmb = (p_mm, gb("p_mm")) if cb % 2 == 0 else (p_mm2, gb("p_mm2"))
                for kb in range(KB):
                    P.mm("tensor", lambda e, S, pm=pm, wb=wb, kb=kb, dbk=dbk: e.matmul(
                            pm[:, 0:T], wblk[wb][:, kb * 128:(kb + 1) * 128],
                            rms_T[dbk][:, kb * T:(kb + 1) * T],
                            start=(kb == 0), stop=(kb == KB - 1)),
                         reads=[gb(f"wblk{wb}"), gb(f"rms_T{dbk}")], psum=pmb,
                         start=(kb == 0), stop=(kb == KB - 1))
                P.op("scalar", lambda e, S, pm=pm, xb=xb: e.copy(
                        xb[:, 4: 4 + T], pm[:, 0:T]),
                     reads=[pmb], writes=[gb(f"xbrd{wb}")])
                P.op("vector", lambda e, S, cb=cb, xb=xb: e.tensor_copy(
                        xb[:, 1:4], halo_t[:, cb * 4 + 1: cb * 4 + 4]),
                     reads=[gb(f"halo{cb}")], writes=[gb(f"xbrh{wb}")])
                for tap in range(4):
                    P.op("vector", lambda e, S, cb=cb, tap=tap, wb=wb: e.tensor_scalar(
                            diag_t[0][:, tap * 128:(tap + 1) * 128], eye_bf[:],
                            conv_w_t[:, cb * 4 + tap:cb * 4 + tap + 1], None, ALU.mult),
                         reads=[gb("eye_bf"), gb("conv_w")], writes=[gb("diag0")])
                for tap in range(4):
                    P.mm("tensor", lambda e, S, tap=tap, xb=xb: e.matmul(
                            p_conv[:, 0:T], diag_t[0][:, tap * 128:(tap + 1) * 128],
                            xb[:, 1 + tap: 1 + tap + T],
                            start=(tap == 0), stop=(tap == 3)),
                         reads=[gb("diag0"), gb(f"xbrd{wb}"), gb(f"xbrh{wb}")],
                         psum=gb("p_conv"), start=(tap == 0), stop=(tap == 3))
                if cb < CBS:
                    P.op("scalar", lambda e, S, cb=cb, dbk=dbk: e.activation(
                            u_bf[dbk][:, cb * T:(cb + 1) * T], p_conv[:, 0:T], AF.Silu,
                            bias=conv_b_t[:, cb:cb + 1]),
                         reads=[gb("p_conv"), gb("conv_b")], writes=[gb(f"u{dbk}_{cb}")])
                else:
                    P.op("scalar", lambda e, S, cb=cb: e.activation(
                            u_oth[:, (cb - CBS) * T:(cb - CBS + 1) * T], p_conv[:, 0:T],
                            AF.Silu, bias=conv_b_t[:, cb:cb + 1]),
                         reads=[gb("p_conv"), gb("conv_b")], writes=[gb(f"uo_{cb}")])
                P.op("vector", lambda e, S, cb=cb, xb=xb: e.tensor_copy(
                        halo_t[:, cb * 4 + 1: cb * 4 + 4], xb[:, T + 1: T + 4]),
                     reads=[gb(f"xbrd{wb}")], writes=[gb(f"halo{cb}")])
            return f

        def xproj():
            def f():
                def usrc(cb):
                    if cb < CBS:
                        return u_bf[dbk][:, cb * T:(cb + 1) * T], gb(f"u{dbk}_{cb}")
                    return u_oth[:, (cb - CBS) * T:(cb - CBS + 1) * T], gb(f"uo_{cb}")
                for cb in range(CB):
                    ua, ub = usrc(cb)
                    P.mm("tensor", lambda e, S, cb=cb, ua=ua: e.matmul(
                            p_mm[:, 0:T], xpj[:, cb * 160: cb * 160 + 128], ua,
                            start=(cb == 0), stop=(cb == CB - 1)),
                         reads=[gb("xpj"), ub], psum=gb("p_mm"),
                         start=(cb == 0), stop=(cb == CB - 1))
                for cb in range(CB):
                    ua, ub = usrc(cb)
                    P.mm("tensor", lambda e, S, cb=cb, ua=ua: e.matmul(
                            p_mm2[0:32, 0:T], xpj[:, cb * 160 + 128: (cb + 1) * 160], ua,
                            start=(cb == 0), stop=(cb == CB - 1)),
                         reads=[gb("xpj"), ub], psum=gb("p_mm2"),
                         start=(cb == 0), stop=(cb == CB - 1))
                P.op("scalar", lambda e, S, dbk=dbk: e.copy(dtl_bf[dbk][:], p_mm[:, 0:T]),
                     reads=[gb("p_mm")], writes=[gb(f"dtl{dbk}")])
                P.op("scalar", lambda e, S, k=k: e.activation(bc_sb[:], p_mm2[0:32, 0:T],
                                                              AF.Copy, scale=mask_t[0:32, k:k + 1]),
                     reads=[gb("p_mm2"), gb("mask")], writes=[gb("bc_sb")])
                P.op("scalar", lambda e, S, dbk=dbk: e.dma_start(out=scr_bc[dbk], in_=bc_sb[:]),
                     reads=[gb("bc_sb")], writes=[gb(f"sbc{dbk}")], dma=True)
            return f

        def bcasts():
            def f():
                bsrc = scr_bc[dbk:dbk + 1].rearrange("c a b -> c (a b)")
                P.op("sync", lambda e, S, bsrc=bsrc: e.dma_start(
                        out=brep[:],
                        in_=bsrc[:, 0:N * T].broadcast_to((128, N * T))),
                     reads=[gb(f"sbc{dbk}")],
                     writes=[gb(f"brep{nn}") for nn in range(N)], dma=True)
                P.op("sync", lambda e, S, bsrc=bsrc: e.dma_start(
                        out=crep[:],
                        in_=bsrc[:, N * T:2 * N * T].broadcast_to((128, N * T))),
                     reads=[gb(f"sbc{dbk}")],
                     writes=[gb(f"crep{nn}") for nn in range(N)], dma=True)
            return f

        for cb in range(CB):
            out.append((0, inproj_conv(cb)))
        out.append((0, xproj()))
        out.append((CBS, bcasts()))
        return out

    def ffn_closures(m):
        if m < 0 or m >= LH // T:
            return []
        items = []
        for tt in range(TT):
            items.append(_ln_xnT(m, tt))
        HF = 2 if TT >= 2 else 1
        fg, fd = _ffn_makers(m)
        for hf in range(HF):
            for hj in range(HB):
                items.append(fg(hf, hj))
            for tl in range(TT // HF):
                items.append(fd(hf, tl))
        nit = len(items)
        return [((i * (CBS - 2)) // nit, f) for i, f in enumerate(items)]

    def _ffn_makers(k):
        dbk = k % 2
        NGU = (2 if TT >= 2 else 1) * HB

        def gu_load(idx):
            hj = idx % HB
            wb = idx % 3
            qe = "sync" if idx % 2 == 0 else "scalar"
            load(wffn[wb][:], wg_d[hj, :, :], f"wffn{wb}", eng=qe)
            load(wffn2[wb][:], wu_d[hj, :, :], f"wffn2{wb}", eng=qe)

        def ffn_gu(hf, hj):
            idx = hf * HB + hj
            def f(hf=hf, hj=hj, idx=idx):
                if idx == 0:
                    gu_load(0)
                    gu_load(1)
                if idx + 2 < NGU:
                    gu_load(idx + 2)
                wb = idx % 3
                for kb in range(KB):
                    P.mm("tensor", lambda e, S, wb=wb, kb=kb, hf=hf: e.matmul(
                            p_mm[:, 0:TF], wffn[wb][:, kb * 128:(kb + 1) * 128],
                            xn_T[:, kb * T + hf * TF: kb * T + (hf + 1) * TF],
                            start=(kb == 0), stop=(kb == KB - 1)),
                         reads=[gb(f"wffn{wb}"), gb("xn_T")], psum=gb("p_mm"),
                         start=(kb == 0), stop=(kb == KB - 1))
                for kb in range(KB):
                    P.mm("tensor", lambda e, S, wb=wb, kb=kb, hf=hf: e.matmul(
                            p_mm2[:, 0:TF], wffn2[wb][:, kb * 128:(kb + 1) * 128],
                            xn_T[:, kb * T + hf * TF: kb * T + (hf + 1) * TF],
                            start=(kb == 0), stop=(kb == KB - 1)),
                         reads=[gb(f"wffn2{wb}"), gb("xn_T")], psum=gb("p_mm2"),
                         start=(kb == 0), stop=(kb == KB - 1))
                s0, s1 = 0, 1
                P.op("scalar", lambda e, S, s0=s0: e.activation(gu_s[s0][:], p_mm[:, 0:TF],
                                                                AF.Silu, scale=QINV),
                     reads=[gb("p_mm")], writes=[gb(f"gu_s{s0}")])
                P.op("scalar", lambda e, S, s1=s1: e.activation(gu_s[s1][:], p_mm2[:, 0:TF],
                                                                AF.Copy, scale=QINV),
                     reads=[gb("p_mm2")], writes=[gb(f"gu_s{s1}")])
                P.op("gpsimd", lambda e, S, hj=hj, s0=s0, s1=s1: e.tensor_tensor(
                        hffn[:, hj * TF:(hj + 1) * TF], gu_s[s0][:], gu_s[s1][:], ALU.mult),
                     reads=[gb(f"gu_s{s0}"), gb(f"gu_s{s1}")], writes=[gb(f"h{hj}")])
            return f

        def dn_load(pu, tl):
            wb = pu % 2
            qe = "sync" if pu % 2 == 0 else "scalar"
            s = wd_d[pu * 256:(pu + 1) * 256, :].rearrange("(h p) m -> p h m", p=128)
            load(wdn[wb][:].rearrange("p (h m) -> p h m", h=2), s, f"wdn{wb}", eng=qe)

        def ffn_dn(hf, tl):
            def f(hf=hf, tl=tl):
                tt = hf * (TT // (2 if TT >= 2 else 1)) + tl
                dn_load(0, tl)
                for pu in range(HB // 2):
                    wb = pu % 2
                    if pu + 1 < HB // 2:
                        dn_load(pu + 1, tl)
                    for h in range(2):
                        hj = 2 * pu + h
                        for dh in range(2):
                            P.mm("tensor", lambda e, S, hj=hj, h=h, wb=wb, dh=dh, tl=tl: e.matmul(
                                    p_o[:, dh * 512:(dh + 1) * 512],
                                    hffn[:, hj * TF + tl * 128: hj * TF + tl * 128 + 128],
                                    wdn[wb][:, h * DM + dh * 512: h * DM + (dh + 1) * 512],
                                    start=(hj == 0), stop=(hj == HB - 1)),
                                 reads=[gb(f"h{hj}"), gb(f"wdn{wb}")], psum=gb("p_o"),
                                 start=(hj == 0 and dh == 0), stop=(hj == HB - 1 and dh == 1))
                P.op("scalar", lambda e, S, tt=tt: e.activation(
                        g_t[:], p_o[:], AF.Silu, scale=QINV),
                     reads=[gb("p_o")], writes=[gb("g_raw")])
                P.op("scalar", lambda e, S, tt=tt: e.dma_start(
                        out=cc_gin[k, tt * 128:(tt + 1) * 128, :], in_=g_t[:]),
                     reads=[gb("g_raw")], writes=[gb(f"sg{k}_{tt}")], dma=True)
            return f

        return ffn_gu, ffn_dn

    def o_closures(k):
        if k < 0:
            return []
        t0 = k * T
        out = []

        def oload():
            def f():
                pw = ([gb(f"pci{r}") for r in range(4)] + [gb(f"pco{r}") for r in range(4)]
                      if k == 0 else [])
                load(yg_lo[:], cc_yout[k, 0:128, :], "yglo", eng="sync",
                     reads=[gb(f"ccyo{k}")], war=pw)
                load(yg_hi[:], cc_yout[k, 128:256, :], "yghi", eng="sync",
                     reads=[gb(f"ccyo{k}")], war=pw)
            return f

        def wo_load(pu):
            wb = pu % 2
            qe = "sync" if pu % 2 == 0 else "scalar"
            s = scr_wout[pu * 256:(pu + 1) * 256, :].rearrange("(h p) m -> p h m", p=128)
            load(woutb[wb][:].rearrange("p (h m) -> p h m", h=2), s, f"woutb{wb}", eng=qe,
                 reads=[gb("woutscr")])

        def omm(tt):
            def f(tt=tt):
                wo_load(0)
                for pu in range(CB // 2):
                    wb = pu % 2
                    if pu + 1 < CB // 2:
                        wo_load(pu + 1)
                    for h2 in range(2):
                        cb = 2 * pu + h2
                        src_t = yg_lo if cb < CBS else yg_hi
                        sk = "yglo" if cb < CBS else "yghi"
                        cbl = cb % CBS
                        for dh in range(2):
                            P.mm("tensor", lambda e, S, cb=cb, cbl=cbl, h2=h2, wb=wb, dh=dh, tt=tt, src_t=src_t: e.matmul(
                                    p_o[:, dh * 512:(dh + 1) * 512],
                                    src_t[:, cbl * T + tt * 128: cbl * T + (tt + 1) * 128],
                                    woutb[wb][:, h2 * DM + dh * 512: h2 * DM + (dh + 1) * 512],
                                    start=(cb == 0), stop=(cb == CB - 1)),
                                 reads=[gb(sk), gb(f"woutb{wb}")], psum=gb("p_o"),
                                 start=(cb == 0 and dh == 0), stop=(cb == CB - 1 and dh == 1))
            return f

        def ofin(tt):
            def f(tt=tt):
                row = t0 + tt * 128
                pp = tt % 2
                load(xt[pp][:], scr_xn[row:row + 128, :], f"xt{pp}", eng="gpsimd",
                     reads=[gb(f"sxn{k}_{tt}")])
                P.op("vector", lambda e, S, pp=pp: e.tensor_tensor(
                        xnt[pp][:], p_o[:], xt[pp][:], ALU.add),
                     reads=[gb("p_o"), gb(f"xt{pp}")], writes=[gb(f"xnt{pp}")])
                ln_rms_stats(xnt[pp][:], gb(f"xnt{pp}"), 7, is_rms=True)
                P.op("vector", lambda e, S, pp=pp: e.tensor_scalar_mul(
                        xnt[pp][:], xnt[pp][:], sc1[:, 7:8]),
                     reads=[gb(f"xnt{pp}"), gb("rstd7")], writes=[gb(f"xnt{pp}")])
                m = k if k < NCH // 2 else k - NCH // 2
                goff = (0 if k < NCH // 2 else T) + tt * 128
                load(g_t[:], cc_gout[m, goff:goff + 128, :], "g_raw", eng="gpsimd",
                     reads=[gb(f"ccgo{m}")])
                P.op("gpsimd", lambda e, S, pp=pp: e.tensor_tensor(
                        xnt[pp][:], xnt[pp][:], normf_rep[:], ALU.mult),
                     reads=[gb(f"xnt{pp}"), gb("normf_rep")], writes=[gb(f"xnt{pp}")])
                P.op("vector", lambda e, S, pp=pp: e.tensor_tensor(
                        xnt[pp][:], xnt[pp][:], g_t[:], ALU.mult),
                     reads=[gb(f"xnt{pp}"), gb("g_raw")], writes=[gb(f"xnt{pp}")])
                load(xt[pp][:], x_d[row:row + 128, :], f"xt{pp}", eng="gpsimd")
                P.op("vector", lambda e, S, pp=pp: e.tensor_tensor(
                        xnt[pp][:], xnt[pp][:], xt[pp][:], ALU.add),
                     reads=[gb(f"xnt{pp}"), gb(f"xt{pp}")], writes=[gb(f"xnt{pp}")])
                orow = k * T + tt * 128
                P.op("scalar", lambda e, S, pp=pp, orow=orow: e.dma_start(
                        out=out_d[orow:orow + 128, :], in_=xnt[pp][:]),
                     reads=[gb(f"xnt{pp}")], writes=[gb("outw")], dma=True)
            return f

        out.append((0, oload()))
        for tt in range(TT):
            out.append((tt * 2, omm(tt)))
            out.append((tt * 2 + 1, ofin(tt)))
        return out

    def emit_scan(k, pumpq, first_budget=0):
        full = True
        dbk = k % 2

        def jit_zdt(cb):
            pp = cb % 2
            P.mm("tensor", lambda e, S, cb=cb, dbk=dbk: e.matmul(
                    p_conv[:, 0:T], dtw_bf[:, cb * 128:(cb + 1) * 128], dtl_bf[dbk][:],
                    start=True, stop=True),
                 reads=[gb("dtw_bf"), gb(f"dtl{dbk}")], psum=gb("p_conv"),
                 start=True, stop=True)
            P.op("scalar", lambda e, S, cb=cb, pp=pp: e.activation(
                    sp_t[pp][:], p_conv[:, 0:T], AF.Exp, bias=dtb_t[:, cb:cb + 1]),
                 reads=[gb("p_conv"), gb("dtb")], writes=[gb(f"sp{pp}")])
            P.op("scalar", lambda e, S, pp=pp: e.activation(
                    sp_t[pp][:], sp_t[pp][:], AF.Ln, bias=cst[:, 1:2]),
                 reads=[gb(f"sp{pp}"), gb("cst")], writes=[gb(f"sp{pp}")])
            if full:
                pmz, pmzb = ((p_mm, gb("p_mm")) if pp == 0 else (p_mm2, gb("p_mm2")))
                wb = pp
                load(wblk[wb][:], scr_win[CB + cb, :, :], f"wblk{wb}", reads=[gb("winscr")])
                for kb in range(KB):
                    P.mm("tensor", lambda e, S, pmz=pmz, wb=wb, kb=kb, dbk=dbk: e.matmul(
                            pmz[:, 0:T], wblk[wb][:, kb * 128:(kb + 1) * 128],
                            rms_T[dbk][:, kb * T:(kb + 1) * T],
                            start=(kb == 0), stop=(kb == KB - 1)),
                         reads=[gb(f"wblk{wb}"), gb(f"rms_T{dbk}")], psum=pmzb,
                         start=(kb == 0), stop=(kb == KB - 1))
                P.op("scalar", lambda e, S, pmz=pmz, pp=pp: e.activation(
                        z_bf[pp][:], pmz[:, 0:T], AF.Silu),
                     reads=[pmzb], writes=[gb(f"z{pp}")])

        nq = len(pumpq)
        budget_per = (nq + CBS - 1) // CBS + 1

        def pump(cur_cb, budget):
            while pumpq and budget > 0 and pumpq[0][0] <= cur_cb:
                pumpq.pop(0)[1]()
                budget -= 1

        jit_zdt(0)
        for cb in range(CBS):
            pp = cb % 2
            if cb + 1 < CBS:
                jit_zdt(cb + 1)
            pump(cb, budget_per + (first_budget if cb == 0 else 0))
            if DEBUG_SCR:
                P.op("scalar", lambda e, S, cb=cb, k=k, dbk=dbk: e.dma_start(
                        out=scr_du[k, :, cb * T:(cb + 1) * T],
                        in_=u_bf[dbk][:, cb * T:(cb + 1) * T]),
                     reads=[gb(f"u{dbk}_{cb}")], writes=[gb("dbgu")], dma=True)
            P.op("vector", lambda e, S, cb=cb, pp=pp, dbk=dbk: e.tensor_tensor(
                    dtu_t[pp][:], sp_t[pp][:], u_bf[dbk][:, cb * T:(cb + 1) * T], ALU.mult),
                 reads=[gb(f"sp{pp}"), gb(f"u{dbk}_{cb}")], writes=[gb(f"dtu{pp}")])
            for g in range(NGRP):
                gp = (cb * NGRP + g) % 3
                yhi = (cb * NGRP + g) % 3
                P.op("scalar", lambda e, S, cb=cb, gp=gp, nn=g, pp=pp: e.activation(
                        a_t[gp][:], sp_t[pp][:],
                        AF.Exp, scale=acol_t[:, cb * N + nn: cb * N + nn + 1]),
                     reads=[gb(f"sp{pp}"), gb("acol")], writes=[gb(f"a{gp}")])
                bp = g % 2
                P.op("vector", lambda e, S, pp=pp, bp=bp, nn=g: e.tensor_tensor(
                        b_t[bp][:], dtu_t[pp][:],
                        brep[:, nn * T:(nn + 1) * T], ALU.mult),
                     reads=[gb(f"dtu{pp}"), gb(f"brep{g}")], writes=[gb(f"b{bp}")])
                P.op("vector", lambda e, S, cb=cb, gp=gp, bp=bp, nn=g: e.tensor_tensor_scan(
                        a_t[gp][:], a_t[gp][:], b_t[bp][:],
                        carry[:, cb * N + nn: cb * N + nn + 1], ALU.mult, ALU.add),
                     reads=[gb(f"a{gp}"), gb(f"b{bp}"), gb("carry")],
                     writes=[gb(f"a{gp}")])
                P.op("scalar", lambda e, S, cb=cb, gp=gp, nn=g: e.copy(
                        carry[:, cb * N + nn: cb * N + nn + 1], a_t[gp][:, T - 1:T]),
                     reads=[gb(f"a{gp}")], writes=[gb("carry")])
                if full:
                    P.op("gpsimd", lambda e, S, gp=gp, nn=g, yhi=yhi: e.tensor_tensor(
                            yh_t[yhi][:], a_t[gp][:],
                            crep[:, nn * T:(nn + 1) * T], ALU.mult),
                         reads=[gb(f"a{gp}"), gb(f"crep{g}")], writes=[gb(f"yh{yhi}")])
                    P.mm("tensor", lambda e, S, cb=cb, yhi=yhi, g=g: e.matmul(
                            p_ys[cb % 2][:, 0:T], eye_bf[:], yh_t[yhi][:],
                            start=(g == 0), stop=(g == NGRP - 1)),
                         reads=[gb(f"yh{yhi}")], psum=gb(f"p_ys{cb % 2}"),
                         start=(g == 0), stop=(g == NGRP - 1))
            if full:
                P.op("vector", lambda e, S, cb=cb, pp=pp, dbk=dbk: e.scalar_tensor_tensor(
                        dtu_t[pp][:], u_bf[dbk][:, cb * T:(cb + 1) * T], dvec_t[:, cb:cb + 1],
                        p_ys[cb % 2][:, 0:T], ALU.mult, ALU.add),
                     reads=[gb(f"u{dbk}_{cb}"), gb("dvec"), gb(f"p_ys{cb % 2}")],
                     writes=[gb(f"dtu{pp}")])
                P.op("gpsimd", lambda e, S, cb=cb, pp=pp: e.tensor_tensor(
                        yg_bf[:, cb * T:(cb + 1) * T], dtu_t[pp][:], z_bf[pp][:], ALU.mult),
                     reads=[gb(f"dtu{pp}"), gb(f"z{pp}")], writes=[gb(f"yg{cb}")])
                if DEBUG_SCR:
                    P.op("scalar", lambda e, S, cb=cb, k=k: e.dma_start(
                            out=scr_dyg[k, :, cb * T:(cb + 1) * T],
                            in_=yg_bf[:, cb * T:(cb + 1) * T]),
                         reads=[gb(f"yg{cb}")], writes=[gb("dbgyg")], dma=True)
        pump(CBS + CB, 10 ** 9)
        P.op("scalar", lambda e, S, k=k: e.dma_start(out=cc_yin[k], in_=yg_bf[:]),
             reads=[gb(f"yg{cb}") for cb in range(CBS)], writes=[gb(f"ccyi{k}")], dma=True)
        P.op("gpsimd", lambda e, S, k=k: e.collective_compute(
                "AllGather", ALU.bypass, ins=[cc_yin[k]], outs=[cc_yout[k]],
                replica_groups=RG),
             reads=[gb(f"ccyi{k}")], writes=[gb(f"ccyo{k}")], sem_override="cC")

    precast_tasks = []
    def precast_to_tasks(fn_emit_iters):
        for c in fn_emit_iters:
            precast_tasks.append((0, c))

    def gen_precast_big(srcd, dst, rows, cols, skey, i32):
        icols = (CBS * T) // 2
        CW = min(512, icols // 2)
        RING = icols // CW
        ivw = yg_lo[:].bitcast(I32 if i32 else F32)
        tiled3 = len(srcd.shape) == 3
        if tiled3:
            blocks = [(t, cc) for t in range(srcd.shape[0])
                      for cc in range(srcd.shape[2] // CW)]
        else:
            blocks = [(rb, cc) for rb in range(rows // 128) for cc in range(cols // CW)]
        nb = len(blocks)
        LA = min(3, RING - 1)

        def emit_load(i):
            rb, cc = blocks[i]
            r = i % RING
            qe = "sync" if i % 2 == 0 else "scalar"
            s = (srcd[rb, :, cc * CW:cc * CW + CW] if tiled3 else
                 srcd[rb * 128:(rb + 1) * 128, cc * CW:cc * CW + CW])
            load(ivw[:, r * CW:(r + 1) * CW], s, f"pci{r}", eng=qe)

        for bi in range(nb):
            def one(bi=bi):
                if bi == 0:
                    for jj in range(min(LA, nb)):
                        emit_load(jj)
                elif bi + LA - 1 < nb:
                    emit_load(bi + LA - 1)
                rb, cc = blocks[bi]
                c0 = cc * CW
                r = bi % RING
                P.op("vector", lambda e, S, r=r: e.tensor_copy(
                        yg_hi[:, r * CW:(r + 1) * CW], ivw[:, r * CW:(r + 1) * CW]),
                     reads=[gb(f"pci{r}")], writes=[gb(f"pco{r}")])
                qe = "scalar" if bi % 2 == 0 else "sync"
                dd = (dst[rb, :, c0:c0 + CW] if tiled3 else
                      dst[rb * 128:(rb + 1) * 128, c0:c0 + CW])
                P.op(qe, lambda e, S, dd=dd, r=r: e.dma_start(
                        out=dd, in_=yg_hi[:, r * CW:(r + 1) * CW]),
                     reads=[gb(f"pco{r}")], writes=[gb(skey + "scr")], dma=True)
            yield one

    precast_to_tasks(gen_precast_big(wout_d, scr_wout, DI, DM, "wout", False))

    emit_ln(0)
    fq0 = [f for _, f in ffn_closures(0)]
    n_head = TT + (2 if TT >= 2 else 1) * HB
    fq0_head, fq0_tail = fq0[:n_head], fq0[n_head:]
    for _, f in x_closures(0):
        f()
        for _ in range(3):
            if fq0_head:
                fq0_head.pop(0)()
    for f in fq0_head:
        f()
    for k in range(NCH):
        q = []
        q += o_closures(k - 2)
        if k == 1:
            nfq = len(fq0_tail)
            q += [((i * (CBS - 2)) // max(nfq, 1), f) for i, f in enumerate(fq0_tail)]
        else:
            q += ffn_closures(k - 1)
        lnA, _ = ln_closures(k + 1) if k + 1 < NCH else ([], [])
        q += lnA
        if k + 1 < NCH:
            q += x_closures(k + 1)
        if k == 0:
            q += precast_tasks
            del precast_tasks[:]
        emit_scan(k, q, first_budget=2)
        m = k - 1
        if 0 <= m < LH // T:
            P.op("gpsimd", lambda e, S, m=m: e.collective_compute(
                    "AllGather", ALU.bypass, ins=[cc_gin[m]], outs=[cc_gout[m]],
                    replica_groups=RG),
                 reads=[gb(f"sg{m}_{tt}") for tt in range(TT)], writes=[gb(f"ccgo{m}")],
                 sem_override="cC")
    for _, f in ffn_closures(NCH - 1):
        f()
    m = NCH - 1
    if m < LH // T:
        P.op("gpsimd", lambda e, S, m=m: e.collective_compute(
                "AllGather", ALU.bypass, ins=[cc_gin[m]], outs=[cc_gout[m]],
                replica_groups=RG),
             reads=[gb(f"sg{m}_{tt}") for tt in range(TT)], writes=[gb(f"ccgo{m}")],
             sem_override="cC")
    for _, f in o_closures(NCH - 2):
        f()
    for _, f in o_closures(NCH - 1):
        f()

    P.raw_wait("gpsimd", gb("outw").w)
    P.flush_queues("gpsimd")

    sems = {}
    sem_ctx = ExitStack()
    for s in ALL_SEMS:
        sems[s] = sem_ctx.enter_context(nc.semaphore(s))
    P.finalize_into(nc, sems)
    ctx.close()
    sem_ctx.close()
    return nc, dict(L=L, T=T, NCH=NCH, NCH_PRE=NCH_PRE, L_OWN=L_OWN, ninstr=P.ninstr)


def make_core_inputs(core_id, inputs, L=4096, NCH=8, NCH_PRE=0):
    b, j = core_id // 2, core_id % 2
    DM, DI, N = 1024, 2048, 16
    DIH = DI // 2
    CB, CBS = 16, 8
    T = L // NCH
    LH = L // 2
    x = np.asarray(inputs["L0_attn_x"][b], dtype=np.float32)
    own = slice(j * DIH, (j + 1) * DIH)
    oth = slice((1 - j) * DIH, (2 - j) * DIH)
    perm = np.r_[np.arange(j * DIH, (j + 1) * DIH), np.arange((1 - j) * DIH, (2 - j) * DIH)]
    ipw = np.asarray(inputs["L0_attn_in_proj_w"], np.float32)
    w_in = np.concatenate([ipw[perm, :], ipw[2048 + j * DIH: 2048 + (j + 1) * DIH, :]], axis=0).T
    cw = np.asarray(inputs["L0_attn_conv_w"], np.float32)[perm]
    cb_ = np.asarray(inputs["L0_attn_conv_b"], np.float32)[perm]
    xpw = np.asarray(inputs["L0_attn_x_proj_w"], np.float32)[:, perm]
    dtw = np.asarray(inputs["L0_attn_dt_proj_w"], np.float32)[own]
    dtb = np.asarray(inputs["L0_attn_dt_proj_b"], np.float32)[own]
    alog = np.asarray(inputs["L0_attn_A_log"], np.float32)[own]
    dvec = np.asarray(inputs["L0_attn_D"], np.float32)[own]
    d = {
        "x": np.ascontiguousarray(x),
        "xf": np.ascontiguousarray(x[j * LH:(j + 1) * LH]),
        "mask": np.ones((128, NCH), np.float32),
        "w_in": np.ascontiguousarray(w_in),
        "norm_w": np.ascontiguousarray(np.asarray(inputs["L0_attn_norm_w"], np.float32).reshape(8, 128).T),
        "conv_w_r": np.ascontiguousarray(cw.reshape(CB, 128, 4).transpose(1, 0, 2).reshape(128, CB * 4)),
        "conv_b_r": np.ascontiguousarray(cb_.reshape(CB, 128).T),
        "xproj_T": np.ascontiguousarray(xpw.T),
        "dtw_T": np.ascontiguousarray(dtw.T),
        "dtb_r": np.ascontiguousarray(dtb.reshape(CBS, 128).T),
        "alog_r": np.ascontiguousarray(alog.reshape(CBS, 128, N).transpose(1, 0, 2).reshape(128, CBS * N)),
        "d_r": np.ascontiguousarray(dvec.reshape(CBS, 128).T),
        "wout_T": np.ascontiguousarray(np.asarray(inputs["L0_attn_out_proj_w"], np.float32).T),
        "norm_f": np.ascontiguousarray(np.asarray(inputs["L0_attn_norm_f_w"], np.float32).reshape(1, DM)),
        "wg_T": _tile_ffn(np.asarray(inputs["L0_ffn_w_gate"], np.int32).T.astype(_BF)),
        "wu_T": _tile_ffn(np.asarray(inputs["L0_ffn_w_up"], np.int32).T.astype(_BF)),
        "wd_T": np.ascontiguousarray(np.asarray(inputs["L0_ffn_w_down"], np.int32).T.astype(_BF)),
        "eye": np.eye(128, dtype=np.float32),
    }
    return d


def _tile_ffn(wT):
    DM, HID = wT.shape
    return np.ascontiguousarray(
        wT.reshape(DM // 128, 128, HID // 128, 128).transpose(2, 1, 0, 3)
          .reshape(HID // 128, 128, DM))


def assemble_output(results, L=4096, NCH=8, NCH_PRE=0, B=4):
    DM = 1024
    LH = L // 2
    out = np.empty((B, L, DM), np.float32)
    for core_id in range(2 * B):
        b, j = core_id // 2, core_id % 2
        o = results[core_id]["out"]
        out[b, j * LH:(j + 1) * LH] = o[j * LH:(j + 1) * LH]
    return out


from concourse.bass_utils import run_bass_kernel_spmd

_BUILT = {}

def _get_program():
    if "nc" not in _BUILT:
        nc, meta = build_program(L=4096, T=512)
        _BUILT["nc"] = nc
        _BUILT["meta"] = meta
    return _BUILT["nc"], _BUILT["meta"]


def kernel(**inputs):
    nc, meta = _get_program()
    in_maps = [make_core_inputs(c, inputs, L=meta["L"], NCH=meta["NCH"],
                                NCH_PRE=meta["NCH_PRE"]) for c in range(8)]
    res = run_bass_kernel_spmd(nc, in_maps, core_ids=list(range(8)))
    return assemble_output(list(res.results), L=meta["L"], NCH=meta["NCH"],
                           NCH_PRE=meta["NCH_PRE"], B=4)


# version 30
# speedup vs baseline: 1.0948x; 1.0004x over previous
import numpy as np
import ml_dtypes
_BF = ml_dtypes.bfloat16
from contextlib import ExitStack
import concourse.bass as bass
import concourse.mybir as mybir

F32 = mybir.dt.float32
BF16 = mybir.dt.bfloat16
I32 = mybir.dt.int32
AF = mybir.ActivationFunctionType
ALU = mybir.AluOpType

ENGINES = ("sync", "scalar", "vector", "tensor", "gpsimd")
CSEM = {"scalar": "cS", "vector": "cV", "tensor": "cT", "gpsimd": "cG"}
DQ = {"sync": tuple(f"dS{i}" for i in range(8)),
      "scalar": tuple(f"dA{i}" for i in range(8)),
      "gpsimd": tuple(f"dG{i}" for i in range(4))}
ALL_SEMS = ("cS", "cV", "cT", "cG", "cC") + DQ["sync"] + DQ["scalar"] + DQ["gpsimd"]
BATCH_MAX = 2


class Ev:
    __slots__ = ("sem", "val")
    def __init__(self, sem, val):
        self.sem = sem
        self.val = val


class DmaEv:
    __slots__ = ("q", "batch")
    def __init__(self, q, batch):
        self.q = q
        self.batch = batch


class PendEv:
    __slots__ = ("eng", "ev")
    def __init__(self, eng):
        self.eng = eng
        self.ev = None


class Buf:
    __slots__ = ("w", "reads")
    def __init__(self):
        self.w = None
        self.reads = []


class Prog:
    def __init__(self):
        self.ops = {e: [] for e in ENGINES}
        self.ctr = {s: 0 for s in ALL_SEMS}
        self.waited = {e: {s: 0 for s in ALL_SEMS} for e in ENGINES}
        self.ninstr = 0
        self.q = {e: dict(nopen=0, nbatch=0, open=None) for e in DQ}
        self.pending = {e: [] for e in ENGINES}

    def _close_batch(self, qeng, which=None):
        st = self.q[qeng]
        if st["open"] is not None:
            batch = st["open"]
            batch[0] = True
            batch[2] = self.ctr[batch[1]]
            st["open"] = None
            st["nopen"] = 0

    def _resolve_pending(self, eng2):
        sem = CSEM[eng2]
        self.ctr[sem] += 1
        val = self.ctr[sem]
        self.ops[eng2].append(lambda e, S, _s=sem: e.nop().then_inc(S[_s], 1))
        self.ninstr += 1
        ev = Ev(sem, val)
        for p in self.pending[eng2]:
            p.ev = ev
        self.pending[eng2] = []
        return ev

    def _wait(self, eng, ev, raw=True):
        if ev is None:
            return
        if isinstance(ev, PendEv):
            if ev.ev is None:
                self._resolve_pending(ev.eng)
            ev = ev.ev
        if isinstance(ev, DmaEv):
            if not ev.batch[0]:
                self._close_batch(ev.q)
            ev = Ev(ev.batch[1], ev.batch[2])
        if self.waited[eng].get(ev.sem, 0) >= ev.val:
            return
        self.waited[eng][ev.sem] = ev.val
        sem_name, val = ev.sem, ev.val
        self.ops[eng].append(lambda e, S, _s=sem_name, _v=val: e.wait_ge(S[_s], _v))
        self.ninstr += 1

    def op(self, eng, fn, reads=(), writes=(), war=(), dma=False, track=True,
           sem_override=None):
        for b in reads:
            self._wait(eng, b.w, raw=True)
        for b in war:
            self._wait(eng, b.w, raw=False)
            for r in b.reads:
                self._wait(eng, r, raw=False)
        for b in writes:
            self._wait(eng, b.w, raw=False)
            for r in b.reads:
                self._wait(eng, r, raw=False)
        if dma:
            st = self.q[eng]
            if st["open"] is None or st["nopen"] >= BATCH_MAX:
                self._close_batch(eng)
                pool = DQ[eng]
                psem = pool[st["nbatch"] % len(pool)]
                st["nbatch"] += 1
                self._wait(eng, Ev(psem, self.ctr[psem]))
                st["open"] = [False, psem, 0]
            batch = st["open"]
            psem = batch[1]
            self.ctr[psem] += 16
            st["nopen"] += 1
            def run(e, S, fn=fn, psem=psem):
                fn(e, S).then_inc(S[psem], 16)
            ev = DmaEv(eng, batch)
            self.ops[eng].append(run)
            self.ninstr += 1
            for b in writes:
                b.w = ev
                b.reads = []
            for b in reads:
                b.reads.append(ev)
            return ev
        if track:
            sem = sem_override or CSEM[eng]
            self.ctr[sem] += 1
            val = self.ctr[sem]
            def run(e, S, fn=fn, sem=sem):
                fn(e, S).then_inc(S[sem], 1)
            ev = Ev(sem, val)
            for p in self.pending[eng]:
                p.ev = ev
            self.pending[eng] = []
        else:
            def run(e, S, fn=fn):
                fn(e, S)
            ev = None
        self.ops[eng].append(run)
        self.ninstr += 1
        if track:
            for b in writes:
                b.w = ev
                b.reads = []
            for b in reads:
                b.reads.append(ev)
        else:
            pe = PendEv(eng)
            self.pending[eng].append(pe)
            for b in reads:
                b.reads.append(pe)
        return ev

    def mm(self, eng, fn, reads=(), psum=None, start=False, stop=False):
        if start and stop:
            return self.op(eng, fn, reads=reads, writes=[psum])
        if start:
            return self.op(eng, fn, reads=reads, war=[psum])
        if stop:
            return self.op(eng, fn, reads=reads, writes=[psum])
        return self.op(eng, fn, reads=reads)

    def raw_wait(self, eng, ev):
        self._wait(eng, ev)

    def flush_queues(self, eng):
        for qeng in DQ:
            self._close_batch(qeng)
            for psem in DQ[qeng]:
                self._wait(eng, Ev(psem, self.ctr[psem]))

    def finalize_into(self, nc, sems):
        with nc.Block() as block:
            for eng in ENGINES:
                ops = self.ops[eng]
                if not ops:
                    continue
                getattr(block, eng)(_mkbody(ops, sems))


def _mkbody(ops, sems):
    def body(e):
        for fn in ops:
            fn(e, sems)
    return body


DEBUG_SCR = False

def build_program(L=4096, T=512, NCH_PRE=None):
    DM, DI, N, RANK, HID = 1024, 2048, 16, 128, 3072
    NCH = L // T
    NCH_PRE = 0
    L_OWN = L
    DIH = DI // 2
    CBS = DIH // 128
    LH = L // 2
    RG = [[0, 1], [2, 3], [4, 5], [6, 7]]
    KB = DM // 128
    CB = DI // 128
    HB = HID // 128
    TT = T // 128
    NG = 1
    NGRP = N // NG
    TF = T // 2 if T >= 256 else T
    EPS = 1e-5
    QINV = float(np.float32(1.0 / 127.0))
    NCO = (DI + DIH) // 128

    nc = bass.Bass("TRN2", target_bir_lowering=False, debug=False)

    def din(name, shape, dt):
        return nc.dram_tensor(name, list(shape), dt, kind="ExternalInput").ap()

    x_d = din("x", (L, DM), F32)
    mask_d = din("mask", (128, NCH), F32)
    w_in_d = din("w_in", (DM, DI + DIH), F32)
    norm_w_d = din("norm_w", (128, KB), F32)
    conv_w_d = din("conv_w_r", (128, CB * 4), F32)
    conv_b_d = din("conv_b_r", (128, CB), F32)
    xproj_d = din("xproj_T", (DI, 160), F32)
    dtw_d = din("dtw_T", (RANK, DIH), F32)
    dtb_d = din("dtb_r", (128, CBS), F32)
    alog_d = din("alog_r", (128, CBS * N), F32)
    dvec_d = din("d_r", (128, CBS), F32)
    wout_d = din("wout_T", (DI, DM), F32)
    normf_d = din("norm_f", (1, DM), F32)
    wg_d = din("wg_T", (HB, 128, KB * 128), BF16)
    wu_d = din("wu_T", (HB, 128, KB * 128), BF16)
    wd_d = din("wd_T", (HID, DM), BF16)
    eye_d = din("eye", (128, 128), F32)
    xf_d = din("xf", (LH, DM), F32)

    out_d = nc.dram_tensor("out", [L_OWN, DM], F32, kind="ExternalOutput").ap()

    def scr(name, shape, dt):
        return nc.dram_tensor(name, list(shape), dt, kind="Internal").ap()

    scr_win = scr("scr_win", (NCO, 128, KB * 128), BF16)
    scr_wout = scr("scr_wout", (DI, DM), BF16)
    scr_bc = scr("scr_bc", (2, 32, T), BF16)
    scr_xn = scr("scr_xn", (L, DM), F32)
    scr_nf = scr("scr_nf", (1, DM), F32)
    cc_yin = scr("cc_yin", (NCH, 128, CBS * T), BF16)
    cc_yout = scr("cc_yout", (NCH, 256, CBS * T), BF16)
    cc_gin = scr("cc_gin", (NCH // 2, T, DM), BF16)
    cc_gout = scr("cc_gout", (NCH // 2, 2 * T, DM), BF16)
    scr_du = scr("scr_du", (NCH, 128, CB * T), BF16) if DEBUG_SCR else None
    scr_dyg = scr("scr_dyg", (NCH, 128, CB * T), BF16) if DEBUG_SCR else None

    ctx = ExitStack()
    sb = lambda n, s, d: ctx.enter_context(nc.sbuf_tensor(n, list(s), d))
    psx = lambda n, s, d: ctx.enter_context(nc.psum_tensor(n, list(s), d))

    eye_bf = sb("eye_bf", (128, 128), BF16)
    norm_w_t = sb("norm_w_t", (128, KB), F32)
    conv_w_t = sb("conv_w_t", (128, CB * 4), F32)
    conv_b_t = sb("conv_b_t", (128, CB), F32)
    dtb_t = sb("dtb_t", (128, CBS), F32)
    acol_t = sb("acol_t", (128, CBS * N), F32)
    dvec_t = sb("dvec_t", (128, CBS), F32)
    mask_t = sb("mask_t", (128, NCH), F32)
    normf_rep = sb("normf_rep", (128, DM), BF16)
    carry = sb("carry", (128, CBS * N), F32)
    xpj = sb("xpj", (128, CB * 160), BF16)
    dtw_bf = sb("dtw_bf", (128, DIH), BF16)
    stat6 = sb("stat6", (128, 12), F32)
    stat2 = sb("stat2", (128, 2), F32)
    sc1 = sb("sc1", (128, 8), F32)
    cst = sb("cst", (128, 2), F32)
    xt = [sb(f"xt{i}", (128, DM), F32) for i in range(2)]
    xnt = [sb(f"xnt{i}", (128, DM), F32) for i in range(2)]
    rms_bf = [sb(f"rmsbf{i}", (128, DM), BF16) for i in range(1)]
    rms_T = [sb(f"rms_T{i}", (128, KB * T), BF16) for i in range(2)]
    xn_T = sb("xn_T", (128, KB * T), BF16)
    xbr = [sb(f"xbr{i}", (128, T + 4), BF16) for i in range(2)]
    halo_t = sb("halo_t", (128, CB * 4), BF16)
    u_bf = [sb(f"u_bf{i}", (128, CBS * T), BF16) for i in range(2)]
    u_oth = sb("u_oth", (128, (CB - CBS) * T), BF16)
    z_bf = [sb(f"zbf{i}", (128, T), BF16) for i in range(2)]
    dtl_bf = [sb(f"dtl{i}", (128, T), BF16) for i in range(2)]
    sp_t = [sb(f"sp{i}", (128, T), F32) for i in range(2)]
    dtu_t = [sb(f"dtu{i}", (128, T), BF16) for i in range(2)]
    a_t = [sb(f"a{i}", (128, NG * T), F32) for i in range(3)]
    b_t = [sb(f"b{i}", (128, NG * T), F32) for i in range(2)]
    yh_t = [sb(f"yh{i}", (128, NG * T), BF16) for i in range(3)]
    brep = sb("brep", (128, N * T), BF16)
    crep = sb("crep", (128, N * T), BF16)
    bc_sb = sb("bc_sb", (32, T), BF16)
    yg_bf = sb("yg_bf", (128, CBS * T), BF16)
    yg_lo = sb("yg_lo", (128, CBS * T), BF16)
    yg_hi = sb("yg_hi", (128, CBS * T), BF16)
    diag_t = [sb("diag0", (128, 4 * 128), BF16)]
    wblk = [sb(f"wblk{i}", (128, KB * 128), BF16) for i in range(2)]
    wffn = [sb(f"wffn{i}", (128, KB * 128), BF16) for i in range(3)]
    wffn2 = [sb(f"wffn2{i}", (128, KB * 128), BF16) for i in range(3)]
    wdn = [sb(f"wdn{i}", (128, 2 * DM), BF16) for i in range(2)]
    woutb = [sb(f"woutb{i}", (128, 2 * DM), BF16) for i in range(2)]
    gu_s = [sb(f"gu_s{i}", (128, TF), BF16) for i in range(2)]
    hffn = sb("hffn", (128, HB * TF), BF16)
    g_t = sb("g_t", (128, DM), BF16)
    castf_t = sp_t[0] if T >= 256 else sb("cast_fs", (128, 256), F32)
    castf_key = "sp0" if T >= 256 else "cast_fs"

    p_mm = psx("p_mm", (128, 512), F32)
    p_mm2 = psx("p_mm2", (128, 512), F32)
    p_conv = psx("p_conv", (128, 512), F32)
    p_ys = [psx(f"p_ys{i}", (128, 512), F32) for i in range(2)]
    p_tr = psx("p_tr", (128, 1024), BF16)
    p_o = psx("p_o", (128, 1024), F32)

    bufs = {}
    def gb(key):
        if key not in bufs:
            bufs[key] = Buf()
        return bufs[key]

    P = Prog()

    def load(dst_ap, src_ap, key=None, eng="sync", reads=(), writes=(), war=()):
        w = list(writes)
        if key is not None:
            w.append(gb(key))
        return P.op(eng, lambda e, S, d=dst_ap, s=src_ap: e.dma_start(out=d, in_=s),
                    reads=list(reads), writes=w, war=list(war), dma=True)

    load(xt[0][:, 0:128], eye_d[:], "xt0")
    P.op("vector", lambda e, S: e.tensor_copy(eye_bf[:], xt[0][:, 0:128]),
         reads=[gb("xt0")], writes=[gb("eye_bf")])
    load(norm_w_t[:], norm_w_d[:], "norm_w")
    load(conv_w_t[:], conv_w_d[:], "conv_w")
    load(conv_b_t[:], conv_b_d[:], "conv_b")
    load(dtb_t[:], dtb_d[:], "dtb")
    load(acol_t[:], alog_d[:], "acol")
    load(dvec_t[:], dvec_d[:], "dvec")
    load(mask_t[:], mask_d[:], "mask")
    load(scr_nf[:], normf_d[:], "scr_nf", eng="scalar")
    load(xnt[0][:], scr_nf[:].broadcast_to((128, DM)), "xnt0", eng="scalar",
         reads=[gb("scr_nf")])
    P.op("vector", lambda e, S: e.tensor_copy(normf_rep[:], xnt[0][:]),
         reads=[gb("xnt0")], writes=[gb("normf_rep")])
    P.op("scalar", lambda e, S: e.activation(acol_t[:], acol_t[:], AF.Exp),
         reads=[gb("acol")], writes=[gb("acol")])
    P.op("vector", lambda e, S: e.tensor_scalar_mul(acol_t[:], acol_t[:], -1.0),
         reads=[gb("acol")], writes=[gb("acol")])
    for cb in range(CB):
        load(castf_t[:, 0:160], xproj_d[cb * 128:(cb + 1) * 128, :], castf_key)
        P.op("vector", lambda e, S, cb=cb: e.tensor_copy(xpj[:, cb * 160:(cb + 1) * 160],
                                                         castf_t[:, 0:160]),
             reads=[gb(castf_key)], writes=[gb("xpj")])
    for j in range(DIH // 256):
        load(castf_t[:, 0:256], dtw_d[:, j * 256:(j + 1) * 256], castf_key)
        P.op("vector", lambda e, S, j=j: e.tensor_copy(dtw_bf[:, j * 256:(j + 1) * 256], castf_t[:, 0:256]),
             reads=[gb(castf_key)], writes=[gb("dtw_bf")])

    def precast_wide(srcd, dst, rows, cols, skey, tiled, fold=False):
        ring = [(xt[0], "xt0"), (xt[1], "xt1"), (xnt[1], "xnt1")]
        CWW = min(DM, cols)
        it = 0
        for rb in range(rows // 128):
            for cc in range(cols // CWW):
                c0 = cc * CWW
                tile, tkey = ring[it % 3]
                it += 1
                load(tile[:, 0:CWW], srcd[rb * 128:(rb + 1) * 128, c0:c0 + CWW], tkey)
                btile, bkey = ((rms_bf[0], "rmsbf0") if it % 2 == 0 else (g_t, "g_raw"))
                if fold:
                    P.op("vector", lambda e, S, rb=rb, tile=tile, btile=btile, CWW=CWW: e.tensor_scalar(
                            btile[:, 0:CWW], tile[:, 0:CWW], norm_w_t[:, rb:rb + 1], None, ALU.mult),
                         reads=[gb(tkey), gb("norm_w")], writes=[gb(bkey)])
                else:
                    P.op("vector", lambda e, S, tile=tile, btile=btile, CWW=CWW: e.tensor_copy(
                            btile[:, 0:CWW], tile[:, 0:CWW]),
                         reads=[gb(tkey)], writes=[gb(bkey)])
                if tiled:
                    for q in range(CWW // 128):
                        co = (c0 + q * 128) // 128
                        P.op("scalar", lambda e, S, co=co, rb=rb, q=q, btile=btile, dst=dst: e.dma_start(
                                out=dst[co, :, rb * 128:(rb + 1) * 128],
                                in_=btile[:, q * 128:(q + 1) * 128]),
                             reads=[gb(bkey)], writes=[gb(skey + "scr")], dma=True)
                else:
                    P.op("scalar", lambda e, S, rb=rb, c0=c0, btile=btile, dst=dst, CWW=CWW: e.dma_start(
                            out=dst[rb * 128:(rb + 1) * 128, c0:c0 + CWW],
                            in_=btile[:, 0:CWW]),
                         reads=[gb(bkey)], writes=[gb(skey + "scr")], dma=True)

    precast_wide(w_in_d, scr_win, DM, DI + DIH, "win", tiled=True, fold=True)


    P.op("vector", lambda e, S: e.memset(cst[:, 0:1], 1e-5), writes=[gb("cst")])
    P.op("vector", lambda e, S: e.memset(cst[:, 1:2], 1.0), writes=[gb("cst")])
    P.op("vector", lambda e, S: e.memset(carry[:], 0.0), writes=[gb("carry")])
    _evx = P.op("vector", lambda e, S: e.memset(halo_t[:], 0.0), writes=[gb("halo_all")])
    for _cb in range(CB):
        gb(f"halo{_cb}").w = _evx

    def ln_rms_stats(src_ap, src_buf, out_col, is_rms):
        P.op("vector", lambda e, S: e.bn_stats(stat6[:, 0:6], src_ap[:, 0:512]),
             reads=[src_buf], writes=[gb("st6")])
        P.op("vector", lambda e, S: e.bn_stats(stat6[:, 6:12], src_ap[:, 512:1024]),
             reads=[src_buf], writes=[gb("st6")])
        P.op("vector", lambda e, S: e.bn_aggr(stat2[:], stat6[:]),
             reads=[gb("st6")], writes=[gb("st2")])
        if is_rms:
            P.op("vector", lambda e, S: e.tensor_tensor(sc1[:, 3:4], stat2[:, 0:1],
                                                        stat2[:, 0:1], ALU.mult),
                 reads=[gb("st2")], writes=[gb("m2")])
            P.op("vector", lambda e, S: e.tensor_tensor(sc1[:, 4:5], sc1[:, 3:4],
                                                        stat2[:, 1:2], ALU.add),
                 reads=[gb("m2"), gb("st2")], writes=[gb("ms")])
            P.op("scalar", lambda e, S: e.activation(sc1[:, 5:6], sc1[:, 4:5], AF.Ln, bias=cst[:, 0:1]),
                 reads=[gb("ms"), gb("cst")], writes=[gb("sd")])
        else:
            P.op("vector", lambda e, S: e.tensor_scalar_mul(
                    sc1[:, 1:2], stat2[:, 0:1], -1.0),
                 reads=[gb("st2")], writes=[gb("negm")])
            P.op("scalar", lambda e, S: e.activation(sc1[:, 5:6], stat2[:, 1:2], AF.Ln, bias=cst[:, 0:1]),
                 reads=[gb("st2"), gb("cst")], writes=[gb("sd")])
        P.op("scalar", lambda e, S, c=out_col: e.activation(sc1[:, c:c + 1], sc1[:, 5:6],
                                                            AF.Exp, scale=-0.5),
             reads=[gb("sd")], writes=[gb(f"rstd{out_col}")])


    def ln_closures(k):
        t0 = k * T
        dbk = k % 2
        a = [(0, _ln_tt(k, tt, True, t0, dbk)) for tt in range(TT)]
        return a, []

    def emit_ln(k):
        a, _ = ln_closures(k)
        for _, f in a:
            f()

    def _ln_tt(k, tt, full, t0, dbk):
      def f(tt=tt):
              row = t0 + tt * 128
              pp = tt % 2
              bx = gb(f"xt{pp}")
              load(xt[pp][:], x_d[row:row + 128, :], f"xt{pp}")
              ln_rms_stats(xt[pp][:], bx, 2, is_rms=False)
              P.op("vector", lambda e, S, pp=pp: e.tensor_scalar(
                      xnt[pp][:], xt[pp][:], sc1[:, 1:2], sc1[:, 2:3], ALU.add, ALU.mult),
                   reads=[bx, gb("negm"), gb("rstd2")], writes=[gb(f"xnt{pp}")])
              if full:
                  P.op("scalar", lambda e, S, pp=pp, row=row: e.dma_start(
                          out=scr_xn[row:row + 128, :], in_=xnt[pp][:]),
                       reads=[gb(f"xnt{pp}")], writes=[gb(f"sxn{k}_{tt}")], dma=True)
              ln_rms_stats(xnt[pp][:], gb(f"xnt{pp}"), 6, is_rms=True)
              P.op("scalar", lambda e, S, pp=pp: e.activation(
                      rms_bf[0][:], xnt[pp][:], AF.Copy, scale=sc1[:, 6:7]),
                   reads=[gb(f"xnt{pp}"), gb("rstd6")], writes=[gb("rmsbf0")])
              for kb in range(KB):
                  P.mm("tensor", lambda e, S, pp=pp, kb=kb: e.transpose(
                          p_tr[:, kb * 128:(kb + 1) * 128],
                          rms_bf[0][:, kb * 128:(kb + 1) * 128], eye_bf[:]),
                       reads=[gb("rmsbf0"), gb("eye_bf")], psum=gb("p_tr"),
                       start=(kb == 0), stop=(kb == KB - 1))
              P.op("scalar", lambda e, S, tt=tt, dbk=dbk: e.copy(
                      rms_T[dbk][:].rearrange("p (kb t) -> p kb t", kb=KB)[:, :, tt * 128:(tt + 1) * 128],
                      p_tr[:].rearrange("p (kb c) -> p kb c", kb=KB)),
                   reads=[gb("p_tr")], writes=[gb(f"rms_T{dbk}")])
      return f

    def _ln_xnT(m, tt):
      def f(tt=tt):
          row = m * T + tt * 128
          pp = tt % 2
          load(xt[pp][:], xf_d[row:row + 128, :], f"xt{pp}", eng="gpsimd")
          ln_rms_stats(xt[pp][:], gb(f"xt{pp}"), 2, is_rms=False)
          P.op("vector", lambda e, S, pp=pp: e.tensor_scalar(
                  xnt[pp][:], xt[pp][:], sc1[:, 1:2], sc1[:, 2:3], ALU.add, ALU.mult),
               reads=[gb(f"xt{pp}"), gb("negm"), gb("rstd2")], writes=[gb(f"xnt{pp}")])
          P.op("vector", lambda e, S, pp=pp: e.tensor_copy(rms_bf[0][:], xnt[pp][:]),
               reads=[gb(f"xnt{pp}")], writes=[gb("rmsbf0")])
          for kb in range(KB):
              P.mm("tensor", lambda e, S, pp=pp, kb=kb: e.transpose(
                      p_tr[:, kb * 128:(kb + 1) * 128],
                      rms_bf[0][:, kb * 128:(kb + 1) * 128], eye_bf[:]),
                   reads=[gb("rmsbf0"), gb("eye_bf")], psum=gb("p_tr"),
                   start=(kb == 0), stop=(kb == KB - 1))
          P.op("scalar", lambda e, S, tt=tt: e.copy(
                  xn_T[:].rearrange("p (kb t) -> p kb t", kb=KB)[:, :, tt * 128:(tt + 1) * 128],
                  p_tr[:].rearrange("p (kb c) -> p kb c", kb=KB)),
               reads=[gb("p_tr")], writes=[gb("xn_T")])
      return f
    def x_closures(k):
        full = k >= NCH_PRE
        dbk = k % 2
        out = []

        def inproj_conv(cb):
            def f(cb=cb):
                wb = cb % 2
                xb = xbr[wb]
                load(wblk[wb][:], scr_win[cb, :, :], f"wblk{wb}", reads=[gb("winscr")])
                pm, pmb = (p_mm, gb("p_mm")) if cb % 2 == 0 else (p_mm2, gb("p_mm2"))
                for kb in range(KB):
                    P.mm("tensor", lambda e, S, pm=pm, wb=wb, kb=kb, dbk=dbk: e.matmul(
                            pm[:, 0:T], wblk[wb][:, kb * 128:(kb + 1) * 128],
                            rms_T[dbk][:, kb * T:(kb + 1) * T],
                            start=(kb == 0), stop=(kb == KB - 1)),
                         reads=[gb(f"wblk{wb}"), gb(f"rms_T{dbk}")], psum=pmb,
                         start=(kb == 0), stop=(kb == KB - 1))
                P.op("scalar", lambda e, S, pm=pm, xb=xb: e.copy(
                        xb[:, 4: 4 + T], pm[:, 0:T]),
                     reads=[pmb], writes=[gb(f"xbrd{wb}")])
                P.op("vector", lambda e, S, cb=cb, xb=xb: e.tensor_copy(
                        xb[:, 1:4], halo_t[:, cb * 4 + 1: cb * 4 + 4]),
                     reads=[gb(f"halo{cb}")], writes=[gb(f"xbrh{wb}")])
                for tap in range(4):
                    P.op("vector", lambda e, S, cb=cb, tap=tap, wb=wb: e.tensor_scalar(
                            diag_t[0][:, tap * 128:(tap + 1) * 128], eye_bf[:],
                            conv_w_t[:, cb * 4 + tap:cb * 4 + tap + 1], None, ALU.mult),
                         reads=[gb("eye_bf"), gb("conv_w")], writes=[gb("diag0")])
                for tap in range(4):
                    P.mm("tensor", lambda e, S, tap=tap, xb=xb: e.matmul(
                            p_conv[:, 0:T], diag_t[0][:, tap * 128:(tap + 1) * 128],
                            xb[:, 1 + tap: 1 + tap + T],
                            start=(tap == 0), stop=(tap == 3)),
                         reads=[gb("diag0"), gb(f"xbrd{wb}"), gb(f"xbrh{wb}")],
                         psum=gb("p_conv"), start=(tap == 0), stop=(tap == 3))
                if cb < CBS:
                    P.op("scalar", lambda e, S, cb=cb, dbk=dbk: e.activation(
                            u_bf[dbk][:, cb * T:(cb + 1) * T], p_conv[:, 0:T], AF.Silu,
                            bias=conv_b_t[:, cb:cb + 1]),
                         reads=[gb("p_conv"), gb("conv_b")], writes=[gb(f"u{dbk}_{cb}")])
                else:
                    P.op("scalar", lambda e, S, cb=cb: e.activation(
                            u_oth[:, (cb - CBS) * T:(cb - CBS + 1) * T], p_conv[:, 0:T],
                            AF.Silu, bias=conv_b_t[:, cb:cb + 1]),
                         reads=[gb("p_conv"), gb("conv_b")], writes=[gb(f"uo_{cb}")])
                P.op("vector", lambda e, S, cb=cb, xb=xb: e.tensor_copy(
                        halo_t[:, cb * 4 + 1: cb * 4 + 4], xb[:, T + 1: T + 4]),
                     reads=[gb(f"xbrd{wb}")], writes=[gb(f"halo{cb}")])
            return f

        def xproj():
            def f():
                def usrc(cb):
                    if cb < CBS:
                        return u_bf[dbk][:, cb * T:(cb + 1) * T], gb(f"u{dbk}_{cb}")
                    return u_oth[:, (cb - CBS) * T:(cb - CBS + 1) * T], gb(f"uo_{cb}")
                for cb in range(CB):
                    ua, ub = usrc(cb)
                    P.mm("tensor", lambda e, S, cb=cb, ua=ua: e.matmul(
                            p_mm[:, 0:T], xpj[:, cb * 160: cb * 160 + 128], ua,
                            start=(cb == 0), stop=(cb == CB - 1)),
                         reads=[gb("xpj"), ub], psum=gb("p_mm"),
                         start=(cb == 0), stop=(cb == CB - 1))
                for cb in range(CB):
                    ua, ub = usrc(cb)
                    P.mm("tensor", lambda e, S, cb=cb, ua=ua: e.matmul(
                            p_mm2[0:32, 0:T], xpj[:, cb * 160 + 128: (cb + 1) * 160], ua,
                            start=(cb == 0), stop=(cb == CB - 1)),
                         reads=[gb("xpj"), ub], psum=gb("p_mm2"),
                         start=(cb == 0), stop=(cb == CB - 1))
                P.op("scalar", lambda e, S, dbk=dbk: e.copy(dtl_bf[dbk][:], p_mm[:, 0:T]),
                     reads=[gb("p_mm")], writes=[gb(f"dtl{dbk}")])
                P.op("scalar", lambda e, S, k=k: e.activation(bc_sb[:], p_mm2[0:32, 0:T],
                                                              AF.Copy, scale=mask_t[0:32, k:k + 1]),
                     reads=[gb("p_mm2"), gb("mask")], writes=[gb("bc_sb")])
                P.op("scalar", lambda e, S, dbk=dbk: e.dma_start(out=scr_bc[dbk], in_=bc_sb[:]),
                     reads=[gb("bc_sb")], writes=[gb(f"sbc{dbk}")], dma=True)
            return f

        def bcasts():
            def f():
                bsrc = scr_bc[dbk:dbk + 1].rearrange("c a b -> c (a b)")
                P.op("sync", lambda e, S, bsrc=bsrc: e.dma_start(
                        out=brep[:],
                        in_=bsrc[:, 0:N * T].broadcast_to((128, N * T))),
                     reads=[gb(f"sbc{dbk}")],
                     writes=[gb(f"brep{nn}") for nn in range(N)], dma=True)
                P.op("sync", lambda e, S, bsrc=bsrc: e.dma_start(
                        out=crep[:],
                        in_=bsrc[:, N * T:2 * N * T].broadcast_to((128, N * T))),
                     reads=[gb(f"sbc{dbk}")],
                     writes=[gb(f"crep{nn}") for nn in range(N)], dma=True)
            return f

        for cb in range(CB):
            out.append((0, inproj_conv(cb)))
        out.append((0, xproj()))
        out.append((CBS, bcasts()))
        return out

    def ffn_closures(m):
        if m < 0 or m >= LH // T:
            return []
        items = []
        for tt in range(TT):
            items.append(_ln_xnT(m, tt))
        HF = 2 if TT >= 2 else 1
        fg, fd = _ffn_makers(m)
        for hf in range(HF):
            for hj in range(HB):
                items.append(fg(hf, hj))
            for tl in range(TT // HF):
                items.append(fd(hf, tl))
        nit = len(items)
        return [((i * (CBS - 2)) // nit, f) for i, f in enumerate(items)]

    def _ffn_makers(k):
        dbk = k % 2
        NGU = (2 if TT >= 2 else 1) * HB

        def gu_load(idx):
            hj = idx % HB
            wb = idx % 3
            qe = "sync" if idx % 2 == 0 else "scalar"
            load(wffn[wb][:], wg_d[hj, :, :], f"wffn{wb}", eng=qe)
            load(wffn2[wb][:], wu_d[hj, :, :], f"wffn2{wb}", eng=qe)

        def ffn_gu(hf, hj):
            idx = hf * HB + hj
            def f(hf=hf, hj=hj, idx=idx):
                if idx == 0:
                    gu_load(0)
                    gu_load(1)
                if idx + 2 < NGU:
                    gu_load(idx + 2)
                wb = idx % 3
                for kb in range(KB):
                    P.mm("tensor", lambda e, S, wb=wb, kb=kb, hf=hf: e.matmul(
                            p_mm[:, 0:TF], wffn[wb][:, kb * 128:(kb + 1) * 128],
                            xn_T[:, kb * T + hf * TF: kb * T + (hf + 1) * TF],
                            start=(kb == 0), stop=(kb == KB - 1)),
                         reads=[gb(f"wffn{wb}"), gb("xn_T")], psum=gb("p_mm"),
                         start=(kb == 0), stop=(kb == KB - 1))
                for kb in range(KB):
                    P.mm("tensor", lambda e, S, wb=wb, kb=kb, hf=hf: e.matmul(
                            p_mm2[:, 0:TF], wffn2[wb][:, kb * 128:(kb + 1) * 128],
                            xn_T[:, kb * T + hf * TF: kb * T + (hf + 1) * TF],
                            start=(kb == 0), stop=(kb == KB - 1)),
                         reads=[gb(f"wffn2{wb}"), gb("xn_T")], psum=gb("p_mm2"),
                         start=(kb == 0), stop=(kb == KB - 1))
                s0, s1 = 0, 1
                P.op("scalar", lambda e, S, s0=s0: e.activation(gu_s[s0][:], p_mm[:, 0:TF],
                                                                AF.Silu, scale=QINV),
                     reads=[gb("p_mm")], writes=[gb(f"gu_s{s0}")])
                P.op("scalar", lambda e, S, s1=s1: e.activation(gu_s[s1][:], p_mm2[:, 0:TF],
                                                                AF.Copy, scale=QINV),
                     reads=[gb("p_mm2")], writes=[gb(f"gu_s{s1}")])
                P.op("gpsimd", lambda e, S, hj=hj, s0=s0, s1=s1: e.tensor_tensor(
                        hffn[:, hj * TF:(hj + 1) * TF], gu_s[s0][:], gu_s[s1][:], ALU.mult),
                     reads=[gb(f"gu_s{s0}"), gb(f"gu_s{s1}")], writes=[gb(f"h{hj}")])
            return f

        def dn_load(pu, tl):
            wb = pu % 2
            qe = "sync" if pu % 2 == 0 else "scalar"
            s = wd_d[pu * 256:(pu + 1) * 256, :].rearrange("(h p) m -> p h m", p=128)
            load(wdn[wb][:].rearrange("p (h m) -> p h m", h=2), s, f"wdn{wb}", eng=qe)

        def ffn_dn(hf, tl):
            def f(hf=hf, tl=tl):
                tt = hf * (TT // (2 if TT >= 2 else 1)) + tl
                dn_load(0, tl)
                for pu in range(HB // 2):
                    wb = pu % 2
                    if pu + 1 < HB // 2:
                        dn_load(pu + 1, tl)
                    for h in range(2):
                        hj = 2 * pu + h
                        for dh in range(2):
                            P.mm("tensor", lambda e, S, hj=hj, h=h, wb=wb, dh=dh, tl=tl: e.matmul(
                                    p_o[:, dh * 512:(dh + 1) * 512],
                                    hffn[:, hj * TF + tl * 128: hj * TF + tl * 128 + 128],
                                    wdn[wb][:, h * DM + dh * 512: h * DM + (dh + 1) * 512],
                                    start=(hj == 0), stop=(hj == HB - 1)),
                                 reads=[gb(f"h{hj}"), gb(f"wdn{wb}")], psum=gb("p_o"),
                                 start=(hj == 0 and dh == 0), stop=(hj == HB - 1 and dh == 1))
                P.op("scalar", lambda e, S, tt=tt: e.activation(
                        g_t[:], p_o[:], AF.Silu, scale=QINV),
                     reads=[gb("p_o")], writes=[gb("g_raw")])
                P.op("scalar", lambda e, S, tt=tt: e.dma_start(
                        out=cc_gin[k, tt * 128:(tt + 1) * 128, :], in_=g_t[:]),
                     reads=[gb("g_raw")], writes=[gb(f"sg{k}_{tt}")], dma=True)
            return f

        return ffn_gu, ffn_dn

    def o_closures(k):
        if k < 0:
            return []
        t0 = k * T
        out = []

        def oload():
            def f():
                pw = ([gb(f"pci{r}") for r in range(4)] + [gb(f"pco{r}") for r in range(4)]
                      if k == 0 else [])
                load(yg_lo[:], cc_yout[k, 0:128, :], "yglo", eng="sync",
                     reads=[gb(f"ccyo{k}")], war=pw)
                load(yg_hi[:], cc_yout[k, 128:256, :], "yghi", eng="sync",
                     reads=[gb(f"ccyo{k}")], war=pw)
            return f

        def wo_load(pu):
            wb = pu % 2
            qe = "sync" if pu % 2 == 0 else "scalar"
            s = scr_wout[pu * 256:(pu + 1) * 256, :].rearrange("(h p) m -> p h m", p=128)
            load(woutb[wb][:].rearrange("p (h m) -> p h m", h=2), s, f"woutb{wb}", eng=qe,
                 reads=[gb("woutscr")])

        def omm(tt):
            def f(tt=tt):
                wo_load(0)
                for pu in range(CB // 2):
                    wb = pu % 2
                    if pu + 1 < CB // 2:
                        wo_load(pu + 1)
                    for h2 in range(2):
                        cb = 2 * pu + h2
                        src_t = yg_lo if cb < CBS else yg_hi
                        sk = "yglo" if cb < CBS else "yghi"
                        cbl = cb % CBS
                        for dh in range(2):
                            P.mm("tensor", lambda e, S, cb=cb, cbl=cbl, h2=h2, wb=wb, dh=dh, tt=tt, src_t=src_t: e.matmul(
                                    p_o[:, dh * 512:(dh + 1) * 512],
                                    src_t[:, cbl * T + tt * 128: cbl * T + (tt + 1) * 128],
                                    woutb[wb][:, h2 * DM + dh * 512: h2 * DM + (dh + 1) * 512],
                                    start=(cb == 0), stop=(cb == CB - 1)),
                                 reads=[gb(sk), gb(f"woutb{wb}")], psum=gb("p_o"),
                                 start=(cb == 0 and dh == 0), stop=(cb == CB - 1 and dh == 1))
            return f

        def ofin(tt):
            def f(tt=tt):
                row = t0 + tt * 128
                pp = tt % 2
                load(xt[pp][:], scr_xn[row:row + 128, :], f"xt{pp}", eng="gpsimd",
                     reads=[gb(f"sxn{k}_{tt}")])
                P.op("vector", lambda e, S, pp=pp: e.tensor_tensor(
                        xnt[pp][:], p_o[:], xt[pp][:], ALU.add),
                     reads=[gb("p_o"), gb(f"xt{pp}")], writes=[gb(f"xnt{pp}")])
                ln_rms_stats(xnt[pp][:], gb(f"xnt{pp}"), 7, is_rms=True)
                P.op("scalar", lambda e, S, pp=pp: e.activation(
                        xnt[pp][:], xnt[pp][:], AF.Copy, scale=sc1[:, 7:8]),
                     reads=[gb(f"xnt{pp}"), gb("rstd7")], writes=[gb(f"xnt{pp}")])
                m = k if k < NCH // 2 else k - NCH // 2
                goff = (0 if k < NCH // 2 else T) + tt * 128
                load(g_t[:], cc_gout[m, goff:goff + 128, :], "g_raw", eng="gpsimd",
                     reads=[gb(f"ccgo{m}")])
                P.op("gpsimd", lambda e, S, pp=pp: e.tensor_tensor(
                        xnt[pp][:], xnt[pp][:], normf_rep[:], ALU.mult),
                     reads=[gb(f"xnt{pp}"), gb("normf_rep")], writes=[gb(f"xnt{pp}")])
                P.op("vector", lambda e, S, pp=pp: e.tensor_tensor(
                        xnt[pp][:], xnt[pp][:], g_t[:], ALU.mult),
                     reads=[gb(f"xnt{pp}"), gb("g_raw")], writes=[gb(f"xnt{pp}")])
                load(xt[pp][:], x_d[row:row + 128, :], f"xt{pp}", eng="gpsimd")
                P.op("vector", lambda e, S, pp=pp: e.tensor_tensor(
                        xnt[pp][:], xnt[pp][:], xt[pp][:], ALU.add),
                     reads=[gb(f"xnt{pp}"), gb(f"xt{pp}")], writes=[gb(f"xnt{pp}")])
                orow = k * T + tt * 128
                P.op("scalar", lambda e, S, pp=pp, orow=orow: e.dma_start(
                        out=out_d[orow:orow + 128, :], in_=xnt[pp][:]),
                     reads=[gb(f"xnt{pp}")], writes=[gb("outw")], dma=True)
            return f

        out.append((0, oload()))
        for tt in range(TT):
            out.append((tt * 2, omm(tt)))
            out.append((tt * 2 + 1, ofin(tt)))
        return out

    def emit_scan(k, pumpq, first_budget=0):
        full = True
        dbk = k % 2

        def jit_zdt(cb):
            pp = cb % 2
            P.mm("tensor", lambda e, S, cb=cb, dbk=dbk: e.matmul(
                    p_conv[:, 0:T], dtw_bf[:, cb * 128:(cb + 1) * 128], dtl_bf[dbk][:],
                    start=True, stop=True),
                 reads=[gb("dtw_bf"), gb(f"dtl{dbk}")], psum=gb("p_conv"),
                 start=True, stop=True)
            P.op("scalar", lambda e, S, cb=cb, pp=pp: e.activation(
                    sp_t[pp][:], p_conv[:, 0:T], AF.Exp, bias=dtb_t[:, cb:cb + 1]),
                 reads=[gb("p_conv"), gb("dtb")], writes=[gb(f"sp{pp}")])
            P.op("scalar", lambda e, S, pp=pp: e.activation(
                    sp_t[pp][:], sp_t[pp][:], AF.Ln, bias=cst[:, 1:2]),
                 reads=[gb(f"sp{pp}"), gb("cst")], writes=[gb(f"sp{pp}")])
            if full:
                pmz, pmzb = ((p_mm, gb("p_mm")) if pp == 0 else (p_mm2, gb("p_mm2")))
                wb = pp
                load(wblk[wb][:], scr_win[CB + cb, :, :], f"wblk{wb}", reads=[gb("winscr")])
                for kb in range(KB):
                    P.mm("tensor", lambda e, S, pmz=pmz, wb=wb, kb=kb, dbk=dbk: e.matmul(
                            pmz[:, 0:T], wblk[wb][:, kb * 128:(kb + 1) * 128],
                            rms_T[dbk][:, kb * T:(kb + 1) * T],
                            start=(kb == 0), stop=(kb == KB - 1)),
                         reads=[gb(f"wblk{wb}"), gb(f"rms_T{dbk}")], psum=pmzb,
                         start=(kb == 0), stop=(kb == KB - 1))
                P.op("scalar", lambda e, S, pmz=pmz, pp=pp: e.activation(
                        z_bf[pp][:], pmz[:, 0:T], AF.Silu),
                     reads=[pmzb], writes=[gb(f"z{pp}")])

        nq = len(pumpq)
        budget_per = (nq + CBS - 1) // CBS + 1

        def pump(cur_cb, budget):
            while pumpq and budget > 0 and pumpq[0][0] <= cur_cb:
                pumpq.pop(0)[1]()
                budget -= 1

        jit_zdt(0)
        for cb in range(CBS):
            pp = cb % 2
            if cb + 1 < CBS:
                jit_zdt(cb + 1)
            pump(cb, budget_per + (first_budget if cb == 0 else 0))
            if DEBUG_SCR:
                P.op("scalar", lambda e, S, cb=cb, k=k, dbk=dbk: e.dma_start(
                        out=scr_du[k, :, cb * T:(cb + 1) * T],
                        in_=u_bf[dbk][:, cb * T:(cb + 1) * T]),
                     reads=[gb(f"u{dbk}_{cb}")], writes=[gb("dbgu")], dma=True)
            P.op("vector", lambda e, S, cb=cb, pp=pp, dbk=dbk: e.tensor_tensor(
                    dtu_t[pp][:], sp_t[pp][:], u_bf[dbk][:, cb * T:(cb + 1) * T], ALU.mult),
                 reads=[gb(f"sp{pp}"), gb(f"u{dbk}_{cb}")], writes=[gb(f"dtu{pp}")])
            for g in range(NGRP):
                gp = (cb * NGRP + g) % 3
                yhi = (cb * NGRP + g) % 3
                P.op("scalar", lambda e, S, cb=cb, gp=gp, nn=g, pp=pp: e.activation(
                        a_t[gp][:], sp_t[pp][:],
                        AF.Exp, scale=acol_t[:, cb * N + nn: cb * N + nn + 1]),
                     reads=[gb(f"sp{pp}"), gb("acol")], writes=[gb(f"a{gp}")])
                bp = g % 2
                P.op("vector", lambda e, S, pp=pp, bp=bp, nn=g: e.tensor_tensor(
                        b_t[bp][:], dtu_t[pp][:],
                        brep[:, nn * T:(nn + 1) * T], ALU.mult),
                     reads=[gb(f"dtu{pp}"), gb(f"brep{g}")], writes=[gb(f"b{bp}")])
                P.op("vector", lambda e, S, cb=cb, gp=gp, bp=bp, nn=g: e.tensor_tensor_scan(
                        a_t[gp][:], a_t[gp][:], b_t[bp][:],
                        carry[:, cb * N + nn: cb * N + nn + 1], ALU.mult, ALU.add),
                     reads=[gb(f"a{gp}"), gb(f"b{bp}"), gb("carry")],
                     writes=[gb(f"a{gp}")])
                P.op("scalar", lambda e, S, cb=cb, gp=gp, nn=g: e.copy(
                        carry[:, cb * N + nn: cb * N + nn + 1], a_t[gp][:, T - 1:T]),
                     reads=[gb(f"a{gp}")], writes=[gb("carry")])
                if full:
                    P.op("gpsimd", lambda e, S, gp=gp, nn=g, yhi=yhi: e.tensor_tensor(
                            yh_t[yhi][:], a_t[gp][:],
                            crep[:, nn * T:(nn + 1) * T], ALU.mult),
                         reads=[gb(f"a{gp}"), gb(f"crep{g}")], writes=[gb(f"yh{yhi}")])
                    P.mm("tensor", lambda e, S, cb=cb, yhi=yhi, g=g: e.matmul(
                            p_ys[cb % 2][:, 0:T], eye_bf[:], yh_t[yhi][:],
                            start=(g == 0), stop=(g == NGRP - 1)),
                         reads=[gb(f"yh{yhi}")], psum=gb(f"p_ys{cb % 2}"),
                         start=(g == 0), stop=(g == NGRP - 1))
            if full:
                P.op("vector", lambda e, S, cb=cb, pp=pp, dbk=dbk: e.scalar_tensor_tensor(
                        dtu_t[pp][:], u_bf[dbk][:, cb * T:(cb + 1) * T], dvec_t[:, cb:cb + 1],
                        p_ys[cb % 2][:, 0:T], ALU.mult, ALU.add),
                     reads=[gb(f"u{dbk}_{cb}"), gb("dvec"), gb(f"p_ys{cb % 2}")],
                     writes=[gb(f"dtu{pp}")])
                P.op("gpsimd", lambda e, S, cb=cb, pp=pp: e.tensor_tensor(
                        yg_bf[:, cb * T:(cb + 1) * T], dtu_t[pp][:], z_bf[pp][:], ALU.mult),
                     reads=[gb(f"dtu{pp}"), gb(f"z{pp}")], writes=[gb(f"yg{cb}")])
                if DEBUG_SCR:
                    P.op("scalar", lambda e, S, cb=cb, k=k: e.dma_start(
                            out=scr_dyg[k, :, cb * T:(cb + 1) * T],
                            in_=yg_bf[:, cb * T:(cb + 1) * T]),
                         reads=[gb(f"yg{cb}")], writes=[gb("dbgyg")], dma=True)
        pump(CBS + CB, 10 ** 9)
        P.op("scalar", lambda e, S, k=k: e.dma_start(out=cc_yin[k], in_=yg_bf[:]),
             reads=[gb(f"yg{cb}") for cb in range(CBS)], writes=[gb(f"ccyi{k}")], dma=True)
        P.op("gpsimd", lambda e, S, k=k: e.collective_compute(
                "AllGather", ALU.bypass, ins=[cc_yin[k]], outs=[cc_yout[k]],
                replica_groups=RG),
             reads=[gb(f"ccyi{k}")], writes=[gb(f"ccyo{k}")], sem_override="cC")

    precast_tasks = []
    def precast_to_tasks(fn_emit_iters):
        for c in fn_emit_iters:
            precast_tasks.append((0, c))

    def gen_precast_big(srcd, dst, rows, cols, skey, i32):
        icols = (CBS * T) // 2
        CW = min(512, icols // 2)
        RING = icols // CW
        ivw = yg_lo[:].bitcast(I32 if i32 else F32)
        tiled3 = len(srcd.shape) == 3
        if tiled3:
            blocks = [(t, cc) for t in range(srcd.shape[0])
                      for cc in range(srcd.shape[2] // CW)]
        else:
            blocks = [(rb, cc) for rb in range(rows // 128) for cc in range(cols // CW)]
        nb = len(blocks)
        LA = min(3, RING - 1)

        def emit_load(i):
            rb, cc = blocks[i]
            r = i % RING
            qe = "sync" if i % 2 == 0 else "scalar"
            s = (srcd[rb, :, cc * CW:cc * CW + CW] if tiled3 else
                 srcd[rb * 128:(rb + 1) * 128, cc * CW:cc * CW + CW])
            load(ivw[:, r * CW:(r + 1) * CW], s, f"pci{r}", eng=qe)

        for bi in range(nb):
            def one(bi=bi):
                if bi == 0:
                    for jj in range(min(LA, nb)):
                        emit_load(jj)
                elif bi + LA - 1 < nb:
                    emit_load(bi + LA - 1)
                rb, cc = blocks[bi]
                c0 = cc * CW
                r = bi % RING
                P.op("vector", lambda e, S, r=r: e.tensor_copy(
                        yg_hi[:, r * CW:(r + 1) * CW], ivw[:, r * CW:(r + 1) * CW]),
                     reads=[gb(f"pci{r}")], writes=[gb(f"pco{r}")])
                qe = "scalar" if bi % 2 == 0 else "sync"
                dd = (dst[rb, :, c0:c0 + CW] if tiled3 else
                      dst[rb * 128:(rb + 1) * 128, c0:c0 + CW])
                P.op(qe, lambda e, S, dd=dd, r=r: e.dma_start(
                        out=dd, in_=yg_hi[:, r * CW:(r + 1) * CW]),
                     reads=[gb(f"pco{r}")], writes=[gb(skey + "scr")], dma=True)
            yield one

    precast_to_tasks(gen_precast_big(wout_d, scr_wout, DI, DM, "wout", False))

    emit_ln(0)
    fq0 = [f for _, f in ffn_closures(0)]
    n_head = TT + (2 if TT >= 2 else 1) * HB
    fq0_head, fq0_tail = fq0[:n_head], fq0[n_head:]
    for _, f in x_closures(0):
        f()
        for _ in range(3):
            if fq0_head:
                fq0_head.pop(0)()
    for f in fq0_head:
        f()
    for k in range(NCH):
        q = []
        q += o_closures(k - 2)
        if k == 1:
            nfq = len(fq0_tail)
            q += [((i * (CBS - 2)) // max(nfq, 1), f) for i, f in enumerate(fq0_tail)]
        else:
            q += ffn_closures(k - 1)
        lnA, _ = ln_closures(k + 1) if k + 1 < NCH else ([], [])
        q += lnA
        if k + 1 < NCH:
            q += x_closures(k + 1)
        if k == 0:
            q += precast_tasks
            del precast_tasks[:]
        emit_scan(k, q, first_budget=2)
        m = k - 1
        if 0 <= m < LH // T:
            P.op("gpsimd", lambda e, S, m=m: e.collective_compute(
                    "AllGather", ALU.bypass, ins=[cc_gin[m]], outs=[cc_gout[m]],
                    replica_groups=RG),
                 reads=[gb(f"sg{m}_{tt}") for tt in range(TT)], writes=[gb(f"ccgo{m}")],
                 sem_override="cC")
    for _, f in ffn_closures(NCH - 1):
        f()
    m = NCH - 1
    if m < LH // T:
        P.op("gpsimd", lambda e, S, m=m: e.collective_compute(
                "AllGather", ALU.bypass, ins=[cc_gin[m]], outs=[cc_gout[m]],
                replica_groups=RG),
             reads=[gb(f"sg{m}_{tt}") for tt in range(TT)], writes=[gb(f"ccgo{m}")],
             sem_override="cC")
    for _, f in o_closures(NCH - 2):
        f()
    for _, f in o_closures(NCH - 1):
        f()

    P.raw_wait("gpsimd", gb("outw").w)
    P.flush_queues("gpsimd")

    sems = {}
    sem_ctx = ExitStack()
    for s in ALL_SEMS:
        sems[s] = sem_ctx.enter_context(nc.semaphore(s))
    P.finalize_into(nc, sems)
    ctx.close()
    sem_ctx.close()
    return nc, dict(L=L, T=T, NCH=NCH, NCH_PRE=NCH_PRE, L_OWN=L_OWN, ninstr=P.ninstr)


def make_core_inputs(core_id, inputs, L=4096, NCH=8, NCH_PRE=0):
    b, j = core_id // 2, core_id % 2
    DM, DI, N = 1024, 2048, 16
    DIH = DI // 2
    CB, CBS = 16, 8
    T = L // NCH
    LH = L // 2
    x = np.asarray(inputs["L0_attn_x"][b], dtype=np.float32)
    own = slice(j * DIH, (j + 1) * DIH)
    oth = slice((1 - j) * DIH, (2 - j) * DIH)
    perm = np.r_[np.arange(j * DIH, (j + 1) * DIH), np.arange((1 - j) * DIH, (2 - j) * DIH)]
    ipw = np.asarray(inputs["L0_attn_in_proj_w"], np.float32)
    w_in = np.concatenate([ipw[perm, :], ipw[2048 + j * DIH: 2048 + (j + 1) * DIH, :]], axis=0).T
    cw = np.asarray(inputs["L0_attn_conv_w"], np.float32)[perm]
    cb_ = np.asarray(inputs["L0_attn_conv_b"], np.float32)[perm]
    xpw = np.asarray(inputs["L0_attn_x_proj_w"], np.float32)[:, perm]
    dtw = np.asarray(inputs["L0_attn_dt_proj_w"], np.float32)[own]
    dtb = np.asarray(inputs["L0_attn_dt_proj_b"], np.float32)[own]
    alog = np.asarray(inputs["L0_attn_A_log"], np.float32)[own]
    dvec = np.asarray(inputs["L0_attn_D"], np.float32)[own]
    d = {
        "x": np.ascontiguousarray(x),
        "xf": np.ascontiguousarray(x[j * LH:(j + 1) * LH]),
        "mask": np.ones((128, NCH), np.float32),
        "w_in": np.ascontiguousarray(w_in),
        "norm_w": np.ascontiguousarray(np.asarray(inputs["L0_attn_norm_w"], np.float32).reshape(8, 128).T),
        "conv_w_r": np.ascontiguousarray(cw.reshape(CB, 128, 4).transpose(1, 0, 2).reshape(128, CB * 4)),
        "conv_b_r": np.ascontiguousarray(cb_.reshape(CB, 128).T),
        "xproj_T": np.ascontiguousarray(xpw.T),
        "dtw_T": np.ascontiguousarray(dtw.T),
        "dtb_r": np.ascontiguousarray(dtb.reshape(CBS, 128).T),
        "alog_r": np.ascontiguousarray(alog.reshape(CBS, 128, N).transpose(1, 0, 2).reshape(128, CBS * N)),
        "d_r": np.ascontiguousarray(dvec.reshape(CBS, 128).T),
        "wout_T": np.ascontiguousarray(np.asarray(inputs["L0_attn_out_proj_w"], np.float32).T),
        "norm_f": np.ascontiguousarray(np.asarray(inputs["L0_attn_norm_f_w"], np.float32).reshape(1, DM)),
        "wg_T": _tile_ffn(np.asarray(inputs["L0_ffn_w_gate"], np.int32).T.astype(_BF)),
        "wu_T": _tile_ffn(np.asarray(inputs["L0_ffn_w_up"], np.int32).T.astype(_BF)),
        "wd_T": np.ascontiguousarray(np.asarray(inputs["L0_ffn_w_down"], np.int32).T.astype(_BF)),
        "eye": np.eye(128, dtype=np.float32),
    }
    return d


def _tile_ffn(wT):
    DM, HID = wT.shape
    return np.ascontiguousarray(
        wT.reshape(DM // 128, 128, HID // 128, 128).transpose(2, 1, 0, 3)
          .reshape(HID // 128, 128, DM))


def assemble_output(results, L=4096, NCH=8, NCH_PRE=0, B=4):
    DM = 1024
    LH = L // 2
    out = np.empty((B, L, DM), np.float32)
    for core_id in range(2 * B):
        b, j = core_id // 2, core_id % 2
        o = results[core_id]["out"]
        out[b, j * LH:(j + 1) * LH] = o[j * LH:(j + 1) * LH]
    return out


from concourse.bass_utils import run_bass_kernel_spmd

_BUILT = {}

def _get_program():
    if "nc" not in _BUILT:
        nc, meta = build_program(L=4096, T=512)
        _BUILT["nc"] = nc
        _BUILT["meta"] = meta
    return _BUILT["nc"], _BUILT["meta"]


def kernel(**inputs):
    nc, meta = _get_program()
    in_maps = [make_core_inputs(c, inputs, L=meta["L"], NCH=meta["NCH"],
                                NCH_PRE=meta["NCH_PRE"]) for c in range(8)]
    res = run_bass_kernel_spmd(nc, in_maps, core_ids=list(range(8)))
    return assemble_output(list(res.results), L=meta["L"], NCH=meta["NCH"],
                           NCH_PRE=meta["NCH_PRE"], B=4)


# version 31
# speedup vs baseline: 1.1071x; 1.0112x over previous
import numpy as np
import ml_dtypes
_BF = ml_dtypes.bfloat16
from contextlib import ExitStack
import concourse.bass as bass
import concourse.mybir as mybir

F32 = mybir.dt.float32
BF16 = mybir.dt.bfloat16
I32 = mybir.dt.int32
AF = mybir.ActivationFunctionType
ALU = mybir.AluOpType

ENGINES = ("sync", "scalar", "vector", "tensor", "gpsimd")
CSEM = {"scalar": "cS", "vector": "cV", "tensor": "cT", "gpsimd": "cG"}
DQ = {"sync": tuple(f"dS{i}" for i in range(8)),
      "scalar": tuple(f"dA{i}" for i in range(8)),
      "gpsimd": tuple(f"dG{i}" for i in range(4))}
ALL_SEMS = ("cS", "cV", "cT", "cG", "cC") + DQ["sync"] + DQ["scalar"] + DQ["gpsimd"]
BATCH_MAX = 2


class Ev:
    __slots__ = ("sem", "val")
    def __init__(self, sem, val):
        self.sem = sem
        self.val = val


class DmaEv:
    __slots__ = ("q", "batch")
    def __init__(self, q, batch):
        self.q = q
        self.batch = batch


class PendEv:
    __slots__ = ("eng", "ev")
    def __init__(self, eng):
        self.eng = eng
        self.ev = None


class Buf:
    __slots__ = ("w", "reads")
    def __init__(self):
        self.w = None
        self.reads = []


class Prog:
    def __init__(self):
        self.ops = {e: [] for e in ENGINES}
        self.ctr = {s: 0 for s in ALL_SEMS}
        self.waited = {e: {s: 0 for s in ALL_SEMS} for e in ENGINES}
        self.ninstr = 0
        self.q = {e: dict(nopen=0, nbatch=0, open=None) for e in DQ}
        self.pending = {e: [] for e in ENGINES}

    def _close_batch(self, qeng, which=None):
        st = self.q[qeng]
        if st["open"] is not None:
            batch = st["open"]
            batch[0] = True
            batch[2] = self.ctr[batch[1]]
            st["open"] = None
            st["nopen"] = 0

    def _resolve_pending(self, eng2):
        sem = CSEM[eng2]
        self.ctr[sem] += 1
        val = self.ctr[sem]
        self.ops[eng2].append(lambda e, S, _s=sem: e.nop().then_inc(S[_s], 1))
        self.ninstr += 1
        ev = Ev(sem, val)
        for p in self.pending[eng2]:
            p.ev = ev
        self.pending[eng2] = []
        return ev

    def _wait(self, eng, ev, raw=True):
        if ev is None:
            return
        if isinstance(ev, PendEv):
            if ev.ev is None:
                self._resolve_pending(ev.eng)
            ev = ev.ev
        if isinstance(ev, DmaEv):
            if not ev.batch[0]:
                self._close_batch(ev.q)
            ev = Ev(ev.batch[1], ev.batch[2])
        if self.waited[eng].get(ev.sem, 0) >= ev.val:
            return
        self.waited[eng][ev.sem] = ev.val
        sem_name, val = ev.sem, ev.val
        self.ops[eng].append(lambda e, S, _s=sem_name, _v=val: e.wait_ge(S[_s], _v))
        self.ninstr += 1

    def op(self, eng, fn, reads=(), writes=(), war=(), dma=False, track=True,
           sem_override=None):
        for b in reads:
            self._wait(eng, b.w, raw=True)
        for b in war:
            self._wait(eng, b.w, raw=False)
            for r in b.reads:
                self._wait(eng, r, raw=False)
        for b in writes:
            self._wait(eng, b.w, raw=False)
            for r in b.reads:
                self._wait(eng, r, raw=False)
        if dma:
            st = self.q[eng]
            if st["open"] is None or st["nopen"] >= BATCH_MAX:
                self._close_batch(eng)
                pool = DQ[eng]
                psem = pool[st["nbatch"] % len(pool)]
                st["nbatch"] += 1
                self._wait(eng, Ev(psem, self.ctr[psem]))
                st["open"] = [False, psem, 0]
            batch = st["open"]
            psem = batch[1]
            self.ctr[psem] += 16
            st["nopen"] += 1
            def run(e, S, fn=fn, psem=psem):
                fn(e, S).then_inc(S[psem], 16)
            ev = DmaEv(eng, batch)
            self.ops[eng].append(run)
            self.ninstr += 1
            for b in writes:
                b.w = ev
                b.reads = []
            for b in reads:
                b.reads.append(ev)
            return ev
        if track:
            sem = sem_override or CSEM[eng]
            self.ctr[sem] += 1
            val = self.ctr[sem]
            def run(e, S, fn=fn, sem=sem):
                fn(e, S).then_inc(S[sem], 1)
            ev = Ev(sem, val)
            for p in self.pending[eng]:
                p.ev = ev
            self.pending[eng] = []
        else:
            def run(e, S, fn=fn):
                fn(e, S)
            ev = None
        self.ops[eng].append(run)
        self.ninstr += 1
        if track:
            for b in writes:
                b.w = ev
                b.reads = []
            for b in reads:
                b.reads.append(ev)
        else:
            pe = PendEv(eng)
            self.pending[eng].append(pe)
            for b in reads:
                b.reads.append(pe)
        return ev

    def mm(self, eng, fn, reads=(), psum=None, start=False, stop=False):
        if start and stop:
            return self.op(eng, fn, reads=reads, writes=[psum])
        if start:
            return self.op(eng, fn, reads=reads, war=[psum])
        if stop:
            return self.op(eng, fn, reads=reads, writes=[psum])
        return self.op(eng, fn, reads=reads)

    def raw_wait(self, eng, ev):
        self._wait(eng, ev)

    def flush_queues(self, eng):
        for qeng in DQ:
            self._close_batch(qeng)
            for psem in DQ[qeng]:
                self._wait(eng, Ev(psem, self.ctr[psem]))

    def finalize_into(self, nc, sems):
        with nc.Block() as block:
            for eng in ENGINES:
                ops = self.ops[eng]
                if not ops:
                    continue
                getattr(block, eng)(_mkbody(ops, sems))


def _mkbody(ops, sems):
    def body(e):
        for fn in ops:
            fn(e, sems)
    return body


DEBUG_SCR = False

def build_program(L=4096, T=512, NCH_PRE=None):
    DM, DI, N, RANK, HID = 1024, 2048, 16, 128, 3072
    NCH = L // T
    NCH_PRE = 0
    L_OWN = L
    DIH = DI // 2
    CBS = DIH // 128
    LH = L // 2
    RG = [[0, 1], [2, 3], [4, 5], [6, 7]]
    KB = DM // 128
    CB = DI // 128
    HB = HID // 128
    TT = T // 128
    NG = 1
    NGRP = N // NG
    TF = T // 2 if T >= 256 else T
    EPS = 1e-5
    QINV = float(np.float32(1.0 / 127.0))
    NCO = (DI + DIH) // 128

    nc = bass.Bass("TRN2", target_bir_lowering=False, debug=False)

    def din(name, shape, dt):
        return nc.dram_tensor(name, list(shape), dt, kind="ExternalInput").ap()

    x_d = din("x", (L, DM), F32)
    mask_d = din("mask", (128, NCH), F32)
    w_in_d = din("w_in", (DM, DI + DIH), F32)
    norm_w_d = din("norm_w", (128, KB), F32)
    conv_w_d = din("conv_w_r", (128, CB * 4), F32)
    conv_b_d = din("conv_b_r", (128, CB), F32)
    xproj_d = din("xproj_T", (DI, 160), F32)
    dtw_d = din("dtw_T", (RANK, DIH), F32)
    dtb_d = din("dtb_r", (128, CBS), F32)
    alog_d = din("alog_r", (128, CBS * N), F32)
    dvec_d = din("d_r", (128, CBS), F32)
    wout_d = din("wout_T", (DI, DM), F32)
    normf_d = din("norm_f", (1, DM), F32)
    wg_d = din("wg_T", (HB, 128, KB * 128), BF16)
    wu_d = din("wu_T", (HB, 128, KB * 128), BF16)
    wd_d = din("wd_T", (HID, DM), BF16)
    eye_d = din("eye", (128, 128), F32)
    xf_d = din("xf", (LH, DM), F32)

    out_d = nc.dram_tensor("out", [L_OWN, DM], F32, kind="ExternalOutput").ap()

    def scr(name, shape, dt):
        return nc.dram_tensor(name, list(shape), dt, kind="Internal").ap()

    scr_win = scr("scr_win", (NCO, 128, KB * 128), BF16)
    scr_wout = scr("scr_wout", (DI, DM), BF16)
    scr_bc = scr("scr_bc", (2, 32, T), BF16)
    scr_xn = scr("scr_xn", (L, DM), F32)
    scr_nf = scr("scr_nf", (1, DM), F32)
    cc_yin = scr("cc_yin", (NCH, 128, CBS * T), BF16)
    cc_yout = scr("cc_yout", (NCH, 256, CBS * T), BF16)
    cc_gin = scr("cc_gin", (NCH // 2, T, DM), BF16)
    cc_gout = scr("cc_gout", (NCH // 2, 2 * T, DM), BF16)
    scr_du = scr("scr_du", (NCH, 128, CB * T), BF16) if DEBUG_SCR else None
    scr_dyg = scr("scr_dyg", (NCH, 128, CB * T), BF16) if DEBUG_SCR else None

    ctx = ExitStack()
    sb = lambda n, s, d: ctx.enter_context(nc.sbuf_tensor(n, list(s), d))
    psx = lambda n, s, d: ctx.enter_context(nc.psum_tensor(n, list(s), d))

    eye_bf = sb("eye_bf", (128, 128), BF16)
    norm_w_t = sb("norm_w_t", (128, KB), F32)
    conv_w_t = sb("conv_w_t", (128, CB * 4), F32)
    conv_b_t = sb("conv_b_t", (128, CB), F32)
    dtb_t = sb("dtb_t", (128, CBS), F32)
    acol_t = sb("acol_t", (128, CBS * N), F32)
    dvec_t = sb("dvec_t", (128, CBS), F32)
    mask_t = sb("mask_t", (128, NCH), F32)
    normf_rep = sb("normf_rep", (128, DM), BF16)
    carry = sb("carry", (128, CBS * N), F32)
    xpj = sb("xpj", (128, CB * 160), BF16)
    dtw_bf = sb("dtw_bf", (128, DIH), BF16)
    stat6 = sb("stat6", (128, 12), F32)
    stat2 = sb("stat2", (128, 2), F32)
    sc1 = sb("sc1", (128, 8), F32)
    cst = sb("cst", (128, 2), F32)
    xt = [sb(f"xt{i}", (128, DM), F32) for i in range(2)]
    xnt = [sb(f"xnt{i}", (128, DM), F32) for i in range(2)]
    rms_bf = [sb(f"rmsbf{i}", (128, DM), BF16) for i in range(1)]
    rms_T = [sb(f"rms_T{i}", (128, KB * T), BF16) for i in range(2)]
    xn_T = sb("xn_T", (128, KB * T), BF16)
    xbr = [sb(f"xbr{i}", (128, T + 4), BF16) for i in range(2)]
    halo_t = sb("halo_t", (128, CB * 4), BF16)
    u_bf = [sb(f"u_bf{i}", (128, CBS * T), BF16) for i in range(2)]
    u_oth = sb("u_oth", (128, (CB - CBS) * T), BF16)
    z_bf = [sb(f"zbf{i}", (128, T), BF16) for i in range(2)]
    dtl_bf = [sb(f"dtl{i}", (128, T), BF16) for i in range(2)]
    sp_t = [sb(f"sp{i}", (128, T), F32) for i in range(2)]
    dtu_t = [sb(f"dtu{i}", (128, T), BF16) for i in range(2)]
    a_t = [sb(f"a{i}", (128, NG * T), F32) for i in range(3)]
    b_t = [sb(f"b{i}", (128, NG * T), BF16) for i in range(2)]
    yh_t = [sb(f"yh{i}", (128, NG * T), BF16) for i in range(3)]
    brep = sb("brep", (128, N * T), BF16)
    crep = sb("crep", (128, N * T), BF16)
    bc_sb = sb("bc_sb", (32, T), BF16)
    yg_bf = sb("yg_bf", (128, CBS * T), BF16)
    yg_lo = sb("yg_lo", (128, CBS * T), BF16)
    yg_hi = sb("yg_hi", (128, CBS * T), BF16)
    diag_t = [sb("diag0", (128, 4 * 128), BF16)]
    wblk = [sb(f"wblk{i}", (128, KB * 128), BF16) for i in range(2)]
    wffn = [sb(f"wffn{i}", (128, KB * 128), BF16) for i in range(3)]
    wffn2 = [sb(f"wffn2{i}", (128, KB * 128), BF16) for i in range(3)]
    wdn = [sb(f"wdn{i}", (128, 2 * DM), BF16) for i in range(2)]
    woutb = [sb(f"woutb{i}", (128, 2 * DM), BF16) for i in range(2)]
    gu_s = [sb(f"gu_s{i}", (128, TF), BF16) for i in range(2)]
    hffn = sb("hffn", (128, HB * TF), BF16)
    g_t = sb("g_t", (128, DM), BF16)
    castf_t = sp_t[0] if T >= 256 else sb("cast_fs", (128, 256), F32)
    castf_key = "sp0" if T >= 256 else "cast_fs"

    p_mm = psx("p_mm", (128, 512), F32)
    p_mm2 = psx("p_mm2", (128, 512), F32)
    p_conv = psx("p_conv", (128, 512), F32)
    p_ys = [psx(f"p_ys{i}", (128, 512), F32) for i in range(2)]
    p_tr = psx("p_tr", (128, 1024), BF16)
    p_o = psx("p_o", (128, 1024), F32)

    bufs = {}
    def gb(key):
        if key not in bufs:
            bufs[key] = Buf()
        return bufs[key]

    P = Prog()

    def load(dst_ap, src_ap, key=None, eng="sync", reads=(), writes=(), war=()):
        w = list(writes)
        if key is not None:
            w.append(gb(key))
        return P.op(eng, lambda e, S, d=dst_ap, s=src_ap: e.dma_start(out=d, in_=s),
                    reads=list(reads), writes=w, war=list(war), dma=True)

    load(xt[0][:, 0:128], eye_d[:], "xt0")
    P.op("vector", lambda e, S: e.tensor_copy(eye_bf[:], xt[0][:, 0:128]),
         reads=[gb("xt0")], writes=[gb("eye_bf")])
    load(norm_w_t[:], norm_w_d[:], "norm_w")
    load(conv_w_t[:], conv_w_d[:], "conv_w")
    load(conv_b_t[:], conv_b_d[:], "conv_b")
    load(dtb_t[:], dtb_d[:], "dtb")
    load(acol_t[:], alog_d[:], "acol")
    load(dvec_t[:], dvec_d[:], "dvec")
    load(mask_t[:], mask_d[:], "mask")
    load(scr_nf[:], normf_d[:], "scr_nf", eng="scalar")
    load(xnt[0][:], scr_nf[:].broadcast_to((128, DM)), "xnt0", eng="scalar",
         reads=[gb("scr_nf")])
    P.op("vector", lambda e, S: e.tensor_copy(normf_rep[:], xnt[0][:]),
         reads=[gb("xnt0")], writes=[gb("normf_rep")])
    P.op("scalar", lambda e, S: e.activation(acol_t[:], acol_t[:], AF.Exp),
         reads=[gb("acol")], writes=[gb("acol")])
    P.op("vector", lambda e, S: e.tensor_scalar_mul(acol_t[:], acol_t[:], -1.0),
         reads=[gb("acol")], writes=[gb("acol")])
    for cb in range(CB):
        load(castf_t[:, 0:160], xproj_d[cb * 128:(cb + 1) * 128, :], castf_key)
        P.op("vector", lambda e, S, cb=cb: e.tensor_copy(xpj[:, cb * 160:(cb + 1) * 160],
                                                         castf_t[:, 0:160]),
             reads=[gb(castf_key)], writes=[gb("xpj")])
    for j in range(DIH // 256):
        load(castf_t[:, 0:256], dtw_d[:, j * 256:(j + 1) * 256], castf_key)
        P.op("vector", lambda e, S, j=j: e.tensor_copy(dtw_bf[:, j * 256:(j + 1) * 256], castf_t[:, 0:256]),
             reads=[gb(castf_key)], writes=[gb("dtw_bf")])

    def precast_wide(srcd, dst, rows, cols, skey, tiled, fold=False):
        ring = [(xt[0], "xt0"), (xt[1], "xt1"), (xnt[1], "xnt1")]
        CWW = min(DM, cols)
        it = 0
        for rb in range(rows // 128):
            for cc in range(cols // CWW):
                c0 = cc * CWW
                tile, tkey = ring[it % 3]
                it += 1
                load(tile[:, 0:CWW], srcd[rb * 128:(rb + 1) * 128, c0:c0 + CWW], tkey)
                btile, bkey = ((rms_bf[0], "rmsbf0") if it % 2 == 0 else (g_t, "g_raw"))
                if fold:
                    P.op("vector", lambda e, S, rb=rb, tile=tile, btile=btile, CWW=CWW: e.tensor_scalar(
                            btile[:, 0:CWW], tile[:, 0:CWW], norm_w_t[:, rb:rb + 1], None, ALU.mult),
                         reads=[gb(tkey), gb("norm_w")], writes=[gb(bkey)])
                else:
                    P.op("vector", lambda e, S, tile=tile, btile=btile, CWW=CWW: e.tensor_copy(
                            btile[:, 0:CWW], tile[:, 0:CWW]),
                         reads=[gb(tkey)], writes=[gb(bkey)])
                if tiled:
                    for q in range(CWW // 128):
                        co = (c0 + q * 128) // 128
                        P.op("scalar", lambda e, S, co=co, rb=rb, q=q, btile=btile, dst=dst: e.dma_start(
                                out=dst[co, :, rb * 128:(rb + 1) * 128],
                                in_=btile[:, q * 128:(q + 1) * 128]),
                             reads=[gb(bkey)], writes=[gb(skey + "scr")], dma=True)
                else:
                    P.op("scalar", lambda e, S, rb=rb, c0=c0, btile=btile, dst=dst, CWW=CWW: e.dma_start(
                            out=dst[rb * 128:(rb + 1) * 128, c0:c0 + CWW],
                            in_=btile[:, 0:CWW]),
                         reads=[gb(bkey)], writes=[gb(skey + "scr")], dma=True)

    precast_wide(w_in_d, scr_win, DM, DI + DIH, "win", tiled=True, fold=True)


    P.op("vector", lambda e, S: e.memset(cst[:, 0:1], 1e-5), writes=[gb("cst")])
    P.op("vector", lambda e, S: e.memset(cst[:, 1:2], 1.0), writes=[gb("cst")])
    P.op("vector", lambda e, S: e.memset(carry[:], 0.0), writes=[gb("carry")])
    _evx = P.op("vector", lambda e, S: e.memset(halo_t[:], 0.0), writes=[gb("halo_all")])
    for _cb in range(CB):
        gb(f"halo{_cb}").w = _evx

    def ln_rms_stats(src_ap, src_buf, out_col, is_rms):
        P.op("vector", lambda e, S: e.bn_stats(stat6[:, 0:6], src_ap[:, 0:512]),
             reads=[src_buf], writes=[gb("st6")])
        P.op("vector", lambda e, S: e.bn_stats(stat6[:, 6:12], src_ap[:, 512:1024]),
             reads=[src_buf], writes=[gb("st6")])
        P.op("vector", lambda e, S: e.bn_aggr(stat2[:], stat6[:]),
             reads=[gb("st6")], writes=[gb("st2")])
        if is_rms:
            P.op("vector", lambda e, S: e.tensor_tensor(sc1[:, 3:4], stat2[:, 0:1],
                                                        stat2[:, 0:1], ALU.mult),
                 reads=[gb("st2")], writes=[gb("m2")])
            P.op("vector", lambda e, S: e.tensor_tensor(sc1[:, 4:5], sc1[:, 3:4],
                                                        stat2[:, 1:2], ALU.add),
                 reads=[gb("m2"), gb("st2")], writes=[gb("ms")])
            P.op("scalar", lambda e, S: e.activation(sc1[:, 5:6], sc1[:, 4:5], AF.Ln, bias=cst[:, 0:1]),
                 reads=[gb("ms"), gb("cst")], writes=[gb("sd")])
        else:
            P.op("vector", lambda e, S: e.tensor_scalar_mul(
                    sc1[:, 1:2], stat2[:, 0:1], -1.0),
                 reads=[gb("st2")], writes=[gb("negm")])
            P.op("scalar", lambda e, S: e.activation(sc1[:, 5:6], stat2[:, 1:2], AF.Ln, bias=cst[:, 0:1]),
                 reads=[gb("st2"), gb("cst")], writes=[gb("sd")])
        P.op("scalar", lambda e, S, c=out_col: e.activation(sc1[:, c:c + 1], sc1[:, 5:6],
                                                            AF.Exp, scale=-0.5),
             reads=[gb("sd")], writes=[gb(f"rstd{out_col}")])


    def ln_closures(k):
        t0 = k * T
        dbk = k % 2
        a = [(0, _ln_tt(k, tt, True, t0, dbk)) for tt in range(TT)]
        return a, []

    def emit_ln(k):
        a, _ = ln_closures(k)
        for _, f in a:
            f()

    def _ln_tt(k, tt, full, t0, dbk):
      def f(tt=tt):
              row = t0 + tt * 128
              pp = tt % 2
              bx = gb(f"xt{pp}")
              load(xt[pp][:], x_d[row:row + 128, :], f"xt{pp}")
              ln_rms_stats(xt[pp][:], bx, 2, is_rms=False)
              P.op("vector", lambda e, S, pp=pp: e.tensor_scalar(
                      xnt[pp][:], xt[pp][:], sc1[:, 1:2], sc1[:, 2:3], ALU.add, ALU.mult),
                   reads=[bx, gb("negm"), gb("rstd2")], writes=[gb(f"xnt{pp}")])
              if full:
                  P.op("scalar", lambda e, S, pp=pp, row=row: e.dma_start(
                          out=scr_xn[row:row + 128, :], in_=xnt[pp][:]),
                       reads=[gb(f"xnt{pp}")], writes=[gb(f"sxn{k}_{tt}")], dma=True)
              ln_rms_stats(xnt[pp][:], gb(f"xnt{pp}"), 6, is_rms=True)
              P.op("vector", lambda e, S, pp=pp: e.tensor_scalar_mul(
                      rms_bf[0][:], xnt[pp][:], sc1[:, 6:7]),
                   reads=[gb(f"xnt{pp}"), gb("rstd6")], writes=[gb("rmsbf0")])
              for kb in range(KB):
                  P.mm("tensor", lambda e, S, pp=pp, kb=kb: e.transpose(
                          p_tr[:, kb * 128:(kb + 1) * 128],
                          rms_bf[0][:, kb * 128:(kb + 1) * 128], eye_bf[:]),
                       reads=[gb("rmsbf0"), gb("eye_bf")], psum=gb("p_tr"),
                       start=(kb == 0), stop=(kb == KB - 1))
              P.op("scalar", lambda e, S, tt=tt, dbk=dbk: e.copy(
                      rms_T[dbk][:].rearrange("p (kb t) -> p kb t", kb=KB)[:, :, tt * 128:(tt + 1) * 128],
                      p_tr[:].rearrange("p (kb c) -> p kb c", kb=KB)),
                   reads=[gb("p_tr")], writes=[gb(f"rms_T{dbk}")])
      return f

    def _ln_xnT(m, tt):
      def f(tt=tt):
          row = m * T + tt * 128
          pp = tt % 2
          load(xt[pp][:], xf_d[row:row + 128, :], f"xt{pp}", eng="gpsimd")
          ln_rms_stats(xt[pp][:], gb(f"xt{pp}"), 2, is_rms=False)
          P.op("vector", lambda e, S, pp=pp: e.tensor_scalar(
                  xnt[pp][:], xt[pp][:], sc1[:, 1:2], sc1[:, 2:3], ALU.add, ALU.mult),
               reads=[gb(f"xt{pp}"), gb("negm"), gb("rstd2")], writes=[gb(f"xnt{pp}")])
          P.op("vector", lambda e, S, pp=pp: e.tensor_copy(rms_bf[0][:], xnt[pp][:]),
               reads=[gb(f"xnt{pp}")], writes=[gb("rmsbf0")])
          for kb in range(KB):
              P.mm("tensor", lambda e, S, pp=pp, kb=kb: e.transpose(
                      p_tr[:, kb * 128:(kb + 1) * 128],
                      rms_bf[0][:, kb * 128:(kb + 1) * 128], eye_bf[:]),
                   reads=[gb("rmsbf0"), gb("eye_bf")], psum=gb("p_tr"),
                   start=(kb == 0), stop=(kb == KB - 1))
          P.op("scalar", lambda e, S, tt=tt: e.copy(
                  xn_T[:].rearrange("p (kb t) -> p kb t", kb=KB)[:, :, tt * 128:(tt + 1) * 128],
                  p_tr[:].rearrange("p (kb c) -> p kb c", kb=KB)),
               reads=[gb("p_tr")], writes=[gb("xn_T")])
      return f
    def x_closures(k):
        full = k >= NCH_PRE
        dbk = k % 2
        out = []

        def inproj_conv(cb):
            def f(cb=cb):
                wb = cb % 2
                xb = xbr[wb]
                load(wblk[wb][:], scr_win[cb, :, :], f"wblk{wb}", reads=[gb("winscr")])
                pm, pmb = (p_mm, gb("p_mm")) if cb % 2 == 0 else (p_mm2, gb("p_mm2"))
                for kb in range(KB):
                    P.mm("tensor", lambda e, S, pm=pm, wb=wb, kb=kb, dbk=dbk: e.matmul(
                            pm[:, 0:T], wblk[wb][:, kb * 128:(kb + 1) * 128],
                            rms_T[dbk][:, kb * T:(kb + 1) * T],
                            start=(kb == 0), stop=(kb == KB - 1)),
                         reads=[gb(f"wblk{wb}"), gb(f"rms_T{dbk}")], psum=pmb,
                         start=(kb == 0), stop=(kb == KB - 1))
                P.op("scalar", lambda e, S, pm=pm, xb=xb: e.copy(
                        xb[:, 4: 4 + T], pm[:, 0:T]),
                     reads=[pmb], writes=[gb(f"xbrd{wb}")])
                P.op("vector", lambda e, S, cb=cb, xb=xb: e.tensor_copy(
                        xb[:, 1:4], halo_t[:, cb * 4 + 1: cb * 4 + 4]),
                     reads=[gb(f"halo{cb}")], writes=[gb(f"xbrh{wb}")])
                for tap in range(4):
                    P.op("vector", lambda e, S, cb=cb, tap=tap, wb=wb: e.tensor_scalar(
                            diag_t[0][:, tap * 128:(tap + 1) * 128], eye_bf[:],
                            conv_w_t[:, cb * 4 + tap:cb * 4 + tap + 1], None, ALU.mult),
                         reads=[gb("eye_bf"), gb("conv_w")], writes=[gb("diag0")])
                for tap in range(4):
                    P.mm("tensor", lambda e, S, tap=tap, xb=xb: e.matmul(
                            p_conv[:, 0:T], diag_t[0][:, tap * 128:(tap + 1) * 128],
                            xb[:, 1 + tap: 1 + tap + T],
                            start=(tap == 0), stop=(tap == 3)),
                         reads=[gb("diag0"), gb(f"xbrd{wb}"), gb(f"xbrh{wb}")],
                         psum=gb("p_conv"), start=(tap == 0), stop=(tap == 3))
                if cb < CBS:
                    P.op("scalar", lambda e, S, cb=cb, dbk=dbk: e.activation(
                            u_bf[dbk][:, cb * T:(cb + 1) * T], p_conv[:, 0:T], AF.Silu,
                            bias=conv_b_t[:, cb:cb + 1]),
                         reads=[gb("p_conv"), gb("conv_b")], writes=[gb(f"u{dbk}_{cb}")])
                else:
                    P.op("scalar", lambda e, S, cb=cb: e.activation(
                            u_oth[:, (cb - CBS) * T:(cb - CBS + 1) * T], p_conv[:, 0:T],
                            AF.Silu, bias=conv_b_t[:, cb:cb + 1]),
                         reads=[gb("p_conv"), gb("conv_b")], writes=[gb(f"uo_{cb}")])
                P.op("vector", lambda e, S, cb=cb, xb=xb: e.tensor_copy(
                        halo_t[:, cb * 4 + 1: cb * 4 + 4], xb[:, T + 1: T + 4]),
                     reads=[gb(f"xbrd{wb}")], writes=[gb(f"halo{cb}")])
            return f

        def xproj():
            def f():
                def usrc(cb):
                    if cb < CBS:
                        return u_bf[dbk][:, cb * T:(cb + 1) * T], gb(f"u{dbk}_{cb}")
                    return u_oth[:, (cb - CBS) * T:(cb - CBS + 1) * T], gb(f"uo_{cb}")
                for cb in range(CB):
                    ua, ub = usrc(cb)
                    P.mm("tensor", lambda e, S, cb=cb, ua=ua: e.matmul(
                            p_mm[:, 0:T], xpj[:, cb * 160: cb * 160 + 128], ua,
                            start=(cb == 0), stop=(cb == CB - 1)),
                         reads=[gb("xpj"), ub], psum=gb("p_mm"),
                         start=(cb == 0), stop=(cb == CB - 1))
                for cb in range(CB):
                    ua, ub = usrc(cb)
                    P.mm("tensor", lambda e, S, cb=cb, ua=ua: e.matmul(
                            p_mm2[0:32, 0:T], xpj[:, cb * 160 + 128: (cb + 1) * 160], ua,
                            start=(cb == 0), stop=(cb == CB - 1)),
                         reads=[gb("xpj"), ub], psum=gb("p_mm2"),
                         start=(cb == 0), stop=(cb == CB - 1))
                P.op("scalar", lambda e, S, dbk=dbk: e.copy(dtl_bf[dbk][:], p_mm[:, 0:T]),
                     reads=[gb("p_mm")], writes=[gb(f"dtl{dbk}")])
                P.op("scalar", lambda e, S, k=k: e.activation(bc_sb[:], p_mm2[0:32, 0:T],
                                                              AF.Copy, scale=mask_t[0:32, k:k + 1]),
                     reads=[gb("p_mm2"), gb("mask")], writes=[gb("bc_sb")])
                P.op("scalar", lambda e, S, dbk=dbk: e.dma_start(out=scr_bc[dbk], in_=bc_sb[:]),
                     reads=[gb("bc_sb")], writes=[gb(f"sbc{dbk}")], dma=True)
            return f

        def bcasts():
            def f():
                bsrc = scr_bc[dbk:dbk + 1].rearrange("c a b -> c (a b)")
                P.op("sync", lambda e, S, bsrc=bsrc: e.dma_start(
                        out=brep[:],
                        in_=bsrc[:, 0:N * T].broadcast_to((128, N * T))),
                     reads=[gb(f"sbc{dbk}")],
                     writes=[gb(f"brep{nn}") for nn in range(N)], dma=True)
                P.op("sync", lambda e, S, bsrc=bsrc: e.dma_start(
                        out=crep[:],
                        in_=bsrc[:, N * T:2 * N * T].broadcast_to((128, N * T))),
                     reads=[gb(f"sbc{dbk}")],
                     writes=[gb(f"crep{nn}") for nn in range(N)], dma=True)
            return f

        for cb in range(CB):
            out.append((0, inproj_conv(cb)))
        out.append((0, xproj()))
        out.append((CBS, bcasts()))
        return out

    def ffn_closures(m):
        if m < 0 or m >= LH // T:
            return []
        items = []
        for tt in range(TT):
            items.append(_ln_xnT(m, tt))
        HF = 2 if TT >= 2 else 1
        fg, fd = _ffn_makers(m)
        for hf in range(HF):
            for hj in range(HB):
                items.append(fg(hf, hj))
            for tl in range(TT // HF):
                items.append(fd(hf, tl))
        nit = len(items)
        return [((i * (CBS - 2)) // nit, f) for i, f in enumerate(items)]

    def _ffn_makers(k):
        dbk = k % 2
        NGU = (2 if TT >= 2 else 1) * HB

        def gu_load(idx):
            hj = idx % HB
            wb = idx % 3
            qe = "sync" if idx % 2 == 0 else "scalar"
            load(wffn[wb][:], wg_d[hj, :, :], f"wffn{wb}", eng=qe)
            load(wffn2[wb][:], wu_d[hj, :, :], f"wffn2{wb}", eng=qe)

        def ffn_gu(hf, hj):
            idx = hf * HB + hj
            def f(hf=hf, hj=hj, idx=idx):
                if idx == 0:
                    gu_load(0)
                    gu_load(1)
                if idx + 2 < NGU:
                    gu_load(idx + 2)
                wb = idx % 3
                for kb in range(KB):
                    P.mm("tensor", lambda e, S, wb=wb, kb=kb, hf=hf: e.matmul(
                            p_mm[:, 0:TF], wffn[wb][:, kb * 128:(kb + 1) * 128],
                            xn_T[:, kb * T + hf * TF: kb * T + (hf + 1) * TF],
                            start=(kb == 0), stop=(kb == KB - 1)),
                         reads=[gb(f"wffn{wb}"), gb("xn_T")], psum=gb("p_mm"),
                         start=(kb == 0), stop=(kb == KB - 1))
                for kb in range(KB):
                    P.mm("tensor", lambda e, S, wb=wb, kb=kb, hf=hf: e.matmul(
                            p_mm2[:, 0:TF], wffn2[wb][:, kb * 128:(kb + 1) * 128],
                            xn_T[:, kb * T + hf * TF: kb * T + (hf + 1) * TF],
                            start=(kb == 0), stop=(kb == KB - 1)),
                         reads=[gb(f"wffn2{wb}"), gb("xn_T")], psum=gb("p_mm2"),
                         start=(kb == 0), stop=(kb == KB - 1))
                s0, s1 = 0, 1
                P.op("scalar", lambda e, S, s0=s0: e.activation(gu_s[s0][:], p_mm[:, 0:TF],
                                                                AF.Silu, scale=QINV),
                     reads=[gb("p_mm")], writes=[gb(f"gu_s{s0}")])
                P.op("scalar", lambda e, S, s1=s1: e.activation(gu_s[s1][:], p_mm2[:, 0:TF],
                                                                AF.Copy, scale=QINV),
                     reads=[gb("p_mm2")], writes=[gb(f"gu_s{s1}")])
                P.op("gpsimd", lambda e, S, hj=hj, s0=s0, s1=s1: e.tensor_tensor(
                        hffn[:, hj * TF:(hj + 1) * TF], gu_s[s0][:], gu_s[s1][:], ALU.mult),
                     reads=[gb(f"gu_s{s0}"), gb(f"gu_s{s1}")], writes=[gb(f"h{hj}")])
            return f

        def dn_load(pu, tl):
            wb = pu % 2
            qe = "sync" if pu % 2 == 0 else "scalar"
            s = wd_d[pu * 256:(pu + 1) * 256, :].rearrange("(h p) m -> p h m", p=128)
            load(wdn[wb][:].rearrange("p (h m) -> p h m", h=2), s, f"wdn{wb}", eng=qe)

        def ffn_dn(hf, tl):
            def f(hf=hf, tl=tl):
                tt = hf * (TT // (2 if TT >= 2 else 1)) + tl
                dn_load(0, tl)
                for pu in range(HB // 2):
                    wb = pu % 2
                    if pu + 1 < HB // 2:
                        dn_load(pu + 1, tl)
                    for h in range(2):
                        hj = 2 * pu + h
                        for dh in range(2):
                            P.mm("tensor", lambda e, S, hj=hj, h=h, wb=wb, dh=dh, tl=tl: e.matmul(
                                    p_o[:, dh * 512:(dh + 1) * 512],
                                    hffn[:, hj * TF + tl * 128: hj * TF + tl * 128 + 128],
                                    wdn[wb][:, h * DM + dh * 512: h * DM + (dh + 1) * 512],
                                    start=(hj == 0), stop=(hj == HB - 1)),
                                 reads=[gb(f"h{hj}"), gb(f"wdn{wb}")], psum=gb("p_o"),
                                 start=(hj == 0 and dh == 0), stop=(hj == HB - 1 and dh == 1))
                P.op("scalar", lambda e, S, tt=tt: e.activation(
                        g_t[:], p_o[:], AF.Silu, scale=QINV),
                     reads=[gb("p_o")], writes=[gb("g_raw")])
                P.op("scalar", lambda e, S, tt=tt: e.dma_start(
                        out=cc_gin[k, tt * 128:(tt + 1) * 128, :], in_=g_t[:]),
                     reads=[gb("g_raw")], writes=[gb(f"sg{k}_{tt}")], dma=True)
            return f

        return ffn_gu, ffn_dn

    def o_closures(k):
        if k < 0:
            return []
        t0 = k * T
        out = []

        def oload():
            def f():
                pw = ([gb(f"pci{r}") for r in range(4)] + [gb(f"pco{r}") for r in range(4)]
                      if k == 0 else [])
                load(yg_lo[:], cc_yout[k, 0:128, :], "yglo", eng="sync",
                     reads=[gb(f"ccyo{k}")], war=pw)
                load(yg_hi[:], cc_yout[k, 128:256, :], "yghi", eng="sync",
                     reads=[gb(f"ccyo{k}")], war=pw)
            return f

        def wo_load(pu):
            wb = pu % 2
            qe = "sync" if pu % 2 == 0 else "scalar"
            s = scr_wout[pu * 256:(pu + 1) * 256, :].rearrange("(h p) m -> p h m", p=128)
            load(woutb[wb][:].rearrange("p (h m) -> p h m", h=2), s, f"woutb{wb}", eng=qe,
                 reads=[gb("woutscr")])

        def omm(tt):
            def f(tt=tt):
                wo_load(0)
                for pu in range(CB // 2):
                    wb = pu % 2
                    if pu + 1 < CB // 2:
                        wo_load(pu + 1)
                    for h2 in range(2):
                        cb = 2 * pu + h2
                        src_t = yg_lo if cb < CBS else yg_hi
                        sk = "yglo" if cb < CBS else "yghi"
                        cbl = cb % CBS
                        for dh in range(2):
                            P.mm("tensor", lambda e, S, cb=cb, cbl=cbl, h2=h2, wb=wb, dh=dh, tt=tt, src_t=src_t: e.matmul(
                                    p_o[:, dh * 512:(dh + 1) * 512],
                                    src_t[:, cbl * T + tt * 128: cbl * T + (tt + 1) * 128],
                                    woutb[wb][:, h2 * DM + dh * 512: h2 * DM + (dh + 1) * 512],
                                    start=(cb == 0), stop=(cb == CB - 1)),
                                 reads=[gb(sk), gb(f"woutb{wb}")], psum=gb("p_o"),
                                 start=(cb == 0 and dh == 0), stop=(cb == CB - 1 and dh == 1))
            return f

        def ofin(tt):
            def f(tt=tt):
                row = t0 + tt * 128
                pp = tt % 2
                load(xt[pp][:], scr_xn[row:row + 128, :], f"xt{pp}", eng="gpsimd",
                     reads=[gb(f"sxn{k}_{tt}")])
                P.op("vector", lambda e, S, pp=pp: e.tensor_tensor(
                        xnt[pp][:], p_o[:], xt[pp][:], ALU.add),
                     reads=[gb("p_o"), gb(f"xt{pp}")], writes=[gb(f"xnt{pp}")])
                ln_rms_stats(xnt[pp][:], gb(f"xnt{pp}"), 7, is_rms=True)
                P.op("vector", lambda e, S, pp=pp: e.tensor_scalar_mul(
                        xnt[pp][:], xnt[pp][:], sc1[:, 7:8]),
                     reads=[gb(f"xnt{pp}"), gb("rstd7")], writes=[gb(f"xnt{pp}")])
                m = k if k < NCH // 2 else k - NCH // 2
                goff = (0 if k < NCH // 2 else T) + tt * 128
                load(g_t[:], cc_gout[m, goff:goff + 128, :], "g_raw", eng="gpsimd",
                     reads=[gb(f"ccgo{m}")])
                P.op("gpsimd", lambda e, S, pp=pp: e.tensor_tensor(
                        xnt[pp][:], xnt[pp][:], normf_rep[:], ALU.mult),
                     reads=[gb(f"xnt{pp}"), gb("normf_rep")], writes=[gb(f"xnt{pp}")])
                P.op("vector", lambda e, S, pp=pp: e.tensor_tensor(
                        xnt[pp][:], xnt[pp][:], g_t[:], ALU.mult),
                     reads=[gb(f"xnt{pp}"), gb("g_raw")], writes=[gb(f"xnt{pp}")])
                load(xt[pp][:], x_d[row:row + 128, :], f"xt{pp}", eng="gpsimd")
                P.op("vector", lambda e, S, pp=pp: e.tensor_tensor(
                        xnt[pp][:], xnt[pp][:], xt[pp][:], ALU.add),
                     reads=[gb(f"xnt{pp}"), gb(f"xt{pp}")], writes=[gb(f"xnt{pp}")])
                orow = k * T + tt * 128
                P.op("scalar", lambda e, S, pp=pp, orow=orow: e.dma_start(
                        out=out_d[orow:orow + 128, :], in_=xnt[pp][:]),
                     reads=[gb(f"xnt{pp}")], writes=[gb("outw")], dma=True)
            return f

        out.append((0, oload()))
        for tt in range(TT):
            out.append((tt * 2, omm(tt)))
            out.append((tt * 2 + 1, ofin(tt)))
        return out

    def emit_scan(k, pumpq, first_budget=0):
        full = True
        dbk = k % 2

        def jit_zdt(cb):
            pp = cb % 2
            P.mm("tensor", lambda e, S, cb=cb, dbk=dbk: e.matmul(
                    p_conv[:, 0:T], dtw_bf[:, cb * 128:(cb + 1) * 128], dtl_bf[dbk][:],
                    start=True, stop=True),
                 reads=[gb("dtw_bf"), gb(f"dtl{dbk}")], psum=gb("p_conv"),
                 start=True, stop=True)
            P.op("scalar", lambda e, S, cb=cb, pp=pp: e.activation(
                    sp_t[pp][:], p_conv[:, 0:T], AF.Exp, bias=dtb_t[:, cb:cb + 1]),
                 reads=[gb("p_conv"), gb("dtb")], writes=[gb(f"sp{pp}")])
            P.op("scalar", lambda e, S, pp=pp: e.activation(
                    sp_t[pp][:], sp_t[pp][:], AF.Ln, bias=cst[:, 1:2]),
                 reads=[gb(f"sp{pp}"), gb("cst")], writes=[gb(f"sp{pp}")])
            if full:
                pmz, pmzb = ((p_mm, gb("p_mm")) if pp == 0 else (p_mm2, gb("p_mm2")))
                wb = pp
                load(wblk[wb][:], scr_win[CB + cb, :, :], f"wblk{wb}", reads=[gb("winscr")])
                for kb in range(KB):
                    P.mm("tensor", lambda e, S, pmz=pmz, wb=wb, kb=kb, dbk=dbk: e.matmul(
                            pmz[:, 0:T], wblk[wb][:, kb * 128:(kb + 1) * 128],
                            rms_T[dbk][:, kb * T:(kb + 1) * T],
                            start=(kb == 0), stop=(kb == KB - 1)),
                         reads=[gb(f"wblk{wb}"), gb(f"rms_T{dbk}")], psum=pmzb,
                         start=(kb == 0), stop=(kb == KB - 1))
                P.op("scalar", lambda e, S, pmz=pmz, pp=pp: e.activation(
                        z_bf[pp][:], pmz[:, 0:T], AF.Silu),
                     reads=[pmzb], writes=[gb(f"z{pp}")])

        nq = len(pumpq)
        budget_per = (nq + CBS - 1) // CBS + 1

        def pump(cur_cb, budget):
            while pumpq and budget > 0 and pumpq[0][0] <= cur_cb:
                pumpq.pop(0)[1]()
                budget -= 1

        jit_zdt(0)
        for cb in range(CBS):
            pp = cb % 2
            if cb + 1 < CBS:
                jit_zdt(cb + 1)
            pump(cb, budget_per + (first_budget if cb == 0 else 0))
            if DEBUG_SCR:
                P.op("scalar", lambda e, S, cb=cb, k=k, dbk=dbk: e.dma_start(
                        out=scr_du[k, :, cb * T:(cb + 1) * T],
                        in_=u_bf[dbk][:, cb * T:(cb + 1) * T]),
                     reads=[gb(f"u{dbk}_{cb}")], writes=[gb("dbgu")], dma=True)
            P.op("vector", lambda e, S, cb=cb, pp=pp, dbk=dbk: e.tensor_tensor(
                    dtu_t[pp][:], sp_t[pp][:], u_bf[dbk][:, cb * T:(cb + 1) * T], ALU.mult),
                 reads=[gb(f"sp{pp}"), gb(f"u{dbk}_{cb}")], writes=[gb(f"dtu{pp}")])
            for g in range(NGRP):
                gp = (cb * NGRP + g) % 3
                yhi = (cb * NGRP + g) % 3
                P.op("scalar", lambda e, S, cb=cb, gp=gp, nn=g, pp=pp: e.activation(
                        a_t[gp][:], sp_t[pp][:],
                        AF.Exp, scale=acol_t[:, cb * N + nn: cb * N + nn + 1]),
                     reads=[gb(f"sp{pp}"), gb("acol")], writes=[gb(f"a{gp}")])
                bp = g % 2
                P.op("vector", lambda e, S, pp=pp, bp=bp, nn=g: e.tensor_tensor(
                        b_t[bp][:], dtu_t[pp][:],
                        brep[:, nn * T:(nn + 1) * T], ALU.mult),
                     reads=[gb(f"dtu{pp}"), gb(f"brep{g}")], writes=[gb(f"b{bp}")])
                P.op("vector", lambda e, S, cb=cb, gp=gp, bp=bp, nn=g: e.tensor_tensor_scan(
                        a_t[gp][:], a_t[gp][:], b_t[bp][:],
                        carry[:, cb * N + nn: cb * N + nn + 1], ALU.mult, ALU.add),
                     reads=[gb(f"a{gp}"), gb(f"b{bp}"), gb("carry")],
                     writes=[gb(f"a{gp}")])
                P.op("scalar", lambda e, S, cb=cb, gp=gp, nn=g: e.copy(
                        carry[:, cb * N + nn: cb * N + nn + 1], a_t[gp][:, T - 1:T]),
                     reads=[gb(f"a{gp}")], writes=[gb("carry")])
                if full:
                    P.op("gpsimd", lambda e, S, gp=gp, nn=g, yhi=yhi: e.tensor_tensor(
                            yh_t[yhi][:], a_t[gp][:],
                            crep[:, nn * T:(nn + 1) * T], ALU.mult),
                         reads=[gb(f"a{gp}"), gb(f"crep{g}")], writes=[gb(f"yh{yhi}")])
                    P.mm("tensor", lambda e, S, cb=cb, yhi=yhi, g=g: e.matmul(
                            p_ys[cb % 2][:, 0:T], eye_bf[:], yh_t[yhi][:],
                            start=(g == 0), stop=(g == NGRP - 1)),
                         reads=[gb(f"yh{yhi}")], psum=gb(f"p_ys{cb % 2}"),
                         start=(g == 0), stop=(g == NGRP - 1))
            if full:
                P.op("vector", lambda e, S, cb=cb, pp=pp, dbk=dbk: e.scalar_tensor_tensor(
                        dtu_t[pp][:], u_bf[dbk][:, cb * T:(cb + 1) * T], dvec_t[:, cb:cb + 1],
                        p_ys[cb % 2][:, 0:T], ALU.mult, ALU.add),
                     reads=[gb(f"u{dbk}_{cb}"), gb("dvec"), gb(f"p_ys{cb % 2}")],
                     writes=[gb(f"dtu{pp}")])
                P.op("gpsimd", lambda e, S, cb=cb, pp=pp: e.tensor_tensor(
                        yg_bf[:, cb * T:(cb + 1) * T], dtu_t[pp][:], z_bf[pp][:], ALU.mult),
                     reads=[gb(f"dtu{pp}"), gb(f"z{pp}")], writes=[gb(f"yg{cb}")])
                if DEBUG_SCR:
                    P.op("scalar", lambda e, S, cb=cb, k=k: e.dma_start(
                            out=scr_dyg[k, :, cb * T:(cb + 1) * T],
                            in_=yg_bf[:, cb * T:(cb + 1) * T]),
                         reads=[gb(f"yg{cb}")], writes=[gb("dbgyg")], dma=True)
        pump(CBS + CB, 10 ** 9)
        P.op("scalar", lambda e, S, k=k: e.dma_start(out=cc_yin[k], in_=yg_bf[:]),
             reads=[gb(f"yg{cb}") for cb in range(CBS)], writes=[gb(f"ccyi{k}")], dma=True)
        P.op("gpsimd", lambda e, S, k=k: e.collective_compute(
                "AllGather", ALU.bypass, ins=[cc_yin[k]], outs=[cc_yout[k]],
                replica_groups=RG),
             reads=[gb(f"ccyi{k}")], writes=[gb(f"ccyo{k}")], sem_override="cC")

    precast_tasks = []
    def precast_to_tasks(fn_emit_iters):
        for c in fn_emit_iters:
            precast_tasks.append((0, c))

    def gen_precast_big(srcd, dst, rows, cols, skey, i32):
        icols = (CBS * T) // 2
        CW = min(512, icols // 2)
        RING = icols // CW
        ivw = yg_lo[:].bitcast(I32 if i32 else F32)
        tiled3 = len(srcd.shape) == 3
        if tiled3:
            blocks = [(t, cc) for t in range(srcd.shape[0])
                      for cc in range(srcd.shape[2] // CW)]
        else:
            blocks = [(rb, cc) for rb in range(rows // 128) for cc in range(cols // CW)]
        nb = len(blocks)
        LA = min(3, RING - 1)

        def emit_load(i):
            rb, cc = blocks[i]
            r = i % RING
            qe = "sync" if i % 2 == 0 else "scalar"
            s = (srcd[rb, :, cc * CW:cc * CW + CW] if tiled3 else
                 srcd[rb * 128:(rb + 1) * 128, cc * CW:cc * CW + CW])
            load(ivw[:, r * CW:(r + 1) * CW], s, f"pci{r}", eng=qe)

        for bi in range(nb):
            def one(bi=bi):
                if bi == 0:
                    for jj in range(min(LA, nb)):
                        emit_load(jj)
                elif bi + LA - 1 < nb:
                    emit_load(bi + LA - 1)
                rb, cc = blocks[bi]
                c0 = cc * CW
                r = bi % RING
                P.op("vector", lambda e, S, r=r: e.tensor_copy(
                        yg_hi[:, r * CW:(r + 1) * CW], ivw[:, r * CW:(r + 1) * CW]),
                     reads=[gb(f"pci{r}")], writes=[gb(f"pco{r}")])
                qe = "scalar" if bi % 2 == 0 else "sync"
                dd = (dst[rb, :, c0:c0 + CW] if tiled3 else
                      dst[rb * 128:(rb + 1) * 128, c0:c0 + CW])
                P.op(qe, lambda e, S, dd=dd, r=r: e.dma_start(
                        out=dd, in_=yg_hi[:, r * CW:(r + 1) * CW]),
                     reads=[gb(f"pco{r}")], writes=[gb(skey + "scr")], dma=True)
            yield one

    precast_to_tasks(gen_precast_big(wout_d, scr_wout, DI, DM, "wout", False))

    emit_ln(0)
    fq0 = [f for _, f in ffn_closures(0)]
    n_head = TT + (2 if TT >= 2 else 1) * HB
    fq0_head, fq0_tail = fq0[:n_head], fq0[n_head:]
    for _, f in x_closures(0):
        f()
        for _ in range(3):
            if fq0_head:
                fq0_head.pop(0)()
    for f in fq0_head:
        f()
    for k in range(NCH):
        q = []
        q += o_closures(k - 2)
        if k == 1:
            nfq = len(fq0_tail)
            q += [((i * (CBS - 2)) // max(nfq, 1), f) for i, f in enumerate(fq0_tail)]
        else:
            q += ffn_closures(k - 1)
        lnA, _ = ln_closures(k + 1) if k + 1 < NCH else ([], [])
        q += lnA
        if k + 1 < NCH:
            q += x_closures(k + 1)
        if k == 0:
            q += precast_tasks
            del precast_tasks[:]
        emit_scan(k, q, first_budget=2)
        m = k - 1
        if 0 <= m < LH // T:
            P.op("gpsimd", lambda e, S, m=m: e.collective_compute(
                    "AllGather", ALU.bypass, ins=[cc_gin[m]], outs=[cc_gout[m]],
                    replica_groups=RG),
                 reads=[gb(f"sg{m}_{tt}") for tt in range(TT)], writes=[gb(f"ccgo{m}")],
                 sem_override="cC")
    for _, f in ffn_closures(NCH - 1):
        f()
    m = NCH - 1
    if m < LH // T:
        P.op("gpsimd", lambda e, S, m=m: e.collective_compute(
                "AllGather", ALU.bypass, ins=[cc_gin[m]], outs=[cc_gout[m]],
                replica_groups=RG),
             reads=[gb(f"sg{m}_{tt}") for tt in range(TT)], writes=[gb(f"ccgo{m}")],
             sem_override="cC")
    for _, f in o_closures(NCH - 2):
        f()
    for _, f in o_closures(NCH - 1):
        f()

    P.raw_wait("gpsimd", gb("outw").w)
    P.flush_queues("gpsimd")

    sems = {}
    sem_ctx = ExitStack()
    for s in ALL_SEMS:
        sems[s] = sem_ctx.enter_context(nc.semaphore(s))
    P.finalize_into(nc, sems)
    ctx.close()
    sem_ctx.close()
    return nc, dict(L=L, T=T, NCH=NCH, NCH_PRE=NCH_PRE, L_OWN=L_OWN, ninstr=P.ninstr)


def make_core_inputs(core_id, inputs, L=4096, NCH=8, NCH_PRE=0):
    b, j = core_id // 2, core_id % 2
    DM, DI, N = 1024, 2048, 16
    DIH = DI // 2
    CB, CBS = 16, 8
    T = L // NCH
    LH = L // 2
    x = np.asarray(inputs["L0_attn_x"][b], dtype=np.float32)
    own = slice(j * DIH, (j + 1) * DIH)
    oth = slice((1 - j) * DIH, (2 - j) * DIH)
    perm = np.r_[np.arange(j * DIH, (j + 1) * DIH), np.arange((1 - j) * DIH, (2 - j) * DIH)]
    ipw = np.asarray(inputs["L0_attn_in_proj_w"], np.float32)
    w_in = np.concatenate([ipw[perm, :], ipw[2048 + j * DIH: 2048 + (j + 1) * DIH, :]], axis=0).T
    cw = np.asarray(inputs["L0_attn_conv_w"], np.float32)[perm]
    cb_ = np.asarray(inputs["L0_attn_conv_b"], np.float32)[perm]
    xpw = np.asarray(inputs["L0_attn_x_proj_w"], np.float32)[:, perm]
    dtw = np.asarray(inputs["L0_attn_dt_proj_w"], np.float32)[own]
    dtb = np.asarray(inputs["L0_attn_dt_proj_b"], np.float32)[own]
    alog = np.asarray(inputs["L0_attn_A_log"], np.float32)[own]
    dvec = np.asarray(inputs["L0_attn_D"], np.float32)[own]
    d = {
        "x": np.ascontiguousarray(x),
        "xf": np.ascontiguousarray(x[j * LH:(j + 1) * LH]),
        "mask": np.ones((128, NCH), np.float32),
        "w_in": np.ascontiguousarray(w_in),
        "norm_w": np.ascontiguousarray(np.asarray(inputs["L0_attn_norm_w"], np.float32).reshape(8, 128).T),
        "conv_w_r": np.ascontiguousarray(cw.reshape(CB, 128, 4).transpose(1, 0, 2).reshape(128, CB * 4)),
        "conv_b_r": np.ascontiguousarray(cb_.reshape(CB, 128).T),
        "xproj_T": np.ascontiguousarray(xpw.T),
        "dtw_T": np.ascontiguousarray(dtw.T),
        "dtb_r": np.ascontiguousarray(dtb.reshape(CBS, 128).T),
        "alog_r": np.ascontiguousarray(alog.reshape(CBS, 128, N).transpose(1, 0, 2).reshape(128, CBS * N)),
        "d_r": np.ascontiguousarray(dvec.reshape(CBS, 128).T),
        "wout_T": np.ascontiguousarray(np.asarray(inputs["L0_attn_out_proj_w"], np.float32).T),
        "norm_f": np.ascontiguousarray(np.asarray(inputs["L0_attn_norm_f_w"], np.float32).reshape(1, DM)),
        "wg_T": _tile_ffn(np.asarray(inputs["L0_ffn_w_gate"], np.int32).T.astype(_BF)),
        "wu_T": _tile_ffn(np.asarray(inputs["L0_ffn_w_up"], np.int32).T.astype(_BF)),
        "wd_T": np.ascontiguousarray(np.asarray(inputs["L0_ffn_w_down"], np.int32).T.astype(_BF)),
        "eye": np.eye(128, dtype=np.float32),
    }
    return d


def _tile_ffn(wT):
    DM, HID = wT.shape
    return np.ascontiguousarray(
        wT.reshape(DM // 128, 128, HID // 128, 128).transpose(2, 1, 0, 3)
          .reshape(HID // 128, 128, DM))


def assemble_output(results, L=4096, NCH=8, NCH_PRE=0, B=4):
    DM = 1024
    LH = L // 2
    out = np.empty((B, L, DM), np.float32)
    for core_id in range(2 * B):
        b, j = core_id // 2, core_id % 2
        o = results[core_id]["out"]
        out[b, j * LH:(j + 1) * LH] = o[j * LH:(j + 1) * LH]
    return out


from concourse.bass_utils import run_bass_kernel_spmd

_BUILT = {}

def _get_program():
    if "nc" not in _BUILT:
        nc, meta = build_program(L=4096, T=512)
        _BUILT["nc"] = nc
        _BUILT["meta"] = meta
    return _BUILT["nc"], _BUILT["meta"]


def kernel(**inputs):
    nc, meta = _get_program()
    in_maps = [make_core_inputs(c, inputs, L=meta["L"], NCH=meta["NCH"],
                                NCH_PRE=meta["NCH_PRE"]) for c in range(8)]
    res = run_bass_kernel_spmd(nc, in_maps, core_ids=list(range(8)))
    return assemble_output(list(res.results), L=meta["L"], NCH=meta["NCH"],
                           NCH_PRE=meta["NCH_PRE"], B=4)


# version 32
# speedup vs baseline: 1.1381x; 1.0280x over previous
import numpy as np
import ml_dtypes
_BF = ml_dtypes.bfloat16
from contextlib import ExitStack
import concourse.bass as bass
import concourse.mybir as mybir

F32 = mybir.dt.float32
BF16 = mybir.dt.bfloat16
I32 = mybir.dt.int32
AF = mybir.ActivationFunctionType
ALU = mybir.AluOpType

ENGINES = ("sync", "scalar", "vector", "tensor", "gpsimd")
CSEM = {"scalar": "cS", "vector": "cV", "tensor": "cT", "gpsimd": "cG"}
DQ = {"sync": tuple(f"dS{i}" for i in range(8)),
      "scalar": tuple(f"dA{i}" for i in range(8)),
      "gpsimd": tuple(f"dG{i}" for i in range(4))}
ALL_SEMS = ("cS", "cV", "cT", "cG", "cC") + DQ["sync"] + DQ["scalar"] + DQ["gpsimd"]
BATCH_MAX = 2


class Ev:
    __slots__ = ("sem", "val")
    def __init__(self, sem, val):
        self.sem = sem
        self.val = val


class DmaEv:
    __slots__ = ("q", "batch")
    def __init__(self, q, batch):
        self.q = q
        self.batch = batch


class PendEv:
    __slots__ = ("eng", "ev")
    def __init__(self, eng):
        self.eng = eng
        self.ev = None


class Buf:
    __slots__ = ("w", "reads")
    def __init__(self):
        self.w = None
        self.reads = []


class Prog:
    def __init__(self):
        self.ops = {e: [] for e in ENGINES}
        self.ctr = {s: 0 for s in ALL_SEMS}
        self.waited = {e: {s: 0 for s in ALL_SEMS} for e in ENGINES}
        self.ninstr = 0
        self.q = {e: dict(nopen=0, nbatch=0, open=None) for e in DQ}
        self.pending = {e: [] for e in ENGINES}

    def _close_batch(self, qeng, which=None):
        st = self.q[qeng]
        if st["open"] is not None:
            batch = st["open"]
            batch[0] = True
            batch[2] = self.ctr[batch[1]]
            st["open"] = None
            st["nopen"] = 0

    def _resolve_pending(self, eng2):
        sem = CSEM[eng2]
        self.ctr[sem] += 1
        val = self.ctr[sem]
        self.ops[eng2].append(lambda e, S, _s=sem: e.nop().then_inc(S[_s], 1))
        self.ninstr += 1
        ev = Ev(sem, val)
        for p in self.pending[eng2]:
            p.ev = ev
        self.pending[eng2] = []
        return ev

    def _wait(self, eng, ev, raw=True):
        if ev is None:
            return
        if isinstance(ev, PendEv):
            if ev.ev is None:
                self._resolve_pending(ev.eng)
            ev = ev.ev
        if isinstance(ev, DmaEv):
            if not ev.batch[0]:
                self._close_batch(ev.q)
            ev = Ev(ev.batch[1], ev.batch[2])
        if self.waited[eng].get(ev.sem, 0) >= ev.val:
            return
        self.waited[eng][ev.sem] = ev.val
        sem_name, val = ev.sem, ev.val
        self.ops[eng].append(lambda e, S, _s=sem_name, _v=val: e.wait_ge(S[_s], _v))
        self.ninstr += 1

    def op(self, eng, fn, reads=(), writes=(), war=(), dma=False, track=True,
           sem_override=None):
        for b in reads:
            self._wait(eng, b.w, raw=True)
        for b in war:
            self._wait(eng, b.w, raw=False)
            for r in b.reads:
                self._wait(eng, r, raw=False)
        for b in writes:
            self._wait(eng, b.w, raw=False)
            for r in b.reads:
                self._wait(eng, r, raw=False)
        if dma:
            st = self.q[eng]
            if st["open"] is None or st["nopen"] >= BATCH_MAX:
                self._close_batch(eng)
                pool = DQ[eng]
                psem = pool[st["nbatch"] % len(pool)]
                st["nbatch"] += 1
                self._wait(eng, Ev(psem, self.ctr[psem]))
                st["open"] = [False, psem, 0]
            batch = st["open"]
            psem = batch[1]
            self.ctr[psem] += 16
            st["nopen"] += 1
            def run(e, S, fn=fn, psem=psem):
                fn(e, S).then_inc(S[psem], 16)
            ev = DmaEv(eng, batch)
            self.ops[eng].append(run)
            self.ninstr += 1
            for b in writes:
                b.w = ev
                b.reads = []
            for b in reads:
                b.reads.append(ev)
            return ev
        if track:
            sem = sem_override or CSEM[eng]
            self.ctr[sem] += 1
            val = self.ctr[sem]
            def run(e, S, fn=fn, sem=sem):
                fn(e, S).then_inc(S[sem], 1)
            ev = Ev(sem, val)
            for p in self.pending[eng]:
                p.ev = ev
            self.pending[eng] = []
        else:
            def run(e, S, fn=fn):
                fn(e, S)
            ev = None
        self.ops[eng].append(run)
        self.ninstr += 1
        if track:
            for b in writes:
                b.w = ev
                b.reads = []
            for b in reads:
                b.reads.append(ev)
        else:
            pe = PendEv(eng)
            self.pending[eng].append(pe)
            for b in reads:
                b.reads.append(pe)
        return ev

    def mm(self, eng, fn, reads=(), psum=None, start=False, stop=False):
        if start and stop:
            return self.op(eng, fn, reads=reads, writes=[psum])
        if start:
            return self.op(eng, fn, reads=reads, war=[psum])
        if stop:
            return self.op(eng, fn, reads=reads, writes=[psum])
        return self.op(eng, fn, reads=reads)

    def raw_wait(self, eng, ev):
        self._wait(eng, ev)

    def flush_queues(self, eng):
        for qeng in DQ:
            self._close_batch(qeng)
            for psem in DQ[qeng]:
                self._wait(eng, Ev(psem, self.ctr[psem]))

    def finalize_into(self, nc, sems):
        with nc.Block() as block:
            for eng in ENGINES:
                ops = self.ops[eng]
                if not ops:
                    continue
                getattr(block, eng)(_mkbody(ops, sems))


def _mkbody(ops, sems):
    def body(e):
        for fn in ops:
            fn(e, sems)
    return body


DEBUG_SCR = False

def build_program(L=4096, T=512, NCH_PRE=None):
    DM, DI, N, RANK, HID = 1024, 2048, 16, 128, 3072
    NCH = L // T
    NCH_PRE = 0
    L_OWN = L
    DIH = DI // 2
    CBS = DIH // 128
    LH = L // 2
    RG = [[0, 1], [2, 3], [4, 5], [6, 7]]
    KB = DM // 128
    CB = DI // 128
    HB = HID // 128
    TT = T // 128
    NG = 1
    NGRP = N // NG
    TF = T // 2 if T >= 256 else T
    EPS = 1e-5
    QINV = float(np.float32(1.0 / 127.0))
    NCO = (DI + DIH) // 128

    nc = bass.Bass("TRN2", target_bir_lowering=False, debug=False)

    def din(name, shape, dt):
        return nc.dram_tensor(name, list(shape), dt, kind="ExternalInput").ap()

    x_d = din("x", (L, DM), F32)
    mask_d = din("mask", (128, NCH), F32)
    w_in_d = din("w_in", (DI + DIH, DM), F32)
    norm_w_row_d = din("norm_w_row", (1, DM), F32)
    norm_w_d = din("norm_w", (128, KB), F32)
    conv_w_d = din("conv_w_r", (128, CB * 4), F32)
    conv_b_d = din("conv_b_r", (128, CB), F32)
    xproj_d = din("xproj_T", (DI, 160), F32)
    dtw_d = din("dtw_T", (RANK, DIH), F32)
    dtb_d = din("dtb_r", (128, CBS), F32)
    alog_d = din("alog_r", (128, CBS * N), F32)
    dvec_d = din("d_r", (128, CBS), F32)
    wout_d = din("wout_T", (DI, DM), F32)
    normf_d = din("norm_f", (1, DM), F32)
    wg_d = din("wg_T", (HB, 128, KB * 128), BF16)
    wu_d = din("wu_T", (HB, 128, KB * 128), BF16)
    wd_d = din("wd_T", (HID, DM), BF16)
    eye_d = din("eye", (128, 128), F32)
    xf_d = din("xf", (LH, DM), F32)

    out_d = nc.dram_tensor("out", [L_OWN, DM], F32, kind="ExternalOutput").ap()

    def scr(name, shape, dt):
        return nc.dram_tensor(name, list(shape), dt, kind="Internal").ap()

    scr_win = scr("scr_win", (NCO, 128, KB * 128), BF16)
    scr_wout = scr("scr_wout", (DI, DM), BF16)
    scr_bc = scr("scr_bc", (2, 32, T), BF16)
    scr_xn = scr("scr_xn", (L, DM), F32)
    scr_nf = scr("scr_nf", (1, DM), F32)
    scr_nw = scr("scr_nw", (1, DM), F32)
    cc_yin = scr("cc_yin", (NCH, 128, CBS * T), BF16)
    cc_yout = scr("cc_yout", (NCH, 256, CBS * T), BF16)
    cc_gin = scr("cc_gin", (NCH // 2, T, DM), BF16)
    cc_gout = scr("cc_gout", (NCH // 2, 2 * T, DM), BF16)
    scr_du = scr("scr_du", (NCH, 128, CB * T), BF16) if DEBUG_SCR else None
    scr_dyg = scr("scr_dyg", (NCH, 128, CB * T), BF16) if DEBUG_SCR else None

    ctx = ExitStack()
    sb = lambda n, s, d: ctx.enter_context(nc.sbuf_tensor(n, list(s), d))
    psx = lambda n, s, d: ctx.enter_context(nc.psum_tensor(n, list(s), d))

    eye_bf = sb("eye_bf", (128, 128), BF16)
    norm_w_t = sb("norm_w_t", (128, KB), F32)
    conv_w_t = sb("conv_w_t", (128, CB * 4), F32)
    conv_b_t = sb("conv_b_t", (128, CB), F32)
    dtb_t = sb("dtb_t", (128, CBS), F32)
    acol_t = sb("acol_t", (128, CBS * N), F32)
    dvec_t = sb("dvec_t", (128, CBS), F32)
    mask_t = sb("mask_t", (128, NCH), F32)
    normf_rep = sb("normf_rep", (128, DM), BF16)
    normw_rep = sb("normw_rep", (128, DM), BF16)
    carry = sb("carry", (128, CBS * N), F32)
    xpj = sb("xpj", (128, CB * 160), BF16)
    dtw_bf = sb("dtw_bf", (128, DIH), BF16)
    stat6 = sb("stat6", (128, 12), F32)
    stat2 = sb("stat2", (128, 2), F32)
    sc1 = sb("sc1", (128, 8), F32)
    cst = sb("cst", (128, 2), F32)
    xt = [sb(f"xt{i}", (128, DM), F32) for i in range(2)]
    xnt = [sb(f"xnt{i}", (128, DM), F32) for i in range(2)]
    rms_bf = [sb(f"rmsbf{i}", (128, DM), BF16) for i in range(1)]
    rms_T = [sb(f"rms_T{i}", (128, KB * T), BF16) for i in range(2)]
    xn_T = sb("xn_T", (128, KB * T), BF16)
    xbr = [sb(f"xbr{i}", (128, T + 4), BF16) for i in range(2)]
    halo_t = sb("halo_t", (128, CB * 4), BF16)
    u_bf = [sb(f"u_bf{i}", (128, CBS * T), BF16) for i in range(2)]
    u_oth = sb("u_oth", (128, (CB - CBS) * T), BF16)
    z_bf = [sb(f"zbf{i}", (128, T), BF16) for i in range(2)]
    dtl_bf = [sb(f"dtl{i}", (128, T), BF16) for i in range(2)]
    sp_t = [sb(f"sp{i}", (128, T), F32) for i in range(2)]
    dtu_t = [sb(f"dtu{i}", (128, T), BF16) for i in range(2)]
    a_t = [sb(f"a{i}", (128, NG * T), F32) for i in range(3)]
    b_t = [sb(f"b{i}", (128, NG * T), BF16) for i in range(2)]
    yh_t = [sb(f"yh{i}", (128, NG * T), BF16) for i in range(3)]
    brep = sb("brep", (128, N * T), BF16)
    crep = sb("crep", (128, N * T), BF16)
    bc_sb = sb("bc_sb", (32, T), BF16)
    yg_bf = sb("yg_bf", (128, CBS * T), BF16)
    yg_lo = sb("yg_lo", (128, CBS * T), BF16)
    yg_hi = sb("yg_hi", (128, CBS * T), BF16)
    diag_t = [sb("diag0", (128, 4 * 128), BF16)]
    wblk = [sb(f"wblk{i}", (128, KB * 128), BF16) for i in range(2)]
    wffn = [sb(f"wffn{i}", (128, KB * 128), BF16) for i in range(3)]
    wffn2 = [sb(f"wffn2{i}", (128, KB * 128), BF16) for i in range(3)]
    wdn = [sb(f"wdn{i}", (128, 2 * DM), BF16) for i in range(2)]
    woutb = [sb(f"woutb{i}", (128, 2 * DM), BF16) for i in range(2)]
    gu_s = [sb(f"gu_s{i}", (128, TF), BF16) for i in range(2)]
    hffn = sb("hffn", (128, HB * TF), BF16)
    g_t = sb("g_t", (128, DM), BF16)
    castf_t = sp_t[0] if T >= 256 else sb("cast_fs", (128, 256), F32)
    castf_key = "sp0" if T >= 256 else "cast_fs"

    p_mm = psx("p_mm", (128, 512), F32)
    p_mm2 = psx("p_mm2", (128, 512), F32)
    p_conv = psx("p_conv", (128, 512), F32)
    p_ys = [psx(f"p_ys{i}", (128, 512), F32) for i in range(2)]
    p_tr = psx("p_tr", (128, 1024), BF16)
    p_o = psx("p_o", (128, 1024), F32)

    bufs = {}
    def gb(key):
        if key not in bufs:
            bufs[key] = Buf()
        return bufs[key]

    P = Prog()

    def load(dst_ap, src_ap, key=None, eng="sync", reads=(), writes=(), war=()):
        w = list(writes)
        if key is not None:
            w.append(gb(key))
        return P.op(eng, lambda e, S, d=dst_ap, s=src_ap: e.dma_start(out=d, in_=s),
                    reads=list(reads), writes=w, war=list(war), dma=True)

    load(xt[0][:, 0:128], eye_d[:], "xt0")
    P.op("vector", lambda e, S: e.tensor_copy(eye_bf[:], xt[0][:, 0:128]),
         reads=[gb("xt0")], writes=[gb("eye_bf")])
    load(norm_w_t[:], norm_w_d[:], "norm_w")
    load(conv_w_t[:], conv_w_d[:], "conv_w")
    load(conv_b_t[:], conv_b_d[:], "conv_b")
    load(dtb_t[:], dtb_d[:], "dtb")
    load(acol_t[:], alog_d[:], "acol")
    load(dvec_t[:], dvec_d[:], "dvec")
    load(mask_t[:], mask_d[:], "mask")
    load(scr_nf[:], normf_d[:], "scr_nf", eng="scalar")
    load(xnt[0][:], scr_nf[:].broadcast_to((128, DM)), "xnt0", eng="scalar",
         reads=[gb("scr_nf")])
    P.op("vector", lambda e, S: e.tensor_copy(normf_rep[:], xnt[0][:]),
         reads=[gb("xnt0")], writes=[gb("normf_rep")])
    P.op("scalar", lambda e, S: e.activation(acol_t[:], acol_t[:], AF.Exp),
         reads=[gb("acol")], writes=[gb("acol")])
    P.op("vector", lambda e, S: e.tensor_scalar_mul(acol_t[:], acol_t[:], -1.0),
         reads=[gb("acol")], writes=[gb("acol")])
    for cb in range(CB):
        load(castf_t[:, 0:160], xproj_d[cb * 128:(cb + 1) * 128, :], castf_key)
        P.op("vector", lambda e, S, cb=cb: e.tensor_copy(xpj[:, cb * 160:(cb + 1) * 160],
                                                         castf_t[:, 0:160]),
             reads=[gb(castf_key)], writes=[gb("xpj")])
    for j in range(DIH // 256):
        load(castf_t[:, 0:256], dtw_d[:, j * 256:(j + 1) * 256], castf_key)
        P.op("vector", lambda e, S, j=j: e.tensor_copy(dtw_bf[:, j * 256:(j + 1) * 256], castf_t[:, 0:256]),
             reads=[gb(castf_key)], writes=[gb("dtw_bf")])

    def precast_wide(srcd, dst, rows, cols, skey, tiled, fold=False):
        ring = [(xt[0], "xt0"), (xt[1], "xt1"), (xnt[1], "xnt1")]
        CWW = min(DM, cols)
        it = 0
        for rb in range(rows // 128):
            for cc in range(cols // CWW):
                c0 = cc * CWW
                tile, tkey = ring[it % 3]
                it += 1
                load(tile[:, 0:CWW], srcd[rb * 128:(rb + 1) * 128, c0:c0 + CWW], tkey)
                btile, bkey = ((rms_bf[0], "rmsbf0") if it % 2 == 0 else (g_t, "g_raw"))
                if fold:
                    P.op("vector", lambda e, S, rb=rb, tile=tile, btile=btile, CWW=CWW: e.tensor_scalar(
                            btile[:, 0:CWW], tile[:, 0:CWW], norm_w_t[:, rb:rb + 1], None, ALU.mult),
                         reads=[gb(tkey), gb("norm_w")], writes=[gb(bkey)])
                else:
                    P.op("vector", lambda e, S, tile=tile, btile=btile, CWW=CWW: e.tensor_copy(
                            btile[:, 0:CWW], tile[:, 0:CWW]),
                         reads=[gb(tkey)], writes=[gb(bkey)])
                if tiled:
                    for q in range(CWW // 128):
                        co = (c0 + q * 128) // 128
                        P.op("scalar", lambda e, S, co=co, rb=rb, q=q, btile=btile, dst=dst: e.dma_start(
                                out=dst[co, :, rb * 128:(rb + 1) * 128],
                                in_=btile[:, q * 128:(q + 1) * 128]),
                             reads=[gb(bkey)], writes=[gb(skey + "scr")], dma=True)
                else:
                    P.op("scalar", lambda e, S, rb=rb, c0=c0, btile=btile, dst=dst, CWW=CWW: e.dma_start(
                            out=dst[rb * 128:(rb + 1) * 128, c0:c0 + CWW],
                            in_=btile[:, 0:CWW]),
                         reads=[gb(bkey)], writes=[gb(skey + "scr")], dma=True)

    load(scr_nw[:], norm_w_row_d[:], "scr_nw2", eng="scalar")
    load(xnt[1][:], scr_nw[:].broadcast_to((128, DM)), "xnt1", eng="scalar",
         reads=[gb("scr_nw2")])
    P.op("vector", lambda e, S: e.tensor_copy(normw_rep[:], xnt[1][:]),
         reads=[gb("xnt1")], writes=[gb("normw_rep")])
    for _co in range(NCO):
        _pp = _co % 2
        load(xt[_pp][:], w_in_d[_co * 128:(_co + 1) * 128, :], f"xt{_pp}")
        P.op("vector", lambda e, S, _pp=_pp: e.tensor_tensor(
                rms_bf[0][:], xt[_pp][:], normw_rep[:], ALU.mult),
             reads=[gb(f"xt{_pp}"), gb("normw_rep")], writes=[gb("rmsbf0")])
        for _kb in range(KB):
            P.mm("tensor", lambda e, S, _kb=_kb: e.transpose(
                    p_tr[:, _kb * 128:(_kb + 1) * 128],
                    rms_bf[0][:, _kb * 128:(_kb + 1) * 128], eye_bf[:]),
                 reads=[gb("rmsbf0"), gb("eye_bf")], psum=gb("p_tr"),
                 start=(_kb == 0), stop=(_kb == KB - 1))
        P.op("scalar", lambda e, S: e.copy(g_t[:], p_tr[:]),
             reads=[gb("p_tr")], writes=[gb("g_raw")])
        P.op("scalar" if _co % 2 == 0 else "sync",
             lambda e, S, _co=_co: e.dma_start(out=scr_win[_co, :, :], in_=g_t[:]),
             reads=[gb("g_raw")], writes=[gb("winscr")], dma=True)


    P.op("vector", lambda e, S: e.memset(cst[:, 0:1], 1e-5), writes=[gb("cst")])
    P.op("vector", lambda e, S: e.memset(cst[:, 1:2], 1.0), writes=[gb("cst")])
    P.op("vector", lambda e, S: e.memset(carry[:], 0.0), writes=[gb("carry")])
    _evx = P.op("vector", lambda e, S: e.memset(halo_t[:], 0.0), writes=[gb("halo_all")])
    for _cb in range(CB):
        gb(f"halo{_cb}").w = _evx

    def ln_rms_stats(src_ap, src_buf, out_col, is_rms):
        P.op("vector", lambda e, S: e.bn_stats(stat6[:, 0:6], src_ap[:, 0:512]),
             reads=[src_buf], writes=[gb("st6")])
        P.op("vector", lambda e, S: e.bn_stats(stat6[:, 6:12], src_ap[:, 512:1024]),
             reads=[src_buf], writes=[gb("st6")])
        P.op("vector", lambda e, S: e.bn_aggr(stat2[:], stat6[:]),
             reads=[gb("st6")], writes=[gb("st2")])
        if is_rms:
            P.op("vector", lambda e, S: e.tensor_tensor(sc1[:, 3:4], stat2[:, 0:1],
                                                        stat2[:, 0:1], ALU.mult),
                 reads=[gb("st2")], writes=[gb("m2")])
            P.op("vector", lambda e, S: e.tensor_tensor(sc1[:, 4:5], sc1[:, 3:4],
                                                        stat2[:, 1:2], ALU.add),
                 reads=[gb("m2"), gb("st2")], writes=[gb("ms")])
            P.op("scalar", lambda e, S: e.activation(sc1[:, 5:6], sc1[:, 4:5], AF.Ln, bias=cst[:, 0:1]),
                 reads=[gb("ms"), gb("cst")], writes=[gb("sd")])
        else:
            P.op("vector", lambda e, S: e.tensor_scalar_mul(
                    sc1[:, 1:2], stat2[:, 0:1], -1.0),
                 reads=[gb("st2")], writes=[gb("negm")])
            P.op("scalar", lambda e, S: e.activation(sc1[:, 5:6], stat2[:, 1:2], AF.Ln, bias=cst[:, 0:1]),
                 reads=[gb("st2"), gb("cst")], writes=[gb("sd")])
        P.op("scalar", lambda e, S, c=out_col: e.activation(sc1[:, c:c + 1], sc1[:, 5:6],
                                                            AF.Exp, scale=-0.5),
             reads=[gb("sd")], writes=[gb(f"rstd{out_col}")])


    def ln_closures(k):
        t0 = k * T
        dbk = k % 2
        a = [(0, _ln_tt(k, tt, True, t0, dbk)) for tt in range(TT)]
        return a, []

    def emit_ln(k):
        a, _ = ln_closures(k)
        for _, f in a:
            f()

    def _ln_tt(k, tt, full, t0, dbk):
      def f(tt=tt):
              row = t0 + tt * 128
              pp = tt % 2
              bx = gb(f"xt{pp}")
              load(xt[pp][:], x_d[row:row + 128, :], f"xt{pp}")
              ln_rms_stats(xt[pp][:], bx, 2, is_rms=False)
              P.op("vector", lambda e, S, pp=pp: e.tensor_scalar(
                      xnt[pp][:], xt[pp][:], sc1[:, 1:2], sc1[:, 2:3], ALU.add, ALU.mult),
                   reads=[bx, gb("negm"), gb("rstd2")], writes=[gb(f"xnt{pp}")])
              if full:
                  P.op("scalar", lambda e, S, pp=pp, row=row: e.dma_start(
                          out=scr_xn[row:row + 128, :], in_=xnt[pp][:]),
                       reads=[gb(f"xnt{pp}")], writes=[gb(f"sxn{k}_{tt}")], dma=True)
              ln_rms_stats(xnt[pp][:], gb(f"xnt{pp}"), 6, is_rms=True)
              P.op("vector", lambda e, S, pp=pp: e.tensor_scalar_mul(
                      rms_bf[0][:], xnt[pp][:], sc1[:, 6:7]),
                   reads=[gb(f"xnt{pp}"), gb("rstd6")], writes=[gb("rmsbf0")])
              for kb in range(KB):
                  P.mm("tensor", lambda e, S, pp=pp, kb=kb: e.transpose(
                          p_tr[:, kb * 128:(kb + 1) * 128],
                          rms_bf[0][:, kb * 128:(kb + 1) * 128], eye_bf[:]),
                       reads=[gb("rmsbf0"), gb("eye_bf")], psum=gb("p_tr"),
                       start=(kb == 0), stop=(kb == KB - 1))
              P.op("scalar", lambda e, S, tt=tt, dbk=dbk: e.copy(
                      rms_T[dbk][:].rearrange("p (kb t) -> p kb t", kb=KB)[:, :, tt * 128:(tt + 1) * 128],
                      p_tr[:].rearrange("p (kb c) -> p kb c", kb=KB)),
                   reads=[gb("p_tr")], writes=[gb(f"rms_T{dbk}")])
      return f

    def _ln_xnT(m, tt):
      def f(tt=tt):
          row = m * T + tt * 128
          pp = tt % 2
          load(xt[pp][:], xf_d[row:row + 128, :], f"xt{pp}", eng="gpsimd")
          ln_rms_stats(xt[pp][:], gb(f"xt{pp}"), 2, is_rms=False)
          P.op("vector", lambda e, S, pp=pp: e.tensor_scalar(
                  xnt[pp][:], xt[pp][:], sc1[:, 1:2], sc1[:, 2:3], ALU.add, ALU.mult),
               reads=[gb(f"xt{pp}"), gb("negm"), gb("rstd2")], writes=[gb(f"xnt{pp}")])
          P.op("vector", lambda e, S, pp=pp: e.tensor_copy(rms_bf[0][:], xnt[pp][:]),
               reads=[gb(f"xnt{pp}")], writes=[gb("rmsbf0")])
          for kb in range(KB):
              P.mm("tensor", lambda e, S, pp=pp, kb=kb: e.transpose(
                      p_tr[:, kb * 128:(kb + 1) * 128],
                      rms_bf[0][:, kb * 128:(kb + 1) * 128], eye_bf[:]),
                   reads=[gb("rmsbf0"), gb("eye_bf")], psum=gb("p_tr"),
                   start=(kb == 0), stop=(kb == KB - 1))
          P.op("scalar", lambda e, S, tt=tt: e.copy(
                  xn_T[:].rearrange("p (kb t) -> p kb t", kb=KB)[:, :, tt * 128:(tt + 1) * 128],
                  p_tr[:].rearrange("p (kb c) -> p kb c", kb=KB)),
               reads=[gb("p_tr")], writes=[gb("xn_T")])
      return f
    def x_closures(k):
        full = k >= NCH_PRE
        dbk = k % 2
        out = []

        def inproj_conv(cb):
            def f(cb=cb):
                wb = cb % 2
                xb = xbr[wb]
                load(wblk[wb][:], scr_win[cb, :, :], f"wblk{wb}", reads=[gb("winscr")])
                pm, pmb = (p_mm, gb("p_mm")) if cb % 2 == 0 else (p_mm2, gb("p_mm2"))
                for kb in range(KB):
                    P.mm("tensor", lambda e, S, pm=pm, wb=wb, kb=kb, dbk=dbk: e.matmul(
                            pm[:, 0:T], wblk[wb][:, kb * 128:(kb + 1) * 128],
                            rms_T[dbk][:, kb * T:(kb + 1) * T],
                            start=(kb == 0), stop=(kb == KB - 1)),
                         reads=[gb(f"wblk{wb}"), gb(f"rms_T{dbk}")], psum=pmb,
                         start=(kb == 0), stop=(kb == KB - 1))
                P.op("scalar", lambda e, S, pm=pm, xb=xb: e.copy(
                        xb[:, 4: 4 + T], pm[:, 0:T]),
                     reads=[pmb], writes=[gb(f"xbrd{wb}")])
                P.op("vector", lambda e, S, cb=cb, xb=xb: e.tensor_copy(
                        xb[:, 1:4], halo_t[:, cb * 4 + 1: cb * 4 + 4]),
                     reads=[gb(f"halo{cb}")], writes=[gb(f"xbrh{wb}")])
                for tap in range(4):
                    P.op("vector", lambda e, S, cb=cb, tap=tap, wb=wb: e.tensor_scalar(
                            diag_t[0][:, tap * 128:(tap + 1) * 128], eye_bf[:],
                            conv_w_t[:, cb * 4 + tap:cb * 4 + tap + 1], None, ALU.mult),
                         reads=[gb("eye_bf"), gb("conv_w")], writes=[gb("diag0")])
                for tap in range(4):
                    P.mm("tensor", lambda e, S, tap=tap, xb=xb: e.matmul(
                            p_conv[:, 0:T], diag_t[0][:, tap * 128:(tap + 1) * 128],
                            xb[:, 1 + tap: 1 + tap + T],
                            start=(tap == 0), stop=(tap == 3)),
                         reads=[gb("diag0"), gb(f"xbrd{wb}"), gb(f"xbrh{wb}")],
                         psum=gb("p_conv"), start=(tap == 0), stop=(tap == 3))
                if cb < CBS:
                    P.op("scalar", lambda e, S, cb=cb, dbk=dbk: e.activation(
                            u_bf[dbk][:, cb * T:(cb + 1) * T], p_conv[:, 0:T], AF.Silu,
                            bias=conv_b_t[:, cb:cb + 1]),
                         reads=[gb("p_conv"), gb("conv_b")], writes=[gb(f"u{dbk}_{cb}")])
                else:
                    P.op("scalar", lambda e, S, cb=cb: e.activation(
                            u_oth[:, (cb - CBS) * T:(cb - CBS + 1) * T], p_conv[:, 0:T],
                            AF.Silu, bias=conv_b_t[:, cb:cb + 1]),
                         reads=[gb("p_conv"), gb("conv_b")], writes=[gb(f"uo_{cb}")])
                P.op("vector", lambda e, S, cb=cb, xb=xb: e.tensor_copy(
                        halo_t[:, cb * 4 + 1: cb * 4 + 4], xb[:, T + 1: T + 4]),
                     reads=[gb(f"xbrd{wb}")], writes=[gb(f"halo{cb}")])
            return f

        def xproj():
            def f():
                def usrc(cb):
                    if cb < CBS:
                        return u_bf[dbk][:, cb * T:(cb + 1) * T], gb(f"u{dbk}_{cb}")
                    return u_oth[:, (cb - CBS) * T:(cb - CBS + 1) * T], gb(f"uo_{cb}")
                for cb in range(CB):
                    ua, ub = usrc(cb)
                    P.mm("tensor", lambda e, S, cb=cb, ua=ua: e.matmul(
                            p_mm[:, 0:T], xpj[:, cb * 160: cb * 160 + 128], ua,
                            start=(cb == 0), stop=(cb == CB - 1)),
                         reads=[gb("xpj"), ub], psum=gb("p_mm"),
                         start=(cb == 0), stop=(cb == CB - 1))
                for cb in range(CB):
                    ua, ub = usrc(cb)
                    P.mm("tensor", lambda e, S, cb=cb, ua=ua: e.matmul(
                            p_mm2[0:32, 0:T], xpj[:, cb * 160 + 128: (cb + 1) * 160], ua,
                            start=(cb == 0), stop=(cb == CB - 1)),
                         reads=[gb("xpj"), ub], psum=gb("p_mm2"),
                         start=(cb == 0), stop=(cb == CB - 1))
                P.op("scalar", lambda e, S, dbk=dbk: e.copy(dtl_bf[dbk][:], p_mm[:, 0:T]),
                     reads=[gb("p_mm")], writes=[gb(f"dtl{dbk}")])
                P.op("scalar", lambda e, S, k=k: e.activation(bc_sb[:], p_mm2[0:32, 0:T],
                                                              AF.Copy, scale=mask_t[0:32, k:k + 1]),
                     reads=[gb("p_mm2"), gb("mask")], writes=[gb("bc_sb")])
                P.op("scalar", lambda e, S, dbk=dbk: e.dma_start(out=scr_bc[dbk], in_=bc_sb[:]),
                     reads=[gb("bc_sb")], writes=[gb(f"sbc{dbk}")], dma=True)
            return f

        def bcasts():
            def f():
                bsrc = scr_bc[dbk:dbk + 1].rearrange("c a b -> c (a b)")
                P.op("sync", lambda e, S, bsrc=bsrc: e.dma_start(
                        out=brep[:],
                        in_=bsrc[:, 0:N * T].broadcast_to((128, N * T))),
                     reads=[gb(f"sbc{dbk}")],
                     writes=[gb(f"brep{nn}") for nn in range(N)], dma=True)
                P.op("sync", lambda e, S, bsrc=bsrc: e.dma_start(
                        out=crep[:],
                        in_=bsrc[:, N * T:2 * N * T].broadcast_to((128, N * T))),
                     reads=[gb(f"sbc{dbk}")],
                     writes=[gb(f"crep{nn}") for nn in range(N)], dma=True)
            return f

        for cb in range(CB):
            out.append((0, inproj_conv(cb)))
        out.append((0, xproj()))
        out.append((CBS, bcasts()))
        return out

    def ffn_closures(m):
        if m < 0 or m >= LH // T:
            return []
        items = []
        for tt in range(TT):
            items.append(_ln_xnT(m, tt))
        HF = 2 if TT >= 2 else 1
        fg, fd = _ffn_makers(m)
        for hf in range(HF):
            for hj in range(HB):
                items.append(fg(hf, hj))
            for tl in range(TT // HF):
                items.append(fd(hf, tl))
        nit = len(items)
        return [((i * (CBS - 2)) // nit, f) for i, f in enumerate(items)]

    def _ffn_makers(k):
        dbk = k % 2
        NGU = (2 if TT >= 2 else 1) * HB

        def gu_load(idx):
            hj = idx % HB
            wb = idx % 3
            qe = "sync" if idx % 2 == 0 else "scalar"
            load(wffn[wb][:], wg_d[hj, :, :], f"wffn{wb}", eng=qe)
            load(wffn2[wb][:], wu_d[hj, :, :], f"wffn2{wb}", eng=qe)

        def ffn_gu(hf, hj):
            idx = hf * HB + hj
            def f(hf=hf, hj=hj, idx=idx):
                if idx == 0:
                    gu_load(0)
                    gu_load(1)
                if idx + 2 < NGU:
                    gu_load(idx + 2)
                wb = idx % 3
                for kb in range(KB):
                    P.mm("tensor", lambda e, S, wb=wb, kb=kb, hf=hf: e.matmul(
                            p_mm[:, 0:TF], wffn[wb][:, kb * 128:(kb + 1) * 128],
                            xn_T[:, kb * T + hf * TF: kb * T + (hf + 1) * TF],
                            start=(kb == 0), stop=(kb == KB - 1)),
                         reads=[gb(f"wffn{wb}"), gb("xn_T")], psum=gb("p_mm"),
                         start=(kb == 0), stop=(kb == KB - 1))
                for kb in range(KB):
                    P.mm("tensor", lambda e, S, wb=wb, kb=kb, hf=hf: e.matmul(
                            p_mm2[:, 0:TF], wffn2[wb][:, kb * 128:(kb + 1) * 128],
                            xn_T[:, kb * T + hf * TF: kb * T + (hf + 1) * TF],
                            start=(kb == 0), stop=(kb == KB - 1)),
                         reads=[gb(f"wffn2{wb}"), gb("xn_T")], psum=gb("p_mm2"),
                         start=(kb == 0), stop=(kb == KB - 1))
                s0, s1 = 0, 1
                P.op("scalar", lambda e, S, s0=s0: e.activation(gu_s[s0][:], p_mm[:, 0:TF],
                                                                AF.Silu, scale=QINV),
                     reads=[gb("p_mm")], writes=[gb(f"gu_s{s0}")])
                P.op("scalar", lambda e, S, s1=s1: e.activation(gu_s[s1][:], p_mm2[:, 0:TF],
                                                                AF.Copy, scale=QINV),
                     reads=[gb("p_mm2")], writes=[gb(f"gu_s{s1}")])
                P.op("gpsimd", lambda e, S, hj=hj, s0=s0, s1=s1: e.tensor_tensor(
                        hffn[:, hj * TF:(hj + 1) * TF], gu_s[s0][:], gu_s[s1][:], ALU.mult),
                     reads=[gb(f"gu_s{s0}"), gb(f"gu_s{s1}")], writes=[gb(f"h{hj}")])
            return f

        def dn_load(pu, tl):
            wb = pu % 2
            qe = "sync" if pu % 2 == 0 else "scalar"
            s = wd_d[pu * 256:(pu + 1) * 256, :].rearrange("(h p) m -> p h m", p=128)
            load(wdn[wb][:].rearrange("p (h m) -> p h m", h=2), s, f"wdn{wb}", eng=qe)

        def ffn_dn(hf, tl):
            def f(hf=hf, tl=tl):
                tt = hf * (TT // (2 if TT >= 2 else 1)) + tl
                dn_load(0, tl)
                for pu in range(HB // 2):
                    wb = pu % 2
                    if pu + 1 < HB // 2:
                        dn_load(pu + 1, tl)
                    for h in range(2):
                        hj = 2 * pu + h
                        for dh in range(2):
                            P.mm("tensor", lambda e, S, hj=hj, h=h, wb=wb, dh=dh, tl=tl: e.matmul(
                                    p_o[:, dh * 512:(dh + 1) * 512],
                                    hffn[:, hj * TF + tl * 128: hj * TF + tl * 128 + 128],
                                    wdn[wb][:, h * DM + dh * 512: h * DM + (dh + 1) * 512],
                                    start=(hj == 0), stop=(hj == HB - 1)),
                                 reads=[gb(f"h{hj}"), gb(f"wdn{wb}")], psum=gb("p_o"),
                                 start=(hj == 0 and dh == 0), stop=(hj == HB - 1 and dh == 1))
                P.op("scalar", lambda e, S, tt=tt: e.activation(
                        g_t[:], p_o[:], AF.Silu, scale=QINV),
                     reads=[gb("p_o")], writes=[gb("g_raw")])
                P.op("scalar", lambda e, S, tt=tt: e.dma_start(
                        out=cc_gin[k, tt * 128:(tt + 1) * 128, :], in_=g_t[:]),
                     reads=[gb("g_raw")], writes=[gb(f"sg{k}_{tt}")], dma=True)
            return f

        return ffn_gu, ffn_dn

    def o_closures(k):
        if k < 0:
            return []
        t0 = k * T
        out = []

        def oload():
            def f():
                pw = ([gb(f"pci{r}") for r in range(4)] + [gb(f"pco{r}") for r in range(4)]
                      if k == 0 else [])
                load(yg_lo[:], cc_yout[k, 0:128, :], "yglo", eng="sync",
                     reads=[gb(f"ccyo{k}")], war=pw)
                load(yg_hi[:], cc_yout[k, 128:256, :], "yghi", eng="sync",
                     reads=[gb(f"ccyo{k}")], war=pw)
            return f

        def wo_load(pu):
            wb = pu % 2
            qe = "sync" if pu % 2 == 0 else "scalar"
            s = scr_wout[pu * 256:(pu + 1) * 256, :].rearrange("(h p) m -> p h m", p=128)
            load(woutb[wb][:].rearrange("p (h m) -> p h m", h=2), s, f"woutb{wb}", eng=qe,
                 reads=[gb("woutscr")])

        def omm(tt):
            def f(tt=tt):
                wo_load(0)
                for pu in range(CB // 2):
                    wb = pu % 2
                    if pu + 1 < CB // 2:
                        wo_load(pu + 1)
                    for h2 in range(2):
                        cb = 2 * pu + h2
                        src_t = yg_lo if cb < CBS else yg_hi
                        sk = "yglo" if cb < CBS else "yghi"
                        cbl = cb % CBS
                        for dh in range(2):
                            P.mm("tensor", lambda e, S, cb=cb, cbl=cbl, h2=h2, wb=wb, dh=dh, tt=tt, src_t=src_t: e.matmul(
                                    p_o[:, dh * 512:(dh + 1) * 512],
                                    src_t[:, cbl * T + tt * 128: cbl * T + (tt + 1) * 128],
                                    woutb[wb][:, h2 * DM + dh * 512: h2 * DM + (dh + 1) * 512],
                                    start=(cb == 0), stop=(cb == CB - 1)),
                                 reads=[gb(sk), gb(f"woutb{wb}")], psum=gb("p_o"),
                                 start=(cb == 0 and dh == 0), stop=(cb == CB - 1 and dh == 1))
            return f

        def ofin(tt):
            def f(tt=tt):
                row = t0 + tt * 128
                pp = tt % 2
                load(xt[pp][:], scr_xn[row:row + 128, :], f"xt{pp}", eng="gpsimd",
                     reads=[gb(f"sxn{k}_{tt}")])
                P.op("vector", lambda e, S, pp=pp: e.tensor_tensor(
                        xnt[pp][:], p_o[:], xt[pp][:], ALU.add),
                     reads=[gb("p_o"), gb(f"xt{pp}")], writes=[gb(f"xnt{pp}")])
                ln_rms_stats(xnt[pp][:], gb(f"xnt{pp}"), 7, is_rms=True)
                P.op("vector", lambda e, S, pp=pp: e.tensor_scalar_mul(
                        xnt[pp][:], xnt[pp][:], sc1[:, 7:8]),
                     reads=[gb(f"xnt{pp}"), gb("rstd7")], writes=[gb(f"xnt{pp}")])
                m = k if k < NCH // 2 else k - NCH // 2
                goff = (0 if k < NCH // 2 else T) + tt * 128
                load(g_t[:], cc_gout[m, goff:goff + 128, :], "g_raw", eng="gpsimd",
                     reads=[gb(f"ccgo{m}")])
                P.op("gpsimd", lambda e, S, pp=pp: e.tensor_tensor(
                        xnt[pp][:], xnt[pp][:], normf_rep[:], ALU.mult),
                     reads=[gb(f"xnt{pp}"), gb("normf_rep")], writes=[gb(f"xnt{pp}")])
                P.op("vector", lambda e, S, pp=pp: e.tensor_tensor(
                        xnt[pp][:], xnt[pp][:], g_t[:], ALU.mult),
                     reads=[gb(f"xnt{pp}"), gb("g_raw")], writes=[gb(f"xnt{pp}")])
                load(xt[pp][:], x_d[row:row + 128, :], f"xt{pp}", eng="gpsimd")
                P.op("vector", lambda e, S, pp=pp: e.tensor_tensor(
                        xnt[pp][:], xnt[pp][:], xt[pp][:], ALU.add),
                     reads=[gb(f"xnt{pp}"), gb(f"xt{pp}")], writes=[gb(f"xnt{pp}")])
                orow = k * T + tt * 128
                P.op("scalar", lambda e, S, pp=pp, orow=orow: e.dma_start(
                        out=out_d[orow:orow + 128, :], in_=xnt[pp][:]),
                     reads=[gb(f"xnt{pp}")], writes=[gb("outw")], dma=True)
            return f

        out.append((0, oload()))
        for tt in range(TT):
            out.append((tt * 2, omm(tt)))
            out.append((tt * 2 + 1, ofin(tt)))
        return out

    def emit_scan(k, pumpq, first_budget=0):
        full = True
        dbk = k % 2

        def jit_zdt(cb):
            pp = cb % 2
            P.mm("tensor", lambda e, S, cb=cb, dbk=dbk: e.matmul(
                    p_conv[:, 0:T], dtw_bf[:, cb * 128:(cb + 1) * 128], dtl_bf[dbk][:],
                    start=True, stop=True),
                 reads=[gb("dtw_bf"), gb(f"dtl{dbk}")], psum=gb("p_conv"),
                 start=True, stop=True)
            P.op("scalar", lambda e, S, cb=cb, pp=pp: e.activation(
                    sp_t[pp][:], p_conv[:, 0:T], AF.Exp, bias=dtb_t[:, cb:cb + 1]),
                 reads=[gb("p_conv"), gb("dtb")], writes=[gb(f"sp{pp}")])
            P.op("scalar", lambda e, S, pp=pp: e.activation(
                    sp_t[pp][:], sp_t[pp][:], AF.Ln, bias=cst[:, 1:2]),
                 reads=[gb(f"sp{pp}"), gb("cst")], writes=[gb(f"sp{pp}")])
            if full:
                pmz, pmzb = ((p_mm, gb("p_mm")) if pp == 0 else (p_mm2, gb("p_mm2")))
                wb = pp
                load(wblk[wb][:], scr_win[CB + cb, :, :], f"wblk{wb}", reads=[gb("winscr")])
                for kb in range(KB):
                    P.mm("tensor", lambda e, S, pmz=pmz, wb=wb, kb=kb, dbk=dbk: e.matmul(
                            pmz[:, 0:T], wblk[wb][:, kb * 128:(kb + 1) * 128],
                            rms_T[dbk][:, kb * T:(kb + 1) * T],
                            start=(kb == 0), stop=(kb == KB - 1)),
                         reads=[gb(f"wblk{wb}"), gb(f"rms_T{dbk}")], psum=pmzb,
                         start=(kb == 0), stop=(kb == KB - 1))
                P.op("scalar", lambda e, S, pmz=pmz, pp=pp: e.activation(
                        z_bf[pp][:], pmz[:, 0:T], AF.Silu),
                     reads=[pmzb], writes=[gb(f"z{pp}")])

        nq = len(pumpq)
        budget_per = (nq + CBS - 1) // CBS + 1

        def pump(cur_cb, budget):
            while pumpq and budget > 0 and pumpq[0][0] <= cur_cb:
                pumpq.pop(0)[1]()
                budget -= 1

        jit_zdt(0)
        for cb in range(CBS):
            pp = cb % 2
            if cb + 1 < CBS:
                jit_zdt(cb + 1)
            pump(cb, budget_per + (first_budget if cb == 0 else 0))
            if DEBUG_SCR:
                P.op("scalar", lambda e, S, cb=cb, k=k, dbk=dbk: e.dma_start(
                        out=scr_du[k, :, cb * T:(cb + 1) * T],
                        in_=u_bf[dbk][:, cb * T:(cb + 1) * T]),
                     reads=[gb(f"u{dbk}_{cb}")], writes=[gb("dbgu")], dma=True)
            P.op("vector", lambda e, S, cb=cb, pp=pp, dbk=dbk: e.tensor_tensor(
                    dtu_t[pp][:], sp_t[pp][:], u_bf[dbk][:, cb * T:(cb + 1) * T], ALU.mult),
                 reads=[gb(f"sp{pp}"), gb(f"u{dbk}_{cb}")], writes=[gb(f"dtu{pp}")])
            for g in range(NGRP):
                gp = (cb * NGRP + g) % 3
                yhi = (cb * NGRP + g) % 3
                P.op("scalar", lambda e, S, cb=cb, gp=gp, nn=g, pp=pp: e.activation(
                        a_t[gp][:], sp_t[pp][:],
                        AF.Exp, scale=acol_t[:, cb * N + nn: cb * N + nn + 1]),
                     reads=[gb(f"sp{pp}"), gb("acol")], writes=[gb(f"a{gp}")])
                bp = g % 2
                P.op("vector", lambda e, S, pp=pp, bp=bp, nn=g: e.tensor_tensor(
                        b_t[bp][:], dtu_t[pp][:],
                        brep[:, nn * T:(nn + 1) * T], ALU.mult),
                     reads=[gb(f"dtu{pp}"), gb(f"brep{g}")], writes=[gb(f"b{bp}")])
                P.op("vector", lambda e, S, cb=cb, gp=gp, bp=bp, nn=g: e.tensor_tensor_scan(
                        a_t[gp][:], a_t[gp][:], b_t[bp][:],
                        carry[:, cb * N + nn: cb * N + nn + 1], ALU.mult, ALU.add),
                     reads=[gb(f"a{gp}"), gb(f"b{bp}"), gb("carry")],
                     writes=[gb(f"a{gp}")])
                P.op("scalar", lambda e, S, cb=cb, gp=gp, nn=g: e.copy(
                        carry[:, cb * N + nn: cb * N + nn + 1], a_t[gp][:, T - 1:T]),
                     reads=[gb(f"a{gp}")], writes=[gb("carry")])
                if full:
                    P.op("gpsimd", lambda e, S, gp=gp, nn=g, yhi=yhi: e.tensor_tensor(
                            yh_t[yhi][:], a_t[gp][:],
                            crep[:, nn * T:(nn + 1) * T], ALU.mult),
                         reads=[gb(f"a{gp}"), gb(f"crep{g}")], writes=[gb(f"yh{yhi}")])
                    P.mm("tensor", lambda e, S, cb=cb, yhi=yhi, g=g: e.matmul(
                            p_ys[cb % 2][:, 0:T], eye_bf[:], yh_t[yhi][:],
                            start=(g == 0), stop=(g == NGRP - 1)),
                         reads=[gb(f"yh{yhi}")], psum=gb(f"p_ys{cb % 2}"),
                         start=(g == 0), stop=(g == NGRP - 1))
            if full:
                P.op("vector", lambda e, S, cb=cb, pp=pp, dbk=dbk: e.scalar_tensor_tensor(
                        dtu_t[pp][:], u_bf[dbk][:, cb * T:(cb + 1) * T], dvec_t[:, cb:cb + 1],
                        p_ys[cb % 2][:, 0:T], ALU.mult, ALU.add),
                     reads=[gb(f"u{dbk}_{cb}"), gb("dvec"), gb(f"p_ys{cb % 2}")],
                     writes=[gb(f"dtu{pp}")])
                P.op("gpsimd", lambda e, S, cb=cb, pp=pp: e.tensor_tensor(
                        yg_bf[:, cb * T:(cb + 1) * T], dtu_t[pp][:], z_bf[pp][:], ALU.mult),
                     reads=[gb(f"dtu{pp}"), gb(f"z{pp}")], writes=[gb(f"yg{cb}")])
                if DEBUG_SCR:
                    P.op("scalar", lambda e, S, cb=cb, k=k: e.dma_start(
                            out=scr_dyg[k, :, cb * T:(cb + 1) * T],
                            in_=yg_bf[:, cb * T:(cb + 1) * T]),
                         reads=[gb(f"yg{cb}")], writes=[gb("dbgyg")], dma=True)
        pump(CBS + CB, 10 ** 9)
        P.op("scalar", lambda e, S, k=k: e.dma_start(out=cc_yin[k], in_=yg_bf[:]),
             reads=[gb(f"yg{cb}") for cb in range(CBS)], writes=[gb(f"ccyi{k}")], dma=True)
        P.op("gpsimd", lambda e, S, k=k: e.collective_compute(
                "AllGather", ALU.bypass, ins=[cc_yin[k]], outs=[cc_yout[k]],
                replica_groups=RG),
             reads=[gb(f"ccyi{k}")], writes=[gb(f"ccyo{k}")], sem_override="cC")

    precast_tasks = []
    def precast_to_tasks(fn_emit_iters):
        for c in fn_emit_iters:
            precast_tasks.append((0, c))

    def gen_precast_big(srcd, dst, rows, cols, skey, i32):
        icols = (CBS * T) // 2
        CW = min(512, icols // 2)
        RING = icols // CW
        ivw = yg_lo[:].bitcast(I32 if i32 else F32)
        tiled3 = len(srcd.shape) == 3
        if tiled3:
            blocks = [(t, cc) for t in range(srcd.shape[0])
                      for cc in range(srcd.shape[2] // CW)]
        else:
            blocks = [(rb, cc) for rb in range(rows // 128) for cc in range(cols // CW)]
        nb = len(blocks)
        LA = min(3, RING - 1)

        def emit_load(i):
            rb, cc = blocks[i]
            r = i % RING
            qe = "sync" if i % 2 == 0 else "scalar"
            s = (srcd[rb, :, cc * CW:cc * CW + CW] if tiled3 else
                 srcd[rb * 128:(rb + 1) * 128, cc * CW:cc * CW + CW])
            load(ivw[:, r * CW:(r + 1) * CW], s, f"pci{r}", eng=qe)

        for bi in range(nb):
            def one(bi=bi):
                if bi == 0:
                    for jj in range(min(LA, nb)):
                        emit_load(jj)
                elif bi + LA - 1 < nb:
                    emit_load(bi + LA - 1)
                rb, cc = blocks[bi]
                c0 = cc * CW
                r = bi % RING
                P.op("vector", lambda e, S, r=r: e.tensor_copy(
                        yg_hi[:, r * CW:(r + 1) * CW], ivw[:, r * CW:(r + 1) * CW]),
                     reads=[gb(f"pci{r}")], writes=[gb(f"pco{r}")])
                qe = "scalar" if bi % 2 == 0 else "sync"
                dd = (dst[rb, :, c0:c0 + CW] if tiled3 else
                      dst[rb * 128:(rb + 1) * 128, c0:c0 + CW])
                P.op(qe, lambda e, S, dd=dd, r=r: e.dma_start(
                        out=dd, in_=yg_hi[:, r * CW:(r + 1) * CW]),
                     reads=[gb(f"pco{r}")], writes=[gb(skey + "scr")], dma=True)
            yield one

    precast_to_tasks(gen_precast_big(wout_d, scr_wout, DI, DM, "wout", False))

    emit_ln(0)
    fq0 = [f for _, f in ffn_closures(0)]
    n_head = TT + (2 if TT >= 2 else 1) * HB
    fq0_head, fq0_tail = fq0[:n_head], fq0[n_head:]
    for _, f in x_closures(0):
        f()
        for _ in range(3):
            if fq0_head:
                fq0_head.pop(0)()
    for f in fq0_head:
        f()
    for k in range(NCH):
        q = []
        q += o_closures(k - 2)
        if k == 1:
            nfq = len(fq0_tail)
            q += [((i * (CBS - 2)) // max(nfq, 1), f) for i, f in enumerate(fq0_tail)]
        else:
            q += ffn_closures(k - 1)
        lnA, _ = ln_closures(k + 1) if k + 1 < NCH else ([], [])
        q += lnA
        if k + 1 < NCH:
            q += x_closures(k + 1)
        if k == 0:
            q += precast_tasks
            del precast_tasks[:]
        emit_scan(k, q, first_budget=2)
        m = k - 1
        if 0 <= m < LH // T:
            P.op("gpsimd", lambda e, S, m=m: e.collective_compute(
                    "AllGather", ALU.bypass, ins=[cc_gin[m]], outs=[cc_gout[m]],
                    replica_groups=RG),
                 reads=[gb(f"sg{m}_{tt}") for tt in range(TT)], writes=[gb(f"ccgo{m}")],
                 sem_override="cC")
    for _, f in ffn_closures(NCH - 1):
        f()
    m = NCH - 1
    if m < LH // T:
        P.op("gpsimd", lambda e, S, m=m: e.collective_compute(
                "AllGather", ALU.bypass, ins=[cc_gin[m]], outs=[cc_gout[m]],
                replica_groups=RG),
             reads=[gb(f"sg{m}_{tt}") for tt in range(TT)], writes=[gb(f"ccgo{m}")],
             sem_override="cC")
    for _, f in o_closures(NCH - 2):
        f()
    for _, f in o_closures(NCH - 1):
        f()

    P.raw_wait("gpsimd", gb("outw").w)
    P.flush_queues("gpsimd")

    sems = {}
    sem_ctx = ExitStack()
    for s in ALL_SEMS:
        sems[s] = sem_ctx.enter_context(nc.semaphore(s))
    P.finalize_into(nc, sems)
    ctx.close()
    sem_ctx.close()
    return nc, dict(L=L, T=T, NCH=NCH, NCH_PRE=NCH_PRE, L_OWN=L_OWN, ninstr=P.ninstr)


def make_core_inputs(core_id, inputs, L=4096, NCH=8, NCH_PRE=0):
    b, j = core_id // 2, core_id % 2
    DM, DI, N = 1024, 2048, 16
    DIH = DI // 2
    CB, CBS = 16, 8
    T = L // NCH
    LH = L // 2
    x = np.asarray(inputs["L0_attn_x"][b], dtype=np.float32)
    own = slice(j * DIH, (j + 1) * DIH)
    oth = slice((1 - j) * DIH, (2 - j) * DIH)
    perm = np.r_[np.arange(j * DIH, (j + 1) * DIH), np.arange((1 - j) * DIH, (2 - j) * DIH)]
    ipw = np.asarray(inputs["L0_attn_in_proj_w"], np.float32)
    w_in = np.concatenate([ipw[perm, :], ipw[2048 + j * DIH: 2048 + (j + 1) * DIH, :]], axis=0).T
    cw = np.asarray(inputs["L0_attn_conv_w"], np.float32)[perm]
    cb_ = np.asarray(inputs["L0_attn_conv_b"], np.float32)[perm]
    xpw = np.asarray(inputs["L0_attn_x_proj_w"], np.float32)[:, perm]
    dtw = np.asarray(inputs["L0_attn_dt_proj_w"], np.float32)[own]
    dtb = np.asarray(inputs["L0_attn_dt_proj_b"], np.float32)[own]
    alog = np.asarray(inputs["L0_attn_A_log"], np.float32)[own]
    dvec = np.asarray(inputs["L0_attn_D"], np.float32)[own]
    d = {
        "x": np.ascontiguousarray(x),
        "xf": np.ascontiguousarray(x[j * LH:(j + 1) * LH]),
        "mask": np.ones((128, NCH), np.float32),
        "w_in": np.ascontiguousarray(w_in.T),
        "norm_w_row": np.ascontiguousarray(np.asarray(inputs["L0_attn_norm_w"], np.float32).reshape(1, DM)),
        "norm_w": np.ascontiguousarray(np.asarray(inputs["L0_attn_norm_w"], np.float32).reshape(8, 128).T),
        "conv_w_r": np.ascontiguousarray(cw.reshape(CB, 128, 4).transpose(1, 0, 2).reshape(128, CB * 4)),
        "conv_b_r": np.ascontiguousarray(cb_.reshape(CB, 128).T),
        "xproj_T": np.ascontiguousarray(xpw.T),
        "dtw_T": np.ascontiguousarray(dtw.T),
        "dtb_r": np.ascontiguousarray(dtb.reshape(CBS, 128).T),
        "alog_r": np.ascontiguousarray(alog.reshape(CBS, 128, N).transpose(1, 0, 2).reshape(128, CBS * N)),
        "d_r": np.ascontiguousarray(dvec.reshape(CBS, 128).T),
        "wout_T": np.ascontiguousarray(np.asarray(inputs["L0_attn_out_proj_w"], np.float32).T),
        "norm_f": np.ascontiguousarray(np.asarray(inputs["L0_attn_norm_f_w"], np.float32).reshape(1, DM)),
        "wg_T": _tile_ffn(np.asarray(inputs["L0_ffn_w_gate"], np.int32).T.astype(_BF)),
        "wu_T": _tile_ffn(np.asarray(inputs["L0_ffn_w_up"], np.int32).T.astype(_BF)),
        "wd_T": np.ascontiguousarray(np.asarray(inputs["L0_ffn_w_down"], np.int32).T.astype(_BF)),
        "eye": np.eye(128, dtype=np.float32),
    }
    return d


def _tile_ffn(wT):
    DM, HID = wT.shape
    return np.ascontiguousarray(
        wT.reshape(DM // 128, 128, HID // 128, 128).transpose(2, 1, 0, 3)
          .reshape(HID // 128, 128, DM))


def assemble_output(results, L=4096, NCH=8, NCH_PRE=0, B=4):
    DM = 1024
    LH = L // 2
    out = np.empty((B, L, DM), np.float32)
    for core_id in range(2 * B):
        b, j = core_id // 2, core_id % 2
        o = results[core_id]["out"]
        out[b, j * LH:(j + 1) * LH] = o[j * LH:(j + 1) * LH]
    return out


from concourse.bass_utils import run_bass_kernel_spmd

_BUILT = {}

def _get_program():
    if "nc" not in _BUILT:
        nc, meta = build_program(L=4096, T=512)
        _BUILT["nc"] = nc
        _BUILT["meta"] = meta
    return _BUILT["nc"], _BUILT["meta"]


def kernel(**inputs):
    nc, meta = _get_program()
    in_maps = [make_core_inputs(c, inputs, L=meta["L"], NCH=meta["NCH"],
                                NCH_PRE=meta["NCH_PRE"]) for c in range(8)]
    res = run_bass_kernel_spmd(nc, in_maps, core_ids=list(range(8)))
    return assemble_output(list(res.results), L=meta["L"], NCH=meta["NCH"],
                           NCH_PRE=meta["NCH_PRE"], B=4)


# version 33
# speedup vs baseline: 1.1393x; 1.0011x over previous
import numpy as np
import ml_dtypes
_BF = ml_dtypes.bfloat16
from contextlib import ExitStack
import concourse.bass as bass
import concourse.mybir as mybir

F32 = mybir.dt.float32
BF16 = mybir.dt.bfloat16
I32 = mybir.dt.int32
AF = mybir.ActivationFunctionType
ALU = mybir.AluOpType

ENGINES = ("sync", "scalar", "vector", "tensor", "gpsimd")
CSEM = {"scalar": "cS", "vector": "cV", "tensor": "cT", "gpsimd": "cG"}
DQ = {"sync": tuple(f"dS{i}" for i in range(8)),
      "scalar": tuple(f"dA{i}" for i in range(8)),
      "gpsimd": tuple(f"dG{i}" for i in range(4))}
ALL_SEMS = ("cS", "cV", "cT", "cG", "cC") + DQ["sync"] + DQ["scalar"] + DQ["gpsimd"]
BATCH_MAX = 2


class Ev:
    __slots__ = ("sem", "val")
    def __init__(self, sem, val):
        self.sem = sem
        self.val = val


class DmaEv:
    __slots__ = ("q", "batch")
    def __init__(self, q, batch):
        self.q = q
        self.batch = batch


class PendEv:
    __slots__ = ("eng", "ev")
    def __init__(self, eng):
        self.eng = eng
        self.ev = None


class Buf:
    __slots__ = ("w", "reads")
    def __init__(self):
        self.w = None
        self.reads = []


class Prog:
    def __init__(self):
        self.ops = {e: [] for e in ENGINES}
        self.ctr = {s: 0 for s in ALL_SEMS}
        self.waited = {e: {s: 0 for s in ALL_SEMS} for e in ENGINES}
        self.ninstr = 0
        self.q = {e: dict(nopen=0, nbatch=0, open=None) for e in DQ}
        self.pending = {e: [] for e in ENGINES}

    def _close_batch(self, qeng, which=None):
        st = self.q[qeng]
        if st["open"] is not None:
            batch = st["open"]
            batch[0] = True
            batch[2] = self.ctr[batch[1]]
            st["open"] = None
            st["nopen"] = 0

    def _resolve_pending(self, eng2):
        sem = CSEM[eng2]
        self.ctr[sem] += 1
        val = self.ctr[sem]
        self.ops[eng2].append(lambda e, S, _s=sem: e.nop().then_inc(S[_s], 1))
        self.ninstr += 1
        ev = Ev(sem, val)
        for p in self.pending[eng2]:
            p.ev = ev
        self.pending[eng2] = []
        return ev

    def _wait(self, eng, ev, raw=True):
        if ev is None:
            return
        if isinstance(ev, PendEv):
            if ev.ev is None:
                self._resolve_pending(ev.eng)
            ev = ev.ev
        if isinstance(ev, DmaEv):
            if not ev.batch[0]:
                self._close_batch(ev.q)
            ev = Ev(ev.batch[1], ev.batch[2])
        if self.waited[eng].get(ev.sem, 0) >= ev.val:
            return
        self.waited[eng][ev.sem] = ev.val
        sem_name, val = ev.sem, ev.val
        self.ops[eng].append(lambda e, S, _s=sem_name, _v=val: e.wait_ge(S[_s], _v))
        self.ninstr += 1

    def op(self, eng, fn, reads=(), writes=(), war=(), dma=False, track=True,
           sem_override=None):
        for b in reads:
            self._wait(eng, b.w, raw=True)
        for b in war:
            self._wait(eng, b.w, raw=False)
            for r in b.reads:
                self._wait(eng, r, raw=False)
        for b in writes:
            self._wait(eng, b.w, raw=False)
            for r in b.reads:
                self._wait(eng, r, raw=False)
        if dma:
            st = self.q[eng]
            if st["open"] is None or st["nopen"] >= BATCH_MAX:
                self._close_batch(eng)
                pool = DQ[eng]
                psem = pool[st["nbatch"] % len(pool)]
                st["nbatch"] += 1
                self._wait(eng, Ev(psem, self.ctr[psem]))
                st["open"] = [False, psem, 0]
            batch = st["open"]
            psem = batch[1]
            self.ctr[psem] += 16
            st["nopen"] += 1
            def run(e, S, fn=fn, psem=psem):
                fn(e, S).then_inc(S[psem], 16)
            ev = DmaEv(eng, batch)
            self.ops[eng].append(run)
            self.ninstr += 1
            for b in writes:
                b.w = ev
                b.reads = []
            for b in reads:
                b.reads.append(ev)
            return ev
        if track:
            sem = sem_override or CSEM[eng]
            self.ctr[sem] += 1
            val = self.ctr[sem]
            def run(e, S, fn=fn, sem=sem):
                fn(e, S).then_inc(S[sem], 1)
            ev = Ev(sem, val)
            for p in self.pending[eng]:
                p.ev = ev
            self.pending[eng] = []
        else:
            def run(e, S, fn=fn):
                fn(e, S)
            ev = None
        self.ops[eng].append(run)
        self.ninstr += 1
        if track:
            for b in writes:
                b.w = ev
                b.reads = []
            for b in reads:
                b.reads.append(ev)
        else:
            pe = PendEv(eng)
            self.pending[eng].append(pe)
            for b in reads:
                b.reads.append(pe)
        return ev

    def mm(self, eng, fn, reads=(), psum=None, start=False, stop=False):
        if start and stop:
            return self.op(eng, fn, reads=reads, writes=[psum])
        if start:
            return self.op(eng, fn, reads=reads, war=[psum])
        if stop:
            return self.op(eng, fn, reads=reads, writes=[psum])
        return self.op(eng, fn, reads=reads)

    def raw_wait(self, eng, ev):
        self._wait(eng, ev)

    def flush_queues(self, eng):
        for qeng in DQ:
            self._close_batch(qeng)
            for psem in DQ[qeng]:
                self._wait(eng, Ev(psem, self.ctr[psem]))

    def finalize_into(self, nc, sems):
        with nc.Block() as block:
            for eng in ENGINES:
                ops = self.ops[eng]
                if not ops:
                    continue
                getattr(block, eng)(_mkbody(ops, sems))


def _mkbody(ops, sems):
    def body(e):
        for fn in ops:
            fn(e, sems)
    return body


DEBUG_SCR = False

def build_program(L=4096, T=512, NCH_PRE=None):
    DM, DI, N, RANK, HID = 1024, 2048, 16, 128, 3072
    NCH = L // T
    NCH_PRE = 0
    L_OWN = L
    DIH = DI // 2
    CBS = DIH // 128
    LH = L // 2
    RG = [[0, 1], [2, 3], [4, 5], [6, 7]]
    KB = DM // 128
    CB = DI // 128
    HB = HID // 128
    TT = T // 128
    NG = 1
    NGRP = N // NG
    TF = T // 2 if T >= 256 else T
    EPS = 1e-5
    QINV = float(np.float32(1.0 / 127.0))
    NCO = (DI + DIH) // 128

    nc = bass.Bass("TRN2", target_bir_lowering=False, debug=False)

    def din(name, shape, dt):
        return nc.dram_tensor(name, list(shape), dt, kind="ExternalInput").ap()

    x_d = din("x", (L, DM), F32)
    mask_d = din("mask", (128, NCH), F32)
    w_in_d = din("w_in", (DI + DIH, DM), F32)
    norm_w_row_d = din("norm_w_row", (1, DM), F32)
    norm_w_d = din("norm_w", (128, KB), F32)
    conv_w_d = din("conv_w_r", (128, CB * 4), F32)
    conv_b_d = din("conv_b_r", (128, CB), F32)
    xproj_d = din("xproj_T", (DI, 160), F32)
    dtw_d = din("dtw_T", (RANK, DIH), F32)
    dtb_d = din("dtb_r", (128, CBS), F32)
    alog_d = din("alog_r", (128, CBS * N), F32)
    dvec_d = din("d_r", (128, CBS), F32)
    wout_d = din("wout_T", (DI, DM), F32)
    normf_d = din("norm_f", (1, DM), F32)
    wg_d = din("wg_T", (HB, 128, KB * 128), BF16)
    wu_d = din("wu_T", (HB, 128, KB * 128), BF16)
    wd_d = din("wd_T", (HID, DM), BF16)
    eye_d = din("eye", (128, 128), F32)
    xf_d = din("xf", (LH, DM), F32)

    out_d = nc.dram_tensor("out", [L_OWN, DM], F32, kind="ExternalOutput").ap()

    def scr(name, shape, dt):
        return nc.dram_tensor(name, list(shape), dt, kind="Internal").ap()

    scr_win = scr("scr_win", (NCO, 128, KB * 128), BF16)
    scr_wout = scr("scr_wout", (DI, DM), BF16)
    scr_bc = scr("scr_bc", (2, 32, T), BF16)
    scr_xn = scr("scr_xn", (L, DM), F32)
    scr_nf = scr("scr_nf", (1, DM), F32)
    scr_nw = scr("scr_nw", (1, DM), F32)
    cc_yin = scr("cc_yin", (NCH, 128, CBS * T), BF16)
    cc_yout = scr("cc_yout", (NCH, 256, CBS * T), BF16)
    cc_gin = scr("cc_gin", (NCH // 2, T, DM), BF16)
    cc_gout = scr("cc_gout", (NCH // 2, 2 * T, DM), BF16)
    scr_du = scr("scr_du", (NCH, 128, CB * T), BF16) if DEBUG_SCR else None
    scr_dyg = scr("scr_dyg", (NCH, 128, CB * T), BF16) if DEBUG_SCR else None

    ctx = ExitStack()
    sb = lambda n, s, d: ctx.enter_context(nc.sbuf_tensor(n, list(s), d))
    psx = lambda n, s, d: ctx.enter_context(nc.psum_tensor(n, list(s), d))

    eye_bf = sb("eye_bf", (128, 128), BF16)
    norm_w_t = sb("norm_w_t", (128, KB), F32)
    conv_w_t = sb("conv_w_t", (128, CB * 4), F32)
    conv_b_t = sb("conv_b_t", (128, CB), F32)
    dtb_t = sb("dtb_t", (128, CBS), F32)
    acol_t = sb("acol_t", (128, CBS * N), F32)
    dvec_t = sb("dvec_t", (128, CBS), F32)
    mask_t = sb("mask_t", (128, NCH), F32)
    normf_rep = sb("normf_rep", (128, DM), BF16)
    normw_rep = sb("normw_rep", (128, DM), BF16)
    carry = sb("carry", (128, CBS * N), F32)
    xpj = sb("xpj", (128, CB * 160), BF16)
    dtw_bf = sb("dtw_bf", (128, DIH), BF16)
    stat6 = sb("stat6", (128, 12), F32)
    stat2 = sb("stat2", (128, 2), F32)
    sc1 = sb("sc1", (128, 8), F32)
    cst = sb("cst", (128, 2), F32)
    xt = [sb(f"xt{i}", (128, DM), F32) for i in range(2)]
    xnt = [sb(f"xnt{i}", (128, DM), F32) for i in range(2)]
    rms_bf = [sb(f"rmsbf{i}", (128, DM), BF16) for i in range(1)]
    rms_T = [sb(f"rms_T{i}", (128, KB * T), BF16) for i in range(2)]
    xn_T = sb("xn_T", (128, KB * T), BF16)
    xbr = [sb(f"xbr{i}", (128, T + 4), BF16) for i in range(2)]
    halo_t = sb("halo_t", (128, CB * 4), BF16)
    u_bf = [sb(f"u_bf{i}", (128, CBS * T), BF16) for i in range(2)]
    u_oth = sb("u_oth", (128, (CB - CBS) * T), BF16)
    z_bf = [sb(f"zbf{i}", (128, T), BF16) for i in range(2)]
    dtl_bf = [sb(f"dtl{i}", (128, T), BF16) for i in range(2)]
    sp_t = [sb(f"sp{i}", (128, T), F32) for i in range(2)]
    dtu_t = [sb(f"dtu{i}", (128, T), BF16) for i in range(2)]
    a_t = [sb(f"a{i}", (128, NG * T), F32) for i in range(3)]
    b_t = [sb(f"b{i}", (128, NG * T), BF16) for i in range(2)]
    yh_t = [sb(f"yh{i}", (128, NG * T), BF16) for i in range(3)]
    brep = sb("brep", (128, N * T), BF16)
    crep = sb("crep", (128, N * T), BF16)
    bc_sb = sb("bc_sb", (32, T), BF16)
    yg_bf = sb("yg_bf", (128, CBS * T), BF16)
    yg_lo = sb("yg_lo", (128, CBS * T), BF16)
    yg_hi = sb("yg_hi", (128, CBS * T), BF16)
    diag_t = [sb("diag0", (128, 4 * 128), BF16)]
    wblk = [sb(f"wblk{i}", (128, KB * 128), BF16) for i in range(2)]
    wffn = [sb(f"wffn{i}", (128, KB * 128), BF16) for i in range(3)]
    wffn2 = [sb(f"wffn2{i}", (128, KB * 128), BF16) for i in range(3)]
    wdn = [sb(f"wdn{i}", (128, 2 * DM), BF16) for i in range(2)]
    woutb = [sb(f"woutb{i}", (128, 2 * DM), BF16) for i in range(2)]
    gu_s = [sb(f"gu_s{i}", (128, TF), BF16) for i in range(2)]
    hffn = sb("hffn", (128, HB * TF), BF16)
    g_t = sb("g_t", (128, DM), BF16)
    castf_t = sp_t[0] if T >= 256 else sb("cast_fs", (128, 256), F32)
    castf_key = "sp0" if T >= 256 else "cast_fs"

    p_mm = psx("p_mm", (128, 512), F32)
    p_mm2 = psx("p_mm2", (128, 512), F32)
    p_conv = psx("p_conv", (128, 512), F32)
    p_ys = [psx(f"p_ys{i}", (128, 512), F32) for i in range(2)]
    p_tr = psx("p_tr", (128, 1024), BF16)
    p_o = psx("p_o", (128, 1024), F32)

    bufs = {}
    def gb(key):
        if key not in bufs:
            bufs[key] = Buf()
        return bufs[key]

    P = Prog()

    def load(dst_ap, src_ap, key=None, eng="sync", reads=(), writes=(), war=()):
        w = list(writes)
        if key is not None:
            w.append(gb(key))
        return P.op(eng, lambda e, S, d=dst_ap, s=src_ap: e.dma_start(out=d, in_=s),
                    reads=list(reads), writes=w, war=list(war), dma=True)

    load(xt[0][:, 0:128], eye_d[:], "xt0")
    P.op("vector", lambda e, S: e.tensor_copy(eye_bf[:], xt[0][:, 0:128]),
         reads=[gb("xt0")], writes=[gb("eye_bf")])
    load(norm_w_t[:], norm_w_d[:], "norm_w")
    load(conv_w_t[:], conv_w_d[:], "conv_w")
    load(conv_b_t[:], conv_b_d[:], "conv_b")
    load(dtb_t[:], dtb_d[:], "dtb")
    load(acol_t[:], alog_d[:], "acol")
    load(dvec_t[:], dvec_d[:], "dvec")
    load(mask_t[:], mask_d[:], "mask")
    load(scr_nf[:], normf_d[:], "scr_nf", eng="scalar")
    load(xnt[0][:], scr_nf[:].broadcast_to((128, DM)), "xnt0", eng="scalar",
         reads=[gb("scr_nf")])
    P.op("vector", lambda e, S: e.tensor_copy(normf_rep[:], xnt[0][:]),
         reads=[gb("xnt0")], writes=[gb("normf_rep")])
    P.op("scalar", lambda e, S: e.activation(acol_t[:], acol_t[:], AF.Exp),
         reads=[gb("acol")], writes=[gb("acol")])
    P.op("vector", lambda e, S: e.tensor_scalar_mul(acol_t[:], acol_t[:], -1.0),
         reads=[gb("acol")], writes=[gb("acol")])
    for cb in range(CB):
        load(castf_t[:, 0:160], xproj_d[cb * 128:(cb + 1) * 128, :], castf_key)
        P.op("vector", lambda e, S, cb=cb: e.tensor_copy(xpj[:, cb * 160:(cb + 1) * 160],
                                                         castf_t[:, 0:160]),
             reads=[gb(castf_key)], writes=[gb("xpj")])
    for j in range(DIH // 256):
        load(castf_t[:, 0:256], dtw_d[:, j * 256:(j + 1) * 256], castf_key)
        P.op("vector", lambda e, S, j=j: e.tensor_copy(dtw_bf[:, j * 256:(j + 1) * 256], castf_t[:, 0:256]),
             reads=[gb(castf_key)], writes=[gb("dtw_bf")])

    def precast_wide(srcd, dst, rows, cols, skey, tiled, fold=False):
        ring = [(xt[0], "xt0"), (xt[1], "xt1"), (xnt[1], "xnt1")]
        CWW = min(DM, cols)
        it = 0
        for rb in range(rows // 128):
            for cc in range(cols // CWW):
                c0 = cc * CWW
                tile, tkey = ring[it % 3]
                it += 1
                load(tile[:, 0:CWW], srcd[rb * 128:(rb + 1) * 128, c0:c0 + CWW], tkey)
                btile, bkey = ((rms_bf[0], "rmsbf0") if it % 2 == 0 else (g_t, "g_raw"))
                if fold:
                    P.op("vector", lambda e, S, rb=rb, tile=tile, btile=btile, CWW=CWW: e.tensor_scalar(
                            btile[:, 0:CWW], tile[:, 0:CWW], norm_w_t[:, rb:rb + 1], None, ALU.mult),
                         reads=[gb(tkey), gb("norm_w")], writes=[gb(bkey)])
                else:
                    P.op("vector", lambda e, S, tile=tile, btile=btile, CWW=CWW: e.tensor_copy(
                            btile[:, 0:CWW], tile[:, 0:CWW]),
                         reads=[gb(tkey)], writes=[gb(bkey)])
                if tiled:
                    for q in range(CWW // 128):
                        co = (c0 + q * 128) // 128
                        P.op("scalar", lambda e, S, co=co, rb=rb, q=q, btile=btile, dst=dst: e.dma_start(
                                out=dst[co, :, rb * 128:(rb + 1) * 128],
                                in_=btile[:, q * 128:(q + 1) * 128]),
                             reads=[gb(bkey)], writes=[gb(skey + "scr")], dma=True)
                else:
                    P.op("scalar", lambda e, S, rb=rb, c0=c0, btile=btile, dst=dst, CWW=CWW: e.dma_start(
                            out=dst[rb * 128:(rb + 1) * 128, c0:c0 + CWW],
                            in_=btile[:, 0:CWW]),
                         reads=[gb(bkey)], writes=[gb(skey + "scr")], dma=True)

    load(scr_nw[:], norm_w_row_d[:], "scr_nw2", eng="scalar")
    load(xnt[1][:], scr_nw[:].broadcast_to((128, DM)), "xnt1", eng="scalar",
         reads=[gb("scr_nw2")])
    P.op("vector", lambda e, S: e.tensor_copy(normw_rep[:], xnt[1][:]),
         reads=[gb("xnt1")], writes=[gb("normw_rep")])
    for _co in range(NCO):
        _pp = _co % 2
        load(xt[_pp][:], w_in_d[_co * 128:(_co + 1) * 128, :], f"xt{_pp}")
        P.op("vector", lambda e, S, _pp=_pp: e.tensor_tensor(
                rms_bf[0][:], xt[_pp][:], normw_rep[:], ALU.mult),
             reads=[gb(f"xt{_pp}"), gb("normw_rep")], writes=[gb("rmsbf0")])
        for _kb in range(KB):
            P.mm("tensor", lambda e, S, _kb=_kb: e.transpose(
                    p_tr[:, _kb * 128:(_kb + 1) * 128],
                    rms_bf[0][:, _kb * 128:(_kb + 1) * 128], eye_bf[:]),
                 reads=[gb("rmsbf0"), gb("eye_bf")], psum=gb("p_tr"),
                 start=(_kb == 0), stop=(_kb == KB - 1))
        P.op("scalar", lambda e, S: e.copy(g_t[:], p_tr[:]),
             reads=[gb("p_tr")], writes=[gb("g_raw")])
        P.op("scalar" if _co % 2 == 0 else "sync",
             lambda e, S, _co=_co: e.dma_start(out=scr_win[_co, :, :], in_=g_t[:]),
             reads=[gb("g_raw")], writes=[gb("winscr")], dma=True)


    P.op("vector", lambda e, S: e.memset(cst[:, 0:1], 1e-5), writes=[gb("cst")])
    P.op("vector", lambda e, S: e.memset(cst[:, 1:2], 1.0), writes=[gb("cst")])
    P.op("vector", lambda e, S: e.memset(carry[:], 0.0), writes=[gb("carry")])
    _evx = P.op("vector", lambda e, S: e.memset(halo_t[:], 0.0), writes=[gb("halo_all")])
    for _cb in range(CB):
        gb(f"halo{_cb}").w = _evx

    def ln_rms_stats(src_ap, src_buf, out_col, is_rms):
        P.op("vector", lambda e, S: e.bn_stats(stat6[:, 0:6], src_ap[:, 0:512]),
             reads=[src_buf], writes=[gb("st6")])
        P.op("vector", lambda e, S: e.bn_stats(stat6[:, 6:12], src_ap[:, 512:1024]),
             reads=[src_buf], writes=[gb("st6")])
        P.op("vector", lambda e, S: e.bn_aggr(stat2[:], stat6[:]),
             reads=[gb("st6")], writes=[gb("st2")])
        if is_rms:
            P.op("vector", lambda e, S: e.tensor_tensor(sc1[:, 3:4], stat2[:, 0:1],
                                                        stat2[:, 0:1], ALU.mult),
                 reads=[gb("st2")], writes=[gb("m2")])
            P.op("vector", lambda e, S: e.tensor_tensor(sc1[:, 4:5], sc1[:, 3:4],
                                                        stat2[:, 1:2], ALU.add),
                 reads=[gb("m2"), gb("st2")], writes=[gb("ms")])
            P.op("scalar", lambda e, S: e.activation(sc1[:, 5:6], sc1[:, 4:5], AF.Ln, bias=cst[:, 0:1]),
                 reads=[gb("ms"), gb("cst")], writes=[gb("sd")])
        else:
            P.op("vector", lambda e, S: e.tensor_scalar_mul(
                    sc1[:, 1:2], stat2[:, 0:1], -1.0),
                 reads=[gb("st2")], writes=[gb("negm")])
            P.op("scalar", lambda e, S: e.activation(sc1[:, 5:6], stat2[:, 1:2], AF.Ln, bias=cst[:, 0:1]),
                 reads=[gb("st2"), gb("cst")], writes=[gb("sd")])
        P.op("scalar", lambda e, S, c=out_col: e.activation(sc1[:, c:c + 1], sc1[:, 5:6],
                                                            AF.Exp, scale=-0.5),
             reads=[gb("sd")], writes=[gb(f"rstd{out_col}")])


    def ln_closures(k):
        t0 = k * T
        dbk = k % 2
        a = [(0, _ln_tt(k, tt, True, t0, dbk)) for tt in range(TT)]
        return a, []

    def emit_ln(k):
        a, _ = ln_closures(k)
        for _, f in a:
            f()

    def _ln_tt(k, tt, full, t0, dbk):
      def f(tt=tt):
              row = t0 + tt * 128
              pp = tt % 2
              bx = gb(f"xt{pp}")
              load(xt[pp][:], x_d[row:row + 128, :], f"xt{pp}")
              ln_rms_stats(xt[pp][:], bx, 2, is_rms=False)
              P.op("vector", lambda e, S, pp=pp: e.tensor_scalar(
                      xnt[pp][:], xt[pp][:], sc1[:, 1:2], sc1[:, 2:3], ALU.add, ALU.mult),
                   reads=[bx, gb("negm"), gb("rstd2")], writes=[gb(f"xnt{pp}")])
              if full:
                  P.op("scalar", lambda e, S, pp=pp, row=row: e.dma_start(
                          out=scr_xn[row:row + 128, :], in_=xnt[pp][:]),
                       reads=[gb(f"xnt{pp}")], writes=[gb(f"sxn{k}_{tt}")], dma=True)
              ln_rms_stats(xnt[pp][:], gb(f"xnt{pp}"), 6, is_rms=True)
              P.op("vector", lambda e, S, pp=pp: e.tensor_scalar_mul(
                      rms_bf[0][:], xnt[pp][:], sc1[:, 6:7]),
                   reads=[gb(f"xnt{pp}"), gb("rstd6")], writes=[gb("rmsbf0")])
              for kb in range(KB):
                  P.mm("tensor", lambda e, S, pp=pp, kb=kb: e.transpose(
                          p_tr[:, kb * 128:(kb + 1) * 128],
                          rms_bf[0][:, kb * 128:(kb + 1) * 128], eye_bf[:]),
                       reads=[gb("rmsbf0"), gb("eye_bf")], psum=gb("p_tr"),
                       start=(kb == 0), stop=(kb == KB - 1))
              P.op("scalar", lambda e, S, tt=tt, dbk=dbk: e.copy(
                      rms_T[dbk][:].rearrange("p (kb t) -> p kb t", kb=KB)[:, :, tt * 128:(tt + 1) * 128],
                      p_tr[:].rearrange("p (kb c) -> p kb c", kb=KB)),
                   reads=[gb("p_tr")], writes=[gb(f"rms_T{dbk}")])
      return f

    def _ln_xnT(m, tt):
      def f(tt=tt):
          row = m * T + tt * 128
          pp = tt % 2
          load(xt[pp][:], xf_d[row:row + 128, :], f"xt{pp}", eng="gpsimd")
          ln_rms_stats(xt[pp][:], gb(f"xt{pp}"), 2, is_rms=False)
          P.op("vector", lambda e, S, pp=pp: e.tensor_scalar(
                  xnt[pp][:], xt[pp][:], sc1[:, 1:2], sc1[:, 2:3], ALU.add, ALU.mult),
               reads=[gb(f"xt{pp}"), gb("negm"), gb("rstd2")], writes=[gb(f"xnt{pp}")])
          P.op("vector", lambda e, S, pp=pp: e.tensor_copy(rms_bf[0][:], xnt[pp][:]),
               reads=[gb(f"xnt{pp}")], writes=[gb("rmsbf0")])
          for kb in range(KB):
              P.mm("tensor", lambda e, S, pp=pp, kb=kb: e.transpose(
                      p_tr[:, kb * 128:(kb + 1) * 128],
                      rms_bf[0][:, kb * 128:(kb + 1) * 128], eye_bf[:]),
                   reads=[gb("rmsbf0"), gb("eye_bf")], psum=gb("p_tr"),
                   start=(kb == 0), stop=(kb == KB - 1))
          P.op("scalar", lambda e, S, tt=tt: e.copy(
                  xn_T[:].rearrange("p (kb t) -> p kb t", kb=KB)[:, :, tt * 128:(tt + 1) * 128],
                  p_tr[:].rearrange("p (kb c) -> p kb c", kb=KB)),
               reads=[gb("p_tr")], writes=[gb("xn_T")])
      return f
    def x_closures(k):
        full = k >= NCH_PRE
        dbk = k % 2
        out = []

        def inproj_conv(cb):
            def f(cb=cb):
                wb = cb % 2
                xb = xbr[wb]
                load(wblk[wb][:], scr_win[cb, :, :], f"wblk{wb}", reads=[gb("winscr")])
                pm, pmb = (p_mm, gb("p_mm")) if cb % 2 == 0 else (p_mm2, gb("p_mm2"))
                for kb in range(KB):
                    P.mm("tensor", lambda e, S, pm=pm, wb=wb, kb=kb, dbk=dbk: e.matmul(
                            pm[:, 0:T], wblk[wb][:, kb * 128:(kb + 1) * 128],
                            rms_T[dbk][:, kb * T:(kb + 1) * T],
                            start=(kb == 0), stop=(kb == KB - 1)),
                         reads=[gb(f"wblk{wb}"), gb(f"rms_T{dbk}")], psum=pmb,
                         start=(kb == 0), stop=(kb == KB - 1))
                P.op("scalar", lambda e, S, pm=pm, xb=xb: e.copy(
                        xb[:, 4: 4 + T], pm[:, 0:T]),
                     reads=[pmb], writes=[gb(f"xbrd{wb}")])
                P.op("vector", lambda e, S, cb=cb, xb=xb: e.tensor_copy(
                        xb[:, 1:4], halo_t[:, cb * 4 + 1: cb * 4 + 4]),
                     reads=[gb(f"halo{cb}")], writes=[gb(f"xbrh{wb}")])
                for tap in range(4):
                    P.op("vector", lambda e, S, cb=cb, tap=tap, wb=wb: e.tensor_scalar(
                            diag_t[0][:, tap * 128:(tap + 1) * 128], eye_bf[:],
                            conv_w_t[:, cb * 4 + tap:cb * 4 + tap + 1], None, ALU.mult),
                         reads=[gb("eye_bf"), gb("conv_w")], writes=[gb("diag0")])
                for tap in range(4):
                    P.mm("tensor", lambda e, S, tap=tap, xb=xb: e.matmul(
                            p_conv[:, 0:T], diag_t[0][:, tap * 128:(tap + 1) * 128],
                            xb[:, 1 + tap: 1 + tap + T],
                            start=(tap == 0), stop=(tap == 3)),
                         reads=[gb("diag0"), gb(f"xbrd{wb}"), gb(f"xbrh{wb}")],
                         psum=gb("p_conv"), start=(tap == 0), stop=(tap == 3))
                if cb < CBS:
                    P.op("scalar", lambda e, S, cb=cb, dbk=dbk: e.activation(
                            u_bf[dbk][:, cb * T:(cb + 1) * T], p_conv[:, 0:T], AF.Silu,
                            bias=conv_b_t[:, cb:cb + 1]),
                         reads=[gb("p_conv"), gb("conv_b")], writes=[gb(f"u{dbk}_{cb}")])
                else:
                    P.op("scalar", lambda e, S, cb=cb: e.activation(
                            u_oth[:, (cb - CBS) * T:(cb - CBS + 1) * T], p_conv[:, 0:T],
                            AF.Silu, bias=conv_b_t[:, cb:cb + 1]),
                         reads=[gb("p_conv"), gb("conv_b")], writes=[gb(f"uo_{cb}")])
                P.op("vector", lambda e, S, cb=cb, xb=xb: e.tensor_copy(
                        halo_t[:, cb * 4 + 1: cb * 4 + 4], xb[:, T + 1: T + 4]),
                     reads=[gb(f"xbrd{wb}")], writes=[gb(f"halo{cb}")])
            return f

        def xproj():
            def f():
                def usrc(cb):
                    if cb < CBS:
                        return u_bf[dbk][:, cb * T:(cb + 1) * T], gb(f"u{dbk}_{cb}")
                    return u_oth[:, (cb - CBS) * T:(cb - CBS + 1) * T], gb(f"uo_{cb}")
                for cb in range(CB):
                    ua, ub = usrc(cb)
                    P.mm("tensor", lambda e, S, cb=cb, ua=ua: e.matmul(
                            p_mm[:, 0:T], xpj[:, cb * 160: cb * 160 + 128], ua,
                            start=(cb == 0), stop=(cb == CB - 1)),
                         reads=[gb("xpj"), ub], psum=gb("p_mm"),
                         start=(cb == 0), stop=(cb == CB - 1))
                for cb in range(CB):
                    ua, ub = usrc(cb)
                    P.mm("tensor", lambda e, S, cb=cb, ua=ua: e.matmul(
                            p_mm2[0:32, 0:T], xpj[:, cb * 160 + 128: (cb + 1) * 160], ua,
                            start=(cb == 0), stop=(cb == CB - 1)),
                         reads=[gb("xpj"), ub], psum=gb("p_mm2"),
                         start=(cb == 0), stop=(cb == CB - 1))
                P.op("scalar", lambda e, S, dbk=dbk: e.copy(dtl_bf[dbk][:], p_mm[:, 0:T]),
                     reads=[gb("p_mm")], writes=[gb(f"dtl{dbk}")])
                P.op("scalar", lambda e, S, k=k: e.activation(bc_sb[:], p_mm2[0:32, 0:T],
                                                              AF.Copy, scale=mask_t[0:32, k:k + 1]),
                     reads=[gb("p_mm2"), gb("mask")], writes=[gb("bc_sb")])
                P.op("scalar", lambda e, S, dbk=dbk: e.dma_start(out=scr_bc[dbk], in_=bc_sb[:]),
                     reads=[gb("bc_sb")], writes=[gb(f"sbc{dbk}")], dma=True)
            return f

        def bcasts():
            def f():
                bsrc = scr_bc[dbk:dbk + 1].rearrange("c a b -> c (a b)")
                H = (N // 2) * T
                for half in range(2):
                    P.op("sync", lambda e, S, bsrc=bsrc, half=half, H=H: e.dma_start(
                            out=brep[:, half * H:(half + 1) * H],
                            in_=bsrc[:, half * H:(half + 1) * H].broadcast_to((128, H))),
                         reads=[gb(f"sbc{dbk}")],
                         writes=[gb(f"brep{nn}") for nn in range(half * N // 2, (half + 1) * N // 2)],
                         dma=True)
                    P.op("sync", lambda e, S, bsrc=bsrc, half=half, H=H: e.dma_start(
                            out=crep[:, half * H:(half + 1) * H],
                            in_=bsrc[:, N * T + half * H:N * T + (half + 1) * H].broadcast_to((128, H))),
                         reads=[gb(f"sbc{dbk}")],
                         writes=[gb(f"crep{nn}") for nn in range(half * N // 2, (half + 1) * N // 2)],
                         dma=True)
            return f

        for cb in range(CB):
            out.append((0, inproj_conv(cb)))
        out.append((0, xproj()))
        out.append((CBS, bcasts()))
        return out

    def ffn_closures(m):
        if m < 0 or m >= LH // T:
            return []
        items = []
        for tt in range(TT):
            items.append(_ln_xnT(m, tt))
        HF = 2 if TT >= 2 else 1
        fg, fd = _ffn_makers(m)
        for hf in range(HF):
            for hj in range(HB):
                items.append(fg(hf, hj))
            for tl in range(TT // HF):
                items.append(fd(hf, tl))
        nit = len(items)
        return [((i * (CBS - 2)) // nit, f) for i, f in enumerate(items)]

    def _ffn_makers(k):
        dbk = k % 2
        NGU = (2 if TT >= 2 else 1) * HB

        def gu_load(idx):
            hj = idx % HB
            wb = idx % 3
            qe = "sync" if idx % 2 == 0 else "scalar"
            load(wffn[wb][:], wg_d[hj, :, :], f"wffn{wb}", eng=qe)
            load(wffn2[wb][:], wu_d[hj, :, :], f"wffn2{wb}", eng=qe)

        def ffn_gu(hf, hj):
            idx = hf * HB + hj
            def f(hf=hf, hj=hj, idx=idx):
                if idx == 0:
                    gu_load(0)
                    gu_load(1)
                if idx + 2 < NGU:
                    gu_load(idx + 2)
                wb = idx % 3
                for kb in range(KB):
                    P.mm("tensor", lambda e, S, wb=wb, kb=kb, hf=hf: e.matmul(
                            p_mm[:, 0:TF], wffn[wb][:, kb * 128:(kb + 1) * 128],
                            xn_T[:, kb * T + hf * TF: kb * T + (hf + 1) * TF],
                            start=(kb == 0), stop=(kb == KB - 1)),
                         reads=[gb(f"wffn{wb}"), gb("xn_T")], psum=gb("p_mm"),
                         start=(kb == 0), stop=(kb == KB - 1))
                for kb in range(KB):
                    P.mm("tensor", lambda e, S, wb=wb, kb=kb, hf=hf: e.matmul(
                            p_mm2[:, 0:TF], wffn2[wb][:, kb * 128:(kb + 1) * 128],
                            xn_T[:, kb * T + hf * TF: kb * T + (hf + 1) * TF],
                            start=(kb == 0), stop=(kb == KB - 1)),
                         reads=[gb(f"wffn2{wb}"), gb("xn_T")], psum=gb("p_mm2"),
                         start=(kb == 0), stop=(kb == KB - 1))
                s0, s1 = 0, 1
                P.op("scalar", lambda e, S, s0=s0: e.activation(gu_s[s0][:], p_mm[:, 0:TF],
                                                                AF.Silu, scale=QINV),
                     reads=[gb("p_mm")], writes=[gb(f"gu_s{s0}")])
                P.op("scalar", lambda e, S, s1=s1: e.activation(gu_s[s1][:], p_mm2[:, 0:TF],
                                                                AF.Copy, scale=QINV),
                     reads=[gb("p_mm2")], writes=[gb(f"gu_s{s1}")])
                P.op("gpsimd", lambda e, S, hj=hj, s0=s0, s1=s1: e.tensor_tensor(
                        hffn[:, hj * TF:(hj + 1) * TF], gu_s[s0][:], gu_s[s1][:], ALU.mult),
                     reads=[gb(f"gu_s{s0}"), gb(f"gu_s{s1}")], writes=[gb(f"h{hj}")])
            return f

        def dn_load(pu, tl):
            wb = pu % 2
            qe = "sync" if pu % 2 == 0 else "scalar"
            s = wd_d[pu * 256:(pu + 1) * 256, :].rearrange("(h p) m -> p h m", p=128)
            load(wdn[wb][:].rearrange("p (h m) -> p h m", h=2), s, f"wdn{wb}", eng=qe)

        def ffn_dn(hf, tl):
            def f(hf=hf, tl=tl):
                tt = hf * (TT // (2 if TT >= 2 else 1)) + tl
                dn_load(0, tl)
                for pu in range(HB // 2):
                    wb = pu % 2
                    if pu + 1 < HB // 2:
                        dn_load(pu + 1, tl)
                    for h in range(2):
                        hj = 2 * pu + h
                        for dh in range(2):
                            P.mm("tensor", lambda e, S, hj=hj, h=h, wb=wb, dh=dh, tl=tl: e.matmul(
                                    p_o[:, dh * 512:(dh + 1) * 512],
                                    hffn[:, hj * TF + tl * 128: hj * TF + tl * 128 + 128],
                                    wdn[wb][:, h * DM + dh * 512: h * DM + (dh + 1) * 512],
                                    start=(hj == 0), stop=(hj == HB - 1)),
                                 reads=[gb(f"h{hj}"), gb(f"wdn{wb}")], psum=gb("p_o"),
                                 start=(hj == 0 and dh == 0), stop=(hj == HB - 1 and dh == 1))
                P.op("scalar", lambda e, S, tt=tt: e.activation(
                        g_t[:], p_o[:], AF.Silu, scale=QINV),
                     reads=[gb("p_o")], writes=[gb("g_raw")])
                P.op("scalar", lambda e, S, tt=tt: e.dma_start(
                        out=cc_gin[k, tt * 128:(tt + 1) * 128, :], in_=g_t[:]),
                     reads=[gb("g_raw")], writes=[gb(f"sg{k}_{tt}")], dma=True)
            return f

        return ffn_gu, ffn_dn

    def o_closures(k):
        if k < 0:
            return []
        t0 = k * T
        out = []

        def oload():
            def f():
                pw = ([gb(f"pci{r}") for r in range(4)] + [gb(f"pco{r}") for r in range(4)]
                      if k == 0 else [])
                load(yg_lo[:], cc_yout[k, 0:128, :], "yglo", eng="sync",
                     reads=[gb(f"ccyo{k}")], war=pw)
                load(yg_hi[:], cc_yout[k, 128:256, :], "yghi", eng="sync",
                     reads=[gb(f"ccyo{k}")], war=pw)
            return f

        def wo_load(pu):
            wb = pu % 2
            qe = "sync" if pu % 2 == 0 else "scalar"
            s = scr_wout[pu * 256:(pu + 1) * 256, :].rearrange("(h p) m -> p h m", p=128)
            load(woutb[wb][:].rearrange("p (h m) -> p h m", h=2), s, f"woutb{wb}", eng=qe,
                 reads=[gb("woutscr")])

        def omm(tt):
            def f(tt=tt):
                wo_load(0)
                for pu in range(CB // 2):
                    wb = pu % 2
                    if pu + 1 < CB // 2:
                        wo_load(pu + 1)
                    for h2 in range(2):
                        cb = 2 * pu + h2
                        src_t = yg_lo if cb < CBS else yg_hi
                        sk = "yglo" if cb < CBS else "yghi"
                        cbl = cb % CBS
                        for dh in range(2):
                            P.mm("tensor", lambda e, S, cb=cb, cbl=cbl, h2=h2, wb=wb, dh=dh, tt=tt, src_t=src_t: e.matmul(
                                    p_o[:, dh * 512:(dh + 1) * 512],
                                    src_t[:, cbl * T + tt * 128: cbl * T + (tt + 1) * 128],
                                    woutb[wb][:, h2 * DM + dh * 512: h2 * DM + (dh + 1) * 512],
                                    start=(cb == 0), stop=(cb == CB - 1)),
                                 reads=[gb(sk), gb(f"woutb{wb}")], psum=gb("p_o"),
                                 start=(cb == 0 and dh == 0), stop=(cb == CB - 1 and dh == 1))
            return f

        def ofin(tt):
            def f(tt=tt):
                row = t0 + tt * 128
                pp = tt % 2
                load(xt[pp][:], scr_xn[row:row + 128, :], f"xt{pp}", eng="gpsimd",
                     reads=[gb(f"sxn{k}_{tt}")])
                P.op("vector", lambda e, S, pp=pp: e.tensor_tensor(
                        xnt[pp][:], p_o[:], xt[pp][:], ALU.add),
                     reads=[gb("p_o"), gb(f"xt{pp}")], writes=[gb(f"xnt{pp}")])
                ln_rms_stats(xnt[pp][:], gb(f"xnt{pp}"), 7, is_rms=True)
                P.op("vector", lambda e, S, pp=pp: e.tensor_scalar_mul(
                        xnt[pp][:], xnt[pp][:], sc1[:, 7:8]),
                     reads=[gb(f"xnt{pp}"), gb("rstd7")], writes=[gb(f"xnt{pp}")])
                m = k if k < NCH // 2 else k - NCH // 2
                goff = (0 if k < NCH // 2 else T) + tt * 128
                load(g_t[:], cc_gout[m, goff:goff + 128, :], "g_raw", eng="gpsimd",
                     reads=[gb(f"ccgo{m}")])
                P.op("gpsimd", lambda e, S, pp=pp: e.tensor_tensor(
                        xnt[pp][:], xnt[pp][:], normf_rep[:], ALU.mult),
                     reads=[gb(f"xnt{pp}"), gb("normf_rep")], writes=[gb(f"xnt{pp}")])
                P.op("vector", lambda e, S, pp=pp: e.tensor_tensor(
                        xnt[pp][:], xnt[pp][:], g_t[:], ALU.mult),
                     reads=[gb(f"xnt{pp}"), gb("g_raw")], writes=[gb(f"xnt{pp}")])
                load(xt[pp][:], x_d[row:row + 128, :], f"xt{pp}", eng="gpsimd")
                P.op("vector", lambda e, S, pp=pp: e.tensor_tensor(
                        xnt[pp][:], xnt[pp][:], xt[pp][:], ALU.add),
                     reads=[gb(f"xnt{pp}"), gb(f"xt{pp}")], writes=[gb(f"xnt{pp}")])
                orow = k * T + tt * 128
                P.op("scalar", lambda e, S, pp=pp, orow=orow: e.dma_start(
                        out=out_d[orow:orow + 128, :], in_=xnt[pp][:]),
                     reads=[gb(f"xnt{pp}")], writes=[gb("outw")], dma=True)
            return f

        out.append((0, oload()))
        for tt in range(TT):
            out.append((tt * 2, omm(tt)))
            out.append((tt * 2 + 1, ofin(tt)))
        return out

    def emit_scan(k, pumpq, first_budget=0):
        full = True
        dbk = k % 2

        def jit_zdt(cb):
            pp = cb % 2
            P.mm("tensor", lambda e, S, cb=cb, dbk=dbk: e.matmul(
                    p_conv[:, 0:T], dtw_bf[:, cb * 128:(cb + 1) * 128], dtl_bf[dbk][:],
                    start=True, stop=True),
                 reads=[gb("dtw_bf"), gb(f"dtl{dbk}")], psum=gb("p_conv"),
                 start=True, stop=True)
            P.op("scalar", lambda e, S, cb=cb, pp=pp: e.activation(
                    sp_t[pp][:], p_conv[:, 0:T], AF.Exp, bias=dtb_t[:, cb:cb + 1]),
                 reads=[gb("p_conv"), gb("dtb")], writes=[gb(f"sp{pp}")])
            P.op("scalar", lambda e, S, pp=pp: e.activation(
                    sp_t[pp][:], sp_t[pp][:], AF.Ln, bias=cst[:, 1:2]),
                 reads=[gb(f"sp{pp}"), gb("cst")], writes=[gb(f"sp{pp}")])
            if full:
                pmz, pmzb = ((p_mm, gb("p_mm")) if pp == 0 else (p_mm2, gb("p_mm2")))
                wb = pp
                load(wblk[wb][:], scr_win[CB + cb, :, :], f"wblk{wb}", reads=[gb("winscr")])
                for kb in range(KB):
                    P.mm("tensor", lambda e, S, pmz=pmz, wb=wb, kb=kb, dbk=dbk: e.matmul(
                            pmz[:, 0:T], wblk[wb][:, kb * 128:(kb + 1) * 128],
                            rms_T[dbk][:, kb * T:(kb + 1) * T],
                            start=(kb == 0), stop=(kb == KB - 1)),
                         reads=[gb(f"wblk{wb}"), gb(f"rms_T{dbk}")], psum=pmzb,
                         start=(kb == 0), stop=(kb == KB - 1))
                P.op("scalar", lambda e, S, pmz=pmz, pp=pp: e.activation(
                        z_bf[pp][:], pmz[:, 0:T], AF.Silu),
                     reads=[pmzb], writes=[gb(f"z{pp}")])

        nq = len(pumpq)
        budget_per = (nq + CBS - 1) // CBS + 1

        def pump(cur_cb, budget):
            while pumpq and budget > 0 and pumpq[0][0] <= cur_cb:
                pumpq.pop(0)[1]()
                budget -= 1

        jit_zdt(0)
        for cb in range(CBS):
            pp = cb % 2
            if cb + 1 < CBS:
                jit_zdt(cb + 1)
            pump(cb, budget_per + (first_budget if cb == 0 else 0))
            if DEBUG_SCR:
                P.op("scalar", lambda e, S, cb=cb, k=k, dbk=dbk: e.dma_start(
                        out=scr_du[k, :, cb * T:(cb + 1) * T],
                        in_=u_bf[dbk][:, cb * T:(cb + 1) * T]),
                     reads=[gb(f"u{dbk}_{cb}")], writes=[gb("dbgu")], dma=True)
            P.op("vector", lambda e, S, cb=cb, pp=pp, dbk=dbk: e.tensor_tensor(
                    dtu_t[pp][:], sp_t[pp][:], u_bf[dbk][:, cb * T:(cb + 1) * T], ALU.mult),
                 reads=[gb(f"sp{pp}"), gb(f"u{dbk}_{cb}")], writes=[gb(f"dtu{pp}")])
            for g in range(NGRP):
                gp = (cb * NGRP + g) % 3
                yhi = (cb * NGRP + g) % 3
                P.op("scalar", lambda e, S, cb=cb, gp=gp, nn=g, pp=pp: e.activation(
                        a_t[gp][:], sp_t[pp][:],
                        AF.Exp, scale=acol_t[:, cb * N + nn: cb * N + nn + 1]),
                     reads=[gb(f"sp{pp}"), gb("acol")], writes=[gb(f"a{gp}")])
                bp = g % 2
                P.op("vector", lambda e, S, pp=pp, bp=bp, nn=g: e.tensor_tensor(
                        b_t[bp][:], dtu_t[pp][:],
                        brep[:, nn * T:(nn + 1) * T], ALU.mult),
                     reads=[gb(f"dtu{pp}"), gb(f"brep{g}")], writes=[gb(f"b{bp}")])
                P.op("vector", lambda e, S, cb=cb, gp=gp, bp=bp, nn=g: e.tensor_tensor_scan(
                        a_t[gp][:], a_t[gp][:], b_t[bp][:],
                        carry[:, cb * N + nn: cb * N + nn + 1], ALU.mult, ALU.add),
                     reads=[gb(f"a{gp}"), gb(f"b{bp}"), gb("carry")],
                     writes=[gb(f"a{gp}")])
                P.op("scalar", lambda e, S, cb=cb, gp=gp, nn=g: e.copy(
                        carry[:, cb * N + nn: cb * N + nn + 1], a_t[gp][:, T - 1:T]),
                     reads=[gb(f"a{gp}")], writes=[gb("carry")])
                if full:
                    P.op("gpsimd", lambda e, S, gp=gp, nn=g, yhi=yhi: e.tensor_tensor(
                            yh_t[yhi][:], a_t[gp][:],
                            crep[:, nn * T:(nn + 1) * T], ALU.mult),
                         reads=[gb(f"a{gp}"), gb(f"crep{g}")], writes=[gb(f"yh{yhi}")])
                    P.mm("tensor", lambda e, S, cb=cb, yhi=yhi, g=g: e.matmul(
                            p_ys[cb % 2][:, 0:T], eye_bf[:], yh_t[yhi][:],
                            start=(g == 0), stop=(g == NGRP - 1)),
                         reads=[gb(f"yh{yhi}")], psum=gb(f"p_ys{cb % 2}"),
                         start=(g == 0), stop=(g == NGRP - 1))
            if full:
                P.op("vector", lambda e, S, cb=cb, pp=pp, dbk=dbk: e.scalar_tensor_tensor(
                        dtu_t[pp][:], u_bf[dbk][:, cb * T:(cb + 1) * T], dvec_t[:, cb:cb + 1],
                        p_ys[cb % 2][:, 0:T], ALU.mult, ALU.add),
                     reads=[gb(f"u{dbk}_{cb}"), gb("dvec"), gb(f"p_ys{cb % 2}")],
                     writes=[gb(f"dtu{pp}")])
                P.op("gpsimd", lambda e, S, cb=cb, pp=pp: e.tensor_tensor(
                        yg_bf[:, cb * T:(cb + 1) * T], dtu_t[pp][:], z_bf[pp][:], ALU.mult),
                     reads=[gb(f"dtu{pp}"), gb(f"z{pp}")], writes=[gb(f"yg{cb}")])
                if DEBUG_SCR:
                    P.op("scalar", lambda e, S, cb=cb, k=k: e.dma_start(
                            out=scr_dyg[k, :, cb * T:(cb + 1) * T],
                            in_=yg_bf[:, cb * T:(cb + 1) * T]),
                         reads=[gb(f"yg{cb}")], writes=[gb("dbgyg")], dma=True)
        pump(CBS + CB, 10 ** 9)
        P.op("scalar", lambda e, S, k=k: e.dma_start(out=cc_yin[k], in_=yg_bf[:]),
             reads=[gb(f"yg{cb}") for cb in range(CBS)], writes=[gb(f"ccyi{k}")], dma=True)
        P.op("gpsimd", lambda e, S, k=k: e.collective_compute(
                "AllGather", ALU.bypass, ins=[cc_yin[k]], outs=[cc_yout[k]],
                replica_groups=RG),
             reads=[gb(f"ccyi{k}")], writes=[gb(f"ccyo{k}")], sem_override="cC")

    precast_tasks = []
    def precast_to_tasks(fn_emit_iters):
        for c in fn_emit_iters:
            precast_tasks.append((0, c))

    def gen_precast_big(srcd, dst, rows, cols, skey, i32):
        icols = (CBS * T) // 2
        CW = min(512, icols // 2)
        RING = icols // CW
        ivw = yg_lo[:].bitcast(I32 if i32 else F32)
        tiled3 = len(srcd.shape) == 3
        if tiled3:
            blocks = [(t, cc) for t in range(srcd.shape[0])
                      for cc in range(srcd.shape[2] // CW)]
        else:
            blocks = [(rb, cc) for rb in range(rows // 128) for cc in range(cols // CW)]
        nb = len(blocks)
        LA = min(3, RING - 1)

        def emit_load(i):
            rb, cc = blocks[i]
            r = i % RING
            qe = "sync" if i % 2 == 0 else "scalar"
            s = (srcd[rb, :, cc * CW:cc * CW + CW] if tiled3 else
                 srcd[rb * 128:(rb + 1) * 128, cc * CW:cc * CW + CW])
            load(ivw[:, r * CW:(r + 1) * CW], s, f"pci{r}", eng=qe)

        for bi in range(nb):
            def one(bi=bi):
                if bi == 0:
                    for jj in range(min(LA, nb)):
                        emit_load(jj)
                elif bi + LA - 1 < nb:
                    emit_load(bi + LA - 1)
                rb, cc = blocks[bi]
                c0 = cc * CW
                r = bi % RING
                P.op("vector", lambda e, S, r=r: e.tensor_copy(
                        yg_hi[:, r * CW:(r + 1) * CW], ivw[:, r * CW:(r + 1) * CW]),
                     reads=[gb(f"pci{r}")], writes=[gb(f"pco{r}")])
                qe = "scalar" if bi % 2 == 0 else "sync"
                dd = (dst[rb, :, c0:c0 + CW] if tiled3 else
                      dst[rb * 128:(rb + 1) * 128, c0:c0 + CW])
                P.op(qe, lambda e, S, dd=dd, r=r: e.dma_start(
                        out=dd, in_=yg_hi[:, r * CW:(r + 1) * CW]),
                     reads=[gb(f"pco{r}")], writes=[gb(skey + "scr")], dma=True)
            yield one

    precast_to_tasks(gen_precast_big(wout_d, scr_wout, DI, DM, "wout", False))

    emit_ln(0)
    fq0 = [f for _, f in ffn_closures(0)]
    n_head = TT + (2 if TT >= 2 else 1) * HB
    fq0_head, fq0_tail = fq0[:n_head], fq0[n_head:]
    for _, f in x_closures(0):
        f()
        for _ in range(3):
            if fq0_head:
                fq0_head.pop(0)()
    for f in fq0_head:
        f()
    for k in range(NCH):
        q = []
        q += o_closures(k - 2)
        if k == 1:
            nfq = len(fq0_tail)
            q += [((i * (CBS - 2)) // max(nfq, 1), f) for i, f in enumerate(fq0_tail)]
        else:
            q += ffn_closures(k - 1)
        lnA, _ = ln_closures(k + 1) if k + 1 < NCH else ([], [])
        q += lnA
        if k + 1 < NCH:
            q += x_closures(k + 1)
        if k == 0:
            q += precast_tasks
            del precast_tasks[:]
        emit_scan(k, q, first_budget=2)
        m = k - 1
        if 0 <= m < LH // T:
            P.op("gpsimd", lambda e, S, m=m: e.collective_compute(
                    "AllGather", ALU.bypass, ins=[cc_gin[m]], outs=[cc_gout[m]],
                    replica_groups=RG),
                 reads=[gb(f"sg{m}_{tt}") for tt in range(TT)], writes=[gb(f"ccgo{m}")],
                 sem_override="cC")
    for _, f in ffn_closures(NCH - 1):
        f()
    m = NCH - 1
    if m < LH // T:
        P.op("gpsimd", lambda e, S, m=m: e.collective_compute(
                "AllGather", ALU.bypass, ins=[cc_gin[m]], outs=[cc_gout[m]],
                replica_groups=RG),
             reads=[gb(f"sg{m}_{tt}") for tt in range(TT)], writes=[gb(f"ccgo{m}")],
             sem_override="cC")
    for _, f in o_closures(NCH - 2):
        f()
    for _, f in o_closures(NCH - 1):
        f()

    P.raw_wait("gpsimd", gb("outw").w)
    P.flush_queues("gpsimd")

    sems = {}
    sem_ctx = ExitStack()
    for s in ALL_SEMS:
        sems[s] = sem_ctx.enter_context(nc.semaphore(s))
    P.finalize_into(nc, sems)
    ctx.close()
    sem_ctx.close()
    return nc, dict(L=L, T=T, NCH=NCH, NCH_PRE=NCH_PRE, L_OWN=L_OWN, ninstr=P.ninstr)


def make_core_inputs(core_id, inputs, L=4096, NCH=8, NCH_PRE=0):
    b, j = core_id // 2, core_id % 2
    DM, DI, N = 1024, 2048, 16
    DIH = DI // 2
    CB, CBS = 16, 8
    T = L // NCH
    LH = L // 2
    x = np.asarray(inputs["L0_attn_x"][b], dtype=np.float32)
    own = slice(j * DIH, (j + 1) * DIH)
    oth = slice((1 - j) * DIH, (2 - j) * DIH)
    perm = np.r_[np.arange(j * DIH, (j + 1) * DIH), np.arange((1 - j) * DIH, (2 - j) * DIH)]
    ipw = np.asarray(inputs["L0_attn_in_proj_w"], np.float32)
    w_in = np.concatenate([ipw[perm, :], ipw[2048 + j * DIH: 2048 + (j + 1) * DIH, :]], axis=0).T
    cw = np.asarray(inputs["L0_attn_conv_w"], np.float32)[perm]
    cb_ = np.asarray(inputs["L0_attn_conv_b"], np.float32)[perm]
    xpw = np.asarray(inputs["L0_attn_x_proj_w"], np.float32)[:, perm]
    dtw = np.asarray(inputs["L0_attn_dt_proj_w"], np.float32)[own]
    dtb = np.asarray(inputs["L0_attn_dt_proj_b"], np.float32)[own]
    alog = np.asarray(inputs["L0_attn_A_log"], np.float32)[own]
    dvec = np.asarray(inputs["L0_attn_D"], np.float32)[own]
    d = {
        "x": np.ascontiguousarray(x),
        "xf": np.ascontiguousarray(x[j * LH:(j + 1) * LH]),
        "mask": np.ones((128, NCH), np.float32),
        "w_in": np.ascontiguousarray(w_in.T),
        "norm_w_row": np.ascontiguousarray(np.asarray(inputs["L0_attn_norm_w"], np.float32).reshape(1, DM)),
        "norm_w": np.ascontiguousarray(np.asarray(inputs["L0_attn_norm_w"], np.float32).reshape(8, 128).T),
        "conv_w_r": np.ascontiguousarray(cw.reshape(CB, 128, 4).transpose(1, 0, 2).reshape(128, CB * 4)),
        "conv_b_r": np.ascontiguousarray(cb_.reshape(CB, 128).T),
        "xproj_T": np.ascontiguousarray(xpw.T),
        "dtw_T": np.ascontiguousarray(dtw.T),
        "dtb_r": np.ascontiguousarray(dtb.reshape(CBS, 128).T),
        "alog_r": np.ascontiguousarray(alog.reshape(CBS, 128, N).transpose(1, 0, 2).reshape(128, CBS * N)),
        "d_r": np.ascontiguousarray(dvec.reshape(CBS, 128).T),
        "wout_T": np.ascontiguousarray(np.asarray(inputs["L0_attn_out_proj_w"], np.float32).T),
        "norm_f": np.ascontiguousarray(np.asarray(inputs["L0_attn_norm_f_w"], np.float32).reshape(1, DM)),
        "wg_T": _tile_ffn(np.asarray(inputs["L0_ffn_w_gate"], np.int32).T.astype(_BF)),
        "wu_T": _tile_ffn(np.asarray(inputs["L0_ffn_w_up"], np.int32).T.astype(_BF)),
        "wd_T": np.ascontiguousarray(np.asarray(inputs["L0_ffn_w_down"], np.int32).T.astype(_BF)),
        "eye": np.eye(128, dtype=np.float32),
    }
    return d


def _tile_ffn(wT):
    DM, HID = wT.shape
    return np.ascontiguousarray(
        wT.reshape(DM // 128, 128, HID // 128, 128).transpose(2, 1, 0, 3)
          .reshape(HID // 128, 128, DM))


def assemble_output(results, L=4096, NCH=8, NCH_PRE=0, B=4):
    DM = 1024
    LH = L // 2
    out = np.empty((B, L, DM), np.float32)
    for core_id in range(2 * B):
        b, j = core_id // 2, core_id % 2
        o = results[core_id]["out"]
        out[b, j * LH:(j + 1) * LH] = o[j * LH:(j + 1) * LH]
    return out


from concourse.bass_utils import run_bass_kernel_spmd

_BUILT = {}

def _get_program():
    if "nc" not in _BUILT:
        nc, meta = build_program(L=4096, T=512)
        _BUILT["nc"] = nc
        _BUILT["meta"] = meta
    return _BUILT["nc"], _BUILT["meta"]


def kernel(**inputs):
    nc, meta = _get_program()
    in_maps = [make_core_inputs(c, inputs, L=meta["L"], NCH=meta["NCH"],
                                NCH_PRE=meta["NCH_PRE"]) for c in range(8)]
    res = run_bass_kernel_spmd(nc, in_maps, core_ids=list(range(8)))
    return assemble_output(list(res.results), L=meta["L"], NCH=meta["NCH"],
                           NCH_PRE=meta["NCH_PRE"], B=4)
